# version 28
# baseline (speedup 1.0000x reference)
import math
from contextlib import ExitStack

import numpy as np
import ml_dtypes

import concourse.bass as bass
import concourse.mybir as mybir
from concourse.bass_utils import run_bass_kernel_spmd

F32 = mybir.dt.float32
BF16 = mybir.dt.bfloat16
AF = mybir.ActivationFunctionType
ALU = mybir.AluOpType
AX = mybir.AxisListType
BF = ml_dtypes.bfloat16

LN_EPS = 1e-5
DEN_EPS = 1e-6
BIG = 30000.0

ENGS = ("tensor", "vector", "scalar", "gpsimd", "sync")

_CNAMES = ["c0", "c1", "c2", "c3", "c4", "c5", "c6",
           "s0", "s1", "s2", "s3", "s4", "s5", "s6"]
COEF_IDX = {n: i for i, n in enumerate(_CNAMES)}


def coef_values():
    bc = [1.0 / math.factorial(2 * j) for j in range(9)]
    bs = [1.0 / math.factorial(2 * j + 1) for j in range(9)]
    v = {"c0": bc[0], "c1": bc[1], "c2": bc[2], "c3": bc[3], "c4": bc[4],
         "c5": bc[5], "c6": bc[6],
         "s0": bs[0], "s1": bs[1], "s2": bs[2], "s3": bs[3], "s4": bs[4],
         "s5": bs[5], "s6": bs[6]}
    return [v[n] for n in _CNAMES]


def default_cfg():
    return dict(T=2048, D=1024, AK=64, E=8, NROT=3, NP=2, NCORES=8, dbg=())


class Evt:
    __slots__ = ("eng", "val", "gid")

    def __init__(self, eng, val, gid):
        self.eng = eng
        self.val = val
        self.gid = gid


class Sched:

    COMPUTE_KEYS = ("tensor", "vector", "scalar", "gpsimd_c", "sync_c")

    def __init__(self):
        self.ops = {e: [] for e in ENGS}
        self.cnt = {}
        self.gid = 0

    def op(self, eng, fn, waits=(), inc=True, dma=None):
        self.gid += 1
        g = self.gid
        for w in waits:
            assert isinstance(w, Evt) and w.gid < g, "wait on future op"
        if isinstance(dma, str):
            key, stp = dma, 16
        elif eng == "gpsimd":
            assert dma is False, "gpsimd op needs dma channel or dma=False"
            key, stp = "gpsimd_c", 1
        elif eng == "sync":
            assert dma is False or dma is None
            key, stp = "sync_c", 1
        else:
            assert dma is None
            key, stp = eng, 1
        evt = None
        if inc:
            self.cnt[key] = self.cnt.get(key, 0) + stp
            evt = Evt(key, self.cnt[key], g)
            self.last_evt = getattr(self, "last_evt", {})
            self.last_evt[key] = evt
        elif key not in self.cnt:
            self.cnt[key] = 0
        self.ops[eng].append((fn, tuple(waits), evt, key, stp))
        return evt

    def emit(self, blk, sems):
        for eng in ENGS:
            oplist = self.ops[eng]
            if not oplist:
                continue

            def run(e, oplist=oplist):
                waited = {}
                first = True
                for fn, waits, evt, key, stp in oplist:
                    need = {}
                    for w in waits:
                        if w.eng == key and key in self.COMPUTE_KEYS:
                            continue
                        need[w.eng] = max(need.get(w.eng, 0), w.val)
                    for s_, v_ in need.items():
                        if waited.get(s_, 0) < v_:
                            e.wait_ge(sems[s_], v_)
                            waited[s_] = v_
                    if key in ("vector", "scalar", "gpsimd_c") and not first:
                        e.drain()
                    first = False
                    last = fn(e)
                    if evt is not None:
                        assert last is not None, f"op on {eng} returned None"
                        last.then_inc(sems[key], stp)
            getattr(blk, eng)(run)


class Builder:
    def __init__(self, cfg):
        self.cfg = cfg
        self.nc = bass.Bass("TRN2", target_bir_lowering=False)
        self.es = ExitStack()
        c = cfg
        self.D, self.T, self.AK, self.E = c["D"], c["T"], c["AK"], c["E"]
        self.NROT, self.NP = c["NROT"], c["NP"]
        self.KD = self.D // 128
        self.CT = self.T // 128
        self.H = 2 * self.D
        self.KH = self.H // 128
        self.TF = self.T // self.NP
        self.CF = self.TF // 128
        self.NT = min(512, self.D)
        self.NJ = self.D // self.NT
        self.QN = min(512, self.T)
        self.NQ = self.T // self.QN
        self.LQ = self.QN // 128
        self.TN = min(512, self.TF)
        self.NJH = self.TF // self.TN
        self.LH = self.TN // 128
        self.dbg = set(c.get("dbg", ()))
        self.dbg_outs = []

    def declare_io(self):
        nc = self.nc
        D, AK, E, H, KD, KH, CT = self.D, self.AK, self.E, self.H, self.KD, self.KH, self.CT
        dt = nc.dram_tensor
        self.d = {}

        def di(name, shape, dtype=F32):
            self.d[name] = dt(name, shape, dtype, kind="ExternalInput").ap()
        di("x", [CT, 128, D])
        di("wqk", [D, 128]); di("wv", [D, AK]); di("wo", [AK, D])
        di("aw", [D, D]); di("gw", [D, E])
        di("ew1", [E, KH, 128, KD, 128])
        di("ew2", [E, H, D])
        di("eb1", [E, H]); di("eb2", [E, D])
        di("gens", [self.NROT, D, D])
        self.ROWL = 5 * D + E + 1
        di("rows", [1, self.ROWL])
        di("tril", [128, 128], BF16)
        di("eye64", [128, 64], BF16)
        di("i128", [128, 128], BF16)
        di("i128f", [128, 128])
        di("eye8f", [8, 8])
        di("ones", [128, 128], BF16)
        self.d_y = dt("y", [CT, 128, D], F32, kind="ExternalOutput").ap()
        self.d_spill = dt("spill", [6, 128, KD, D], BF16, kind="Internal").ap()
        self.d_ew1b = dt("ew1b_h", [E, KH, 128, KD, 128], BF16, kind="Internal").ap()
        self.d_ew2b = dt("ew2b_h", [E, H, D], BF16, kind="Internal").ap()
        self.d_awb = dt("awb_h", [D, D], BF16, kind="Internal").ap()
        self.d_wqkb = dt("wqkb_h", [D, 128], BF16, kind="Internal").ap()
        self.d_wvb = dt("wvb_h", [D, self.AK], BF16, kind="Internal").ap()
        self.d_wob = dt("wob_h", [self.AK, D], BF16, kind="Internal").ap()
        self.d_rows4b = dt("rows4b_h", [1, 4 * D], BF16, kind="Internal").ap()

    def dbg_tensor(self, name, shape, dtype):
        ap = self.nc.dram_tensor("dbg_" + name, shape, dtype, kind="ExternalOutput").ap()
        self.dbg_outs.append("dbg_" + name)
        return ap

    def alloc_persistent(self):
        nc, es = self.nc, self.es
        D, AK, E = self.D, self.AK, self.E
        KD, KH, CT = self.KD, self.KH, self.CT
        sb = lambda name, shape, dtype: es.enter_context(nc.sbuf_tensor(name, shape, dtype))
        self.gw_sb = sb("gw_sb", [128, KD, E], BF16)
        self.eb2_sb = sb("eb2_sb", [E, D], BF16)
        self.eb1c = sb("eb1c", [128, KH, E], F32)
        self.eye8f_sb = sb("eye8f_sb", [8, 8], F32)
        self.ones_sb = sb("ones_sb", [128, 128], BF16)
        self.rows2_bf = sb("rows2_bf", [1, D + E + 1], BF16)
        self.gcb_col = sb("gcb_col", [128, 1], F32)
        self.eps5 = sb("eps5", [128, 1], F32)
        self.eps6 = sb("eps6", [128, 1], F32)
        col = lambda name: sb(name, [128, CT], F32)
        for n in ("c_s1", "c_s2", "c_mu1", "c_r1", "c_mu2", "c_r2", "c_muu",
                  "c_ru", "c_rho", "c_t1", "c_t2", "c_t3", "c_t4",
                  "c_sc", "c_bi", "c_d1", "c_d2a", "c_d2b", "c_g",
                  "c_mf", "c_rf", "c_var2", "c_rr"):
            setattr(self, n, col(n))
        self.xs = sb("xs", [128, CT, self.D], BF16)
        self.rgc = sb("rgc", [128, KD], BF16)
        self.rgc_f = sb("rgc_f", [128, KD], F32)
        self.grow = sb("grow", [1, D], BF16)

    def build(self):
        self.declare_io()
        self.alloc_persistent()
        self.S = Sched()
        PH = self.cfg.get("phases", 99)
        with self.nc.Block() as blk:
            self.ph_loads()
            if PH >= 1:
                self.ph_prep()
            if PH >= 2:
                self.ph_expm()
            else:
                self._gstack.close()
            self.alloc_main()
            if PH >= 3:
                self.ph_attention()
            if PH >= 4:
                self.ph_ffn_pre()
            if PH >= 5:
                self._att.close()
            self.e_pass_done = {}
            self._stores = []
            if PH >= 6:
                for p in range(min(self.NP, PH - 5)):
                    self.ph_pass(p)
            self.ph_tail()
            if PH < 5 and hasattr(self, "_att"):
                self._att.close()
            if PH in (5, 6) and hasattr(self, "_gate_ps"):
                pass
            if PH < 2 and hasattr(self, "_prep_ps"):
                self._prep_ps.close()
            sems = {k: self.es.enter_context(self.nc.semaphore("sem_" + k))
                    for k in self.S.cnt}
            self.S.emit(blk, sems)
        self.es.close()
        return self.nc

    def alloc_main(self):
        nc, es = self.nc, self.es
        D, CT, KD = self.D, self.CT, self.KD
        sb = lambda name, shape, dtype: es.enter_context(nc.sbuf_tensor(name, shape, dtype))
        self.i128f_sb = sb("i128f_sb", [128, 128], F32)
        self.w_tm = sb("w_tm", [128, CT, self.E], F32)
        self.w_fm = sb("w_fm", [self.E, self.T], BF16)
        self.xnf = sb("xnf", [128, CT, KD, 128], BF16)
        self.aw_sb = sb("aw_sb", [128, KD, D], BF16)
        self.w1b = sb("w1b", [128, D], BF16)
        self.scr = [sb(f"scr{i}", [128, D], F32) for i in range(2)]
        self.zsq = sb("zsq", [128, D], F32)
        self.jscr = sb("jscr", [128, D], BF16)
        self.mo_scr = [sb(f"mo{i}", [128, D], BF16) for i in range(6)]

    def ph_loads(self):
        b, S = self, self.S
        self._last_tp = None
        D = b.D
        self._gstack = ExitStack()
        self.mGf = self._gstack.enter_context(
            b.nc.sbuf_tensor("mGf", [128, b.KD, D], F32))
        self.e_gl0 = S.op("sync", lambda e: e.dma_start(
            out=b.mGf[:, :, :],
            in_=b.d["gens"][0].rearrange("(k p) d -> p k d", p=128)),
            dma="s_gens")
        last = None
        for dst, src in [
            (b.eye8f_sb[:, :], b.d["eye8f"]),
            (b.ones_sb[:, :], b.d["ones"]),
        ]:
            last = S.op("sync", lambda e, d=dst, s_=src: e.dma_start(out=d, in_=s_),
                        dma="s_cst")
        self.e_cst = last
        self.e_x = S.op("gpsimd", lambda e: e.dma_start(
            out=b.xs[:, :, :], in_=b.d["x"].rearrange("c t d -> t c d")),
            waits=[self.e_gl0], dma="g_x")
        e1_ = S.op("gpsimd", lambda e: e.memset(b.eps5[:, :], LN_EPS), dma=False)
        self.e_eps = S.op("gpsimd", lambda e: e.memset(b.eps6[:, :], DEN_EPS),
                          waits=[e1_], dma=False)
        last = None
        for dst, src in [
            (b.gw_sb[:, :, :], b.d["gw"].rearrange("(k p) m -> p k m", p=128)),
            (b.eb2_sb[:, :], b.d["eb2"]),
            (b.rows2_bf[:, :], b.d["rows"][0:1, 4 * D:]),
            (b.grow[:, :], b.d["rows"][0:1, 3 * D:4 * D]),
        ]:
            last = S.op("gpsimd", lambda e, d=dst, s_=src: e.dma_start(out=d, in_=s_),
                        dma="g_w")
        self.e_w = last
        self.e_cvA = S.op("gpsimd", lambda e: e.dma_start(
            out=b.d_awb, in_=b.d["aw"]), dma="g_cvA")
        for dst_, src_ in [(b.d_wqkb, b.d["wqk"]), (b.d_wvb, b.d["wv"]),
                           (b.d_wob, b.d["wo"]),
                           (b.d_rows4b, b.d["rows"][0:1, 0:4 * D])]:
            self.e_cvW = S.op("gpsimd", lambda e, d=dst_, s_=src_: e.dma_start(
                out=d, in_=s_), dma="g_cvW")
        self.e_cv1, self.e_cv2 = {}, {}
        for ei in range(self.E):
            self.e_cv1[ei] = S.op("gpsimd", lambda e, ei=ei: e.dma_start(
                out=b.d_ew1b[ei], in_=b.d["ew1"][ei]), dma=f"g_cv1_{ei}")
            self.e_cv2[ei] = S.op("gpsimd", lambda e, ei=ei: e.dma_start(
                out=b.d_ew2b[ei], in_=b.d["ew2"][ei]), dma=f"g_cv2_{ei}")

    def ph_prep(self):
        b, S = self, self.S
        nc, D, E, KH = b.nc, b.D, b.E, b.KH
        ps = ExitStack()
        b._eb1r = ps.enter_context(nc.sbuf_tensor("eb1r", [E, self.H], F32))
        pb = [ps.enter_context(nc.psum_tensor(f"p_pr{i}", [128, 512], F32)) for i in range(2)]
        e_eb1 = S.op("sync", lambda e: e.dma_start(out=b._eb1r[:, :], in_=b.d["eb1"]),
                     dma="s_eb1")
        em = S.op("tensor", lambda e: e.matmul(
            pb[0][:, 0:1], b.ones_sb[0:1, :],
            b.rows2_bf[0:1, D + E:D + E + 1], start=True, stop=True),
            waits=[self.e_cst, self.e_w])
        e_gcb = S.op("vector", lambda e: e.tensor_copy(b.gcb_col[:, :], pb[0][:, 0:1]),
                     waits=[em])
        prev = {}
        for m in range(KH):
            pt = pb[m % 2]
            w = [e_eb1, e_gcb] + ([prev[m - 2]] if m - 2 in prev else [])
            em = S.op("tensor", lambda e, pt=pt, m=m: e.transpose(
                pt[:, 0:E], b._eb1r[0:E, m * 128:(m + 1) * 128], b.eye8f_sb[:, :]),
                waits=w)
            prev[m] = S.op("vector", lambda e, pt=pt, m=m: e.tensor_copy(
                b.eb1c[:, m, :], pt[:, 0:E]), waits=[em])
        self.e_prep = prev[KH - 1]
        self._prep_ps = ps

    def ph_expm(self):
        b, S = self, self.S
        nc, D, KD, NJ, NT = b.nc, b.D, b.KD, b.NJ, b.NT
        self._prep_ps.close()
        ws = ExitStack()
        sbm = lambda n: ws.enter_context(nc.sbuf_tensor(n, [128, KD, D], BF16))
        mG = sbm("mG")
        mB, mB2, mB3 = sbm("mB"), sbm("mB2"), sbm("mB3")
        mCmb, mU, mC, mS_ = sbm("mCmb"), sbm("mU"), sbm("mC"), sbm("mS")
        mGf = b.mGf
        b.ceye_sb = ws.enter_context(
            nc.sbuf_tensor("ceye_sb", [128, len(_CNAMES), 128], BF16))
        b.i128_sb = ws.enter_context(nc.sbuf_tensor("i128_sb", [128, 128], BF16))
        e_i1 = S.op("sync", lambda e: e.dma_start(
            out=b.i128_sb[:, :], in_=b.d["i128"]), waits=[], dma="s_ce")
        e_ce = None
        for ci, cv in enumerate(coef_values()):
            e_ce = S.op("scalar", lambda e, ci=ci, cv=cv: e.activation(
                b.ceye_sb[:, ci, :], b.i128_sb[:, :], AF.Identity, scale=cv),
                waits=[e_i1])
        ps = ExitStack()
        pp = [ps.enter_context(nc.psum_tensor(f"p_e{i}", [128, 512], F32)) for i in range(4)]
        pbx = ps.enter_context(nc.psum_tensor("p_ebx", [128, 512], F32))
        wsG = ws.enter_context(nc.sbuf_tensor("wsG", [128, D], BF16))

        bc = [1.0 / math.factorial(2 * j) for j in range(9)]
        bs = [1.0 / math.factorial(2 * j + 1) for j in range(9)]
        CE = COEF_IDX

        def mat_fm(lhsT, rhs, out, waits, scale=1.0, addm=None, diag=None,
                   addm_w=(), coefs=()):
            evacs = {}
            last = None
            for mc in range(KD):
                for j in range(NJ):
                    ti = mc * NJ + j
                    pt = pp[ti % 4]
                    w = list(waits) + ([evacs[ti - 4]] if ti - 4 in evacs else [])
                    dj = None
                    if diag is not None:
                        o = mc * 128 - j * NT
                        if 0 <= o and o + 128 <= NT:
                            dj = o

                    def mm(e, pt=pt, mc=mc, j=j, dj=dj, lhsT=lhsT, rhs=rhs):
                        r = None
                        n_extra = len(coefs) + (1 if dj is not None else 0)
                        for kc in range(KD):
                            stop = (kc == KD - 1) and n_extra == 0
                            r = e.matmul(pt[:, 0:NT],
                                         lhsT[:, kc, mc * 128:(mc + 1) * 128],
                                         rhs[:, kc, j * NT:(j + 1) * NT],
                                         start=(kc == 0), stop=stop)
                        k = 0
                        for ci, mat in coefs:
                            k += 1
                            stop = (k == n_extra)
                            r = e.matmul(pt[:, 0:NT], b.ceye_sb[:, ci, :],
                                         mat[:, mc, j * NT:(j + 1) * NT],
                                         start=False, stop=stop)
                        if dj is not None:
                            r = e.matmul(pt[:, dj:dj + 128], b.i128_sb[:, :],
                                         b.ceye_sb[:, diag, :],
                                         start=False, stop=True)
                        return r
                    em = S.op("tensor", mm, waits=w)

                    if addm is None:
                        def evs(e, pt=pt, mc=mc, j=j, out=out, scale=scale):
                            sl = (slice(None), mc, slice(j * NT, (j + 1) * NT))
                            return e.activation(out[sl], pt[:, 0:NT], AF.Identity,
                                                scale=scale)
                        evacs[ti] = last = S.op("scalar", evs, waits=[em])
                    else:
                        def ev(e, pt=pt, mc=mc, j=j, out=out, addm=addm, scale=scale):
                            sl = (slice(None), mc, slice(j * NT, (j + 1) * NT))
                            if scale == 1.0:
                                return e.tensor_tensor(out[sl], pt[:, 0:NT],
                                                       addm[sl], ALU.add)
                            return e.scalar_tensor_tensor(out[sl], pt[:, 0:NT], scale,
                                                          addm[sl], ALU.mult, ALU.add)
                        evacs[ti] = last = S.op("vector", ev,
                                                waits=[em] + list(addm_w))
            return last

        def combo(dst, srcs, waits):
            def vf(e):
                r = None
                for n_, (cf, m_) in enumerate(srcs):
                    if n_ > 0:
                        e.drain()
                    if n_ == 0:
                        r = e.tensor_scalar(dst[:, :, :], m_[:, :, :], cf, None,
                                            ALU.mult, ALU.bypass)
                    else:
                        r = e.scalar_tensor_tensor(dst[:, :, :], m_[:, :, :], cf,
                                                   dst[:, :, :], ALU.mult, ALU.add)
                return r
            return S.op("vector", vf, waits=waits)

        prev_last = [self.e_prep]
        sp_gate = []
        e_gl = self.e_gl0
        e_cast_last = None
        for i in range(b.NROT):
            e_g = S.op("vector", lambda e: e.tensor_scalar(
                mG[:, :, :], mGf[:, :, :], 1.0, None, ALU.mult, ALU.bypass),
                waits=[e_gl] + list(prev_last))
            e_cast_last = e_g
            if i + 1 < b.NROT:
                e_gl = S.op("sync", lambda e, i=i: e.dma_start(
                    out=mGf[:, :, :],
                    in_=b.d["gens"][i + 1].rearrange("(k p) d -> p k d", p=128)),
                    waits=[e_g], dma="s_gens")
            eB = mat_fm(mG, mG, mB, [e_g, e_ce], scale=-1.0)
            eB2 = mat_fm(mB, mB, mB2, [eB])
            eB3 = mat_fm(mB2, mB, mB3, [eB2])
            def acc3(dst, cfs, first_w):
                t1 = S.op("vector", lambda e: e.tensor_scalar(
                    dst[:, :, :], mB[:, :, :], cfs[0], None, ALU.mult,
                    ALU.bypass), waits=[eB] + first_w)
                t2 = S.op("vector", lambda e: e.scalar_tensor_tensor(
                    dst[:, :, :], mB2[:, :, :], cfs[1], dst[:, :, :],
                    ALU.mult, ALU.add), waits=[eB2, t1])
                return S.op("vector", lambda e: e.scalar_tensor_tensor(
                    dst[:, :, :], mB3[:, :, :], cfs[2], dst[:, :, :],
                    ALU.mult, ALU.add), waits=[eB3, t2])
            eD2 = acc3(mCmb, (bc[4], bc[5], bc[6]), sp_gate)
            eD2s = acc3(mU, (bs[4], bs[5], bs[6]), sp_gate)
            eC = mat_fm(mB3, mCmb, mC, [eD2],
                        coefs=[(CE["c1"], mB), (CE["c2"], mB2),
                               (CE["c3"], mB3)], diag=CE["c0"])
            eS = mat_fm(mB3, mU, mS_, [eD2s, eC],
                        coefs=[(CE["s1"], mB), (CE["s2"], mB2),
                               (CE["s3"], mB3)], diag=CE["s0"])
            if i == 0:
                eEv = mat_fm(mS_, mG, mU, [eS], scale=-1.0, addm=mC)
            else:
                eEv = mat_fm(mG, mS_, mU, [eS], scale=-1.0, addm=mC)
            prev_last = [eEv]
            if i < 2:
                e_sp1 = S.op("sync", lambda e, i=i: e.dma_start(
                    out=b.d_spill[i], in_=mU[:, :, :]), waits=[eEv], dma="s_sp")
                sp_gate = [e_sp1]
            else:
                sp_gate = []
        red = None
        for c in range(b.CT):
            red = S.op("vector", lambda e, c=c: e.reduce_sum(
                b.c_s1[:, c:c + 1], b.xs[:, c, :], axis=AX.X),
                waits=[self.e_x, e_cast_last])
        e_xred = red
        e_l = None
        for idx, dst in [(0, mB2), (1, mB3)]:
            e_l = S.op("sync", lambda e, idx=idx, dst=dst: e.dma_start(
                out=dst[:, :, :], in_=b.d_spill[idx]), waits=list(prev_last),
                dma="s_sp")
        eM = mat_fm(mB2, mB3, mC, [e_l])
        eMT = mat_fm(mB3, mB2, mS_, [eM])
        eRv = mat_fm(mS_, mU, mCmb, [eMT])
        self.e_R = S.op("sync", lambda e: e.dma_start(
            out=b.d_spill[5], in_=mCmb[:, :, :]), waits=[eRv], dma="s_sp")
        e_bx = None
        for j in range(b.NJ):
            embx = S.op("tensor", lambda e, j=j: e.matmul(
                pbx[:, 0:NT], b.ones_sb[0:1, :], b.grow[0:1, j * NT:(j + 1) * NT],
                start=True, stop=True),
                waits=[self.e_cst, self.e_w] + ([e_bx] if e_bx else []))
            e_bx = S.op("vector", lambda e, j=j: e.tensor_copy(
                wsG[:, j * NT:(j + 1) * NT], pbx[:, 0:NT]), waits=[embx])
        e_rg = None
        for kc in range(KD):
            e_rg = S.op("vector", lambda e, kc=kc: e.scalar_tensor_tensor(
                mGf[:, 0, :], mCmb[:, kc, :], 1.0, wsG[:, :],
                ALU.bypass, ALU.mult, accum_out=b.rgc_f[:, kc:kc + 1]),
                waits=[eRv, e_bx])
        self.e_rgc = S.op("scalar", lambda e: e.activation(
            b.rgc[:, :], b.rgc_f[:, :], AF.Identity), waits=[e_rg])
        red2 = {}
        sqr = None
        for c in range(b.CT):
            sl = 2 + (c % 2) if KD >= 4 else 1
            w = [self.e_x, e_cast_last]
            back = 2 if KD >= 4 else 1
            if c - back in red2:
                w.append(red2[c - back])
            sqr = S.op("scalar", lambda e, c=c, sl=sl: e.activation(
                mGf[:, sl, :], b.xs[:, c, :], AF.Square), waits=w)
            red2[c] = S.op("vector", lambda e, c=c, sl=sl: e.reduce_sum(
                b.c_s2[:, c:c + 1], mGf[:, sl, :], axis=AX.X), waits=[sqr])
        self.e_xstat = self.emit_mu_r(b.c_s1, b.c_s2, b.c_mu1, b.c_r1,
                                      [e_xred, red2[b.CT - 1]])
        if "R" in b.dbg:
            dR = b.dbg_tensor("R", [128, KD, D], BF16)
            S.op("sync", lambda e, dR=dR: e.dma_start(out=dR, in_=b.d_spill[5]),
                 waits=[self.e_R], dma="s_dbg_R")
        ps.close()
        ws.close()
        self._gstack.close()

    def emit_mu_r(self, s1, s2, mu, r, waits, var_out=None, sl=slice(None),
                  mu_src=None):
        b, S, D = self, self.S, self.D

        def colm(e):
            if mu_src is None:
                e.tensor_scalar(mu[:, sl], s1[:, sl], 1.0 / D, None, ALU.mult,
                                ALU.bypass)
            else:
                e.tensor_copy(mu[:, sl], mu_src[:, sl])
            e.drain()
            e.tensor_scalar(b.c_t1[:, sl], s2[:, sl], 1.0 / D, None, ALU.mult,
                            ALU.bypass)
            e.drain()
            e.tensor_tensor(b.c_t2[:, sl], mu[:, sl], mu[:, sl], ALU.mult)
            e.drain()
            e.tensor_tensor(b.c_t1[:, sl], b.c_t1[:, sl], b.c_t2[:, sl],
                            ALU.subtract)
            if var_out is not None:
                e.drain()
                e.tensor_copy(var_out[:, sl], b.c_t1[:, sl])
            e.drain()
            return e.tensor_scalar(b.c_t1[:, sl], b.c_t1[:, sl], LN_EPS, None,
                                   ALU.add, ALU.bypass)
        e1 = S.op("vector", colm, waits=waits)
        e2 = S.op("scalar", lambda e: e.activation(
            b.c_t2[:, sl], b.c_t1[:, sl], AF.Sqrt), waits=[e1])
        return S.op("vector", lambda e: e.reciprocal(r[:, sl], b.c_t2[:, sl]),
                    waits=[e2])

    def ph_attention(self):
        b, S = self, self.S
        nc, D, KD, AK, T, CT = b.nc, b.D, b.KD, b.AK, b.T, b.CT
        att = ExitStack()
        self._att = att
        wsret = [getattr(self, n) for n in ("e_R", "e_xstat", "e_rgc")
                 if hasattr(self, n)] or [self.e_prep]
        eR0 = wsret[0]
        sb = lambda n, shape, dtype: att.enter_context(nc.sbuf_tensor(n, shape, dtype))
        b.xn_tm = sb("xn_tm", [128, CT, D], BF16)
        b.gb = sb("gb", [128, D], BF16)
        b.bb = sb("bb", [128, D], BF16)
        rows4 = sb("rows4", [1, 4 * D], BF16)
        wqk_sb = sb("wqk_sb", [128, KD, 128], BF16)
        wv_sb = sb("wv_sb", [128, KD, AK], BF16)
        wo_sb = sb("wo_sb", [AK, D], BF16)
        b.tril_sb = sb("tril_sb", [128, 128], BF16)
        b.eye64_sb = sb("eye64_sb", [128, 64], BF16)
        e_trl = S.op("sync", lambda e: e.dma_start(
            out=b.tril_sb[:, :], in_=b.d["tril"]), waits=list(wsret), dma="s_att")
        self.e_i128f = S.op("sync", lambda e: e.dma_start(
            out=b.i128f_sb[:, :], in_=b.d["i128f"]), waits=list(wsret),
            dma="s_att")
        e_trl = S.op("sync", lambda e: e.dma_start(
            out=b.eye64_sb[:, :], in_=b.d["eye64"]), waits=[], dma="s_att")
        Q_fm = sb("Q_fm", [64, T], BF16)
        K_fm = sb("K_fm", [64, T], BF16)
        V_fm = sb("V_fm", [64, T], BF16)
        K_tm = sb("K_tm", [128, CT, 64], BF16)
        V1_tm = sb("V1_tm", [128, CT, 65], BF16)
        num_fm = sb("num_fm", [64, T], BF16)
        S_all = sb("S_all", [64, CT * 65], BF16)
        S_acc = sb("S_acc", [64, 65], F32)
        A_sb = sb("A_sb", [128, 2, 128], BF16)
        escr = sb("escr", [128, 512], F32)
        escr2 = sb("escr2", [128, 512], F32)
        wo_rsf = sb("wo_rsf", [64, 1], F32)
        wo_rsb = sb("wo_rsb", [64, 1], BF16)

        eR = wsret[0]
        self.e_awl = S.op("sync", lambda e: e.dma_start(
            out=b.aw_sb[:, :, :],
            in_=b.d_awb.rearrange("(k p) d -> p k d", p=128)),
            waits=wsret + [self.e_cvA], dma="s_aw")
        last = None
        for dst, srcap in [
            (wqk_sb[:, :, :], b.d_wqkb.rearrange("(k p) m -> p k m", p=128)),
            (wv_sb[:, :, :], b.d_wvb.rearrange("(k p) m -> p k m", p=128)),
            (wo_sb[:, :], b.d_wob),
            (rows4[:, :], b.d_rows4b),
        ]:
            last = S.op("sync", lambda e, d=dst, s_=srcap: e.dma_start(
                out=d, in_=s_), waits=list(wsret) + [self.e_cvW], dma="s_att")
        self.e_w2 = last
        psb = ExitStack()
        pbx = [psb.enter_context(nc.psum_tensor(f"p_bx{i}", [128, 512], F32))
               for i in range(2)]
        wbt = min(512, D)
        nbt = max(1, D // wbt)
        bca = {}
        dsts = [b.gb, b.bb, b.w1b]
        for r in range(3):
            for j in range(nbt):
                idx = r * nbt + j
                w = [self.e_w2] + ([bca[idx - 2]] if idx - 2 in bca else [])
                off = r * D + j * wbt
                em0 = S.op("tensor", lambda e, off=off, idx=idx: e.matmul(
                    pbx[idx % 2][:, 0:wbt], b.ones_sb[0:1, :],
                    rows4[0:1, off:off + wbt], start=True, stop=True), waits=w)
                bca[idx] = S.op("vector", lambda e, r=r, j=j, idx=idx: e.tensor_copy(
                    dsts[r][:, j * wbt:(j + 1) * wbt], pbx[idx % 2][:, 0:wbt]),
                    waits=[em0])
        self.e_bc = bca[3 * nbt - 1]
        psb.close()

        e_mr = self.e_xstat
        if "mu1" in b.dbg:
            dmu = b.dbg_tensor("mu1", [128, 2 * CT], F32)
            S.op("sync", lambda e, dmu=dmu: e.dma_start(
                out=dmu[:, 0:CT], in_=b.c_mu1[:, 0:CT]), waits=[e_mr],
                dma="s_dbg_mu1")
            S.op("sync", lambda e, dmu=dmu: e.dma_start(
                out=dmu[:, CT:2 * CT], in_=b.c_r1[:, 0:CT]), waits=[e_mr],
                dma="s_dbg_mu1")
        e_xn_parts = {}
        G4p = 4 if CT % 4 == 0 else CT
        for c in range(CT):
            e_xn = S.op("vector", lambda e, c=c: e.tensor_scalar(
                b.xn_tm[:, c, :], b.xs[:, c, :], b.c_mu1[:, c:c + 1],
                b.c_r1[:, c:c + 1], ALU.subtract, ALU.mult),
                waits=[e_mr] + wsret)
            if (c + 1) % G4p == 0:
                e_xn_parts[c // G4p] = e_xn
        G4 = 4 if CT % 4 == 0 else CT
        NG = CT // G4
        tpz1 = {}
        for g in range(NG):
            w = [e_xn_parts[g]]
            if self._last_tp is not None:
                w.append(self._last_tp)
            e_tpg = S.op("sync", lambda e, g=g: e.dma_start(
                out=b.xnf[:, g * G4:(g + 1) * G4, :, :],
                in_=b.xn_tm[:, g * G4:(g + 1) * G4, :]
                    .rearrange("p c d -> p (c d)"), transpose=True),
                waits=w, dma=f"s_t1_{g}")
            tpz1[g] = e_tpg
            self._last_tp = e_tpg
        e_tp = tpz1[NG - 1]
        self.e_tp_xn1 = e_tp
        self._tp1_groups = tpz1
        self._G4 = G4
        self.e_dbg_xn1 = None
        if "xn1" in b.dbg:
            dx = b.dbg_tensor("xn1", [128, CT, KD, 128], BF16)
            self.e_dbg_xn1 = S.op("sync", lambda e, dx=dx: e.dma_start(
                out=dx, in_=b.xnf[:, :, :, :]), waits=[e_tp], dma="s_dbg_xn1")

        SUB = self.cfg.get("attn_sub", 9)
        if SUB < 3:
            self.e_attn = [e_tp] * CT
            return
        ps1 = ExitStack()
        pq = [ps1.enter_context(nc.psum_tensor(f"p_q{i}", [128, 512], F32)) for i in range(2)]
        pv = [ps1.enter_context(nc.psum_tensor(f"p_v{i}", [128, 512], F32)) for i in range(2)]

        def rhs_ap(k, j):
            return b.xnf[:, self.LQ * j:self.LQ * (j + 1), k, :]

        eluv = {}
        for j in range(b.NQ):
            gj = min((self.LQ * (j + 1) - 1) // self._G4, len(tpz1) - 1)
            w = [tpz1[gj], self.e_w2]
            if j - 2 in eluv:
                w.append(eluv[j - 2][0])

            def mmq(e, j=j):
                for kc in range(KD):
                    e.matmul(pq[j % 2][0:64, 0:b.QN], wqk_sb[:, kc, 0:64],
                             rhs_ap(kc, j), start=(kc == 0), stop=(kc == KD - 1))
                r = None
                for kc in range(KD):
                    r = e.matmul(pq[j % 2][64:128, 0:b.QN], wqk_sb[:, kc, 64:128],
                                 rhs_ap(kc, j), start=(kc == 0), stop=(kc == KD - 1))
                return r
            em_q = S.op("tensor", mmq, waits=w)
            w2 = [eluv[j - 2][1]] if j - 2 in eluv else []

            def mmv(e, j=j):
                r = None
                for kc in range(KD):
                    r = e.matmul(pv[j % 2][0:64, 0:b.QN], wv_sb[:, kc, 0:AK],
                                 rhs_ap(kc, j), start=(kc == 0), stop=(kc == KD - 1))
                return r
            em_v = S.op("tensor", mmv, waits=w2)
            e1 = S.op("vector", lambda e, j=j: e.tensor_scalar(
                escr[:, 0:b.QN], pq[j % 2][:, 0:b.QN], 0.0, None, ALU.min, ALU.bypass),
                waits=[em_q])
            e2 = S.op("scalar", lambda e, j=j: e.activation(
                escr2[:, 0:b.QN], escr[:, 0:b.QN], AF.Exp), waits=[e1])
            e3 = S.op("vector", lambda e, j=j: e.tensor_scalar(
                escr[:, 0:b.QN], pq[j % 2][:, 0:b.QN], 0.0, None, ALU.max, ALU.bypass),
                waits=[e2, e1])
            e4a = S.op("vector", lambda e, j=j: e.tensor_tensor(
                Q_fm[0:64, j * b.QN:(j + 1) * b.QN], escr[0:64, 0:b.QN],
                escr2[0:64, 0:b.QN], ALU.add), waits=[e3, e2])
            e4 = S.op("vector", lambda e, j=j: e.tensor_tensor(
                K_fm[0:64, j * b.QN:(j + 1) * b.QN], escr[64:128, 0:b.QN],
                escr2[64:128, 0:b.QN], ALU.add), waits=[e4a])
            e5 = S.op("vector", lambda e, j=j: e.tensor_copy(
                V_fm[0:64, j * b.QN:(j + 1) * b.QN], pv[j % 2][0:64, 0:b.QN]),
                waits=[em_v])
            eluv[j] = (e4, e5)
        e_qkv_last = [eluv[b.NQ - 1][0], eluv[b.NQ - 1][1]]

        if SUB < 4:
            self.e_attn = [e_qkv_last[0]] * CT
            ps1.close()
            return
        e_wr1 = S.op("vector", lambda e: e.tensor_reduce(
            wo_rsf[0:64, 0:1], wo_sb[0:64, :], axis=AX.X, op=ALU.add),
            waits=[self.e_w2])
        e_wr2 = S.op("scalar", lambda e: e.activation(
            wo_rsb[0:64, 0:1], wo_rsf[0:64, 0:1], AF.Identity), waits=[e_wr1])
        ps1.close()
        ps_i = ExitStack()
        pA = ps_i.enter_context(nc.psum_tensor("p_A", [128, 512], F32))
        pN = [ps_i.enter_context(nc.psum_tensor(f"p_N{i}", [128, 512], F32))
              for i in range(2)]
        pDn = ps_i.enter_context(nc.psum_tensor("p_Dn", [128, 2 * CT], F32))
        ps_t = ExitStack()
        ptb = ps_t.enter_context(nc.psum_tensor("p_tb", [128, 1024], BF16))
        pSt = [ps_t.enter_context(nc.psum_tensor(f"p_st{i}", [128, 512], F32))
               for i in range(2)]
        e_ones = S.op("gpsimd", lambda e: e.memset(V1_tm[:, :, 64:65], 1.0),
                      waits=list(wsret), dma=False)
        e_zs = S.op("gpsimd", lambda e: e.memset(S_all[:, 0:65], 0.0),
                    waits=list(wsret), dma=False)
        e_zero = S.op("gpsimd", lambda e: e.memset(S_acc[:, :], 0.0),
                      waits=[e_zs], dma=False)
        qv_last = [eluv[b.NQ - 1][0], eluv[b.NQ - 1][1]]
        if b.NQ >= 2:
            qv_last += [eluv[b.NQ - 2][0], eluv[b.NQ - 2][1]]

        trs = {}
        e_sac, e_scw, e_mask, e_emn, e_emd, e_nev = {}, {}, {}, {}, {}, {}

        def phase_a(c):
            ws_ = [trs[c][0], trs[c][1]]
            ws_.append(e_sac[c - 2] if c - 2 in e_sac else e_zero)
            em_s = S.op("tensor", lambda e, c=c: e.matmul(
                pSt[c % 2][0:64, 0:65], K_tm[:, c, :], V1_tm[:, c, :],
                start=True, stop=True), waits=ws_)
            e_sac[c] = S.op("vector", lambda e, c=c: e.tensor_tensor(
                S_acc[:, :], S_acc[:, :], pSt[c % 2][0:64, 0:65], ALU.add),
                waits=[em_s] + ([e_zero] if c == 0 else []))
            e_scw[c] = S.op("vector", lambda e, c=c: e.tensor_copy(
                S_all[:, (c + 1) * 65:(c + 2) * 65], S_acc[:, :]),
                waits=[e_sac[c]])

        def phase_b(c):
            Q = Q_fm[0:64, c * 128:(c + 1) * 128]
            K = K_fm[0:64, c * 128:(c + 1) * 128]
            sA = slice((c % 4) * 128, (c % 4 + 1) * 128)
            w0 = list(qv_last)
            if c - 4 in e_mask:
                w0.append(e_mask[c - 4])
            em_a = S.op("tensor", lambda e, Q=Q, K=K, sA=sA: e.matmul(
                pA[:, sA], K, Q, start=True, stop=True), waits=w0)
            wm = [em_a] + ([e_trl] if c < 2 else [])
            for k_ in (c - 2,):
                if k_ in e_emn:
                    wm += [e_emn[k_], e_emd[k_]]
            e_mask[c] = S.op("vector", lambda e, c=c, sA=sA: e.tensor_tensor(
                A_sb[:, c % 2, :], pA[:, sA], b.tril_sb[:, :], ALU.mult),
                waits=wm)
            eS_ = e_scw[c - 1] if c - 1 in e_scw else e_zs

            def mmn(e, Q=Q, c=c):
                e.matmul(pN[c % 2][0:65, 0:128], S_all[:, c * 65:(c + 1) * 65],
                         Q, start=True, stop=False)
                return e.matmul(pN[c % 2][0:65, 0:128], V1_tm[:, c, :],
                                A_sb[:, c % 2, :], start=False, stop=True)
            wn = [e_mask[c], eS_, trs[c][1]]
            if c - 2 in e_nev:
                wn.append(e_nev[c - 2])
            e_emn[c] = S.op("tensor", mmn, waits=wn)

            def mmd(e, Q=Q, c=c):
                e.matmul(pDn[:, c:c + 1], Q, S_all[:, c * 65 + 64:(c + 1) * 65],
                         start=True, stop=False)
                return e.matmul(pDn[:, c:c + 1], A_sb[:, c % 2, :],
                                b.ones_sb[:, 0:1], start=False, stop=True)
            e_emd[c] = S.op("tensor", mmd, waits=[e_mask[c], eS_])
            e_nev[c] = S.op("vector", lambda e, c=c: e.tensor_copy(
                num_fm[0:64, c * 128:(c + 1) * 128], pN[c % 2][0:64, 0:128]),
                waits=[e_emn[c]])

        for c in range(CT):
            jj = min(b.NQ - 1, (c * 128) // b.QN)
            w = [eluv[jj][0], eluv[jj][1]]
            if c == 0:
                w += qv_last
            if c - 1 in trs:
                w.append(trs[c - 1][1])
            ek = S.op("tensor", lambda e, c=c: e.transpose(
                ptb[:, 0:64], K_fm[0:64, c * 128:(c + 1) * 128],
                b.eye64_sb[0:64, :]), waits=w + [e_trl])
            ekc = S.op("vector", lambda e, c=c: e.tensor_copy(
                K_tm[:, c, :], ptb[:, 0:64]), waits=[ek])
            ev_ = S.op("tensor", lambda e, c=c: e.transpose(
                ptb[:, 0:64], V_fm[0:64, c * 128:(c + 1) * 128],
                b.eye64_sb[0:64, :]), waits=[ekc])
            evc = S.op("vector", lambda e, c=c: e.tensor_copy(
                V1_tm[:, c, 0:64], ptb[:, 0:64]), waits=[ev_, e_ones])
            trs[c] = (ekc, evc)
            if c >= 1:
                if c - 1 < CT - 1:
                    phase_a(c - 1)
                if SUB >= 5:
                    phase_b(c - 1)
        if SUB < 5:
            self.e_attn = [trs[CT - 1][1]] * CT
            ps_t.close()
            ps_i.close()
            return
        phase_b(CT - 1)
        e_dad = S.op("vector", lambda e: e.tensor_scalar(
            b.c_t3[:, 0:CT], pDn[:, 0:CT], DEN_EPS, None, ALU.add, ALU.bypass),
            waits=[e_emd[c_] for c_ in range(CT)])
        e_rho_all = S.op("vector", lambda e: e.reciprocal(
            b.c_rho[:, 0:CT], b.c_t3[:, 0:CT]), waits=[e_dad])

        if "vfm" in b.dbg:
            dv = b.dbg_tensor("vfm", [64, T], BF16)
            S.op("sync", lambda e, dv=dv: e.dma_start(
                out=dv, in_=V_fm[0:64, :]),
                waits=[eluv[j_][1] for j_ in range(b.NQ)], dma="s_dbg_vfm")
        if "sall" in b.dbg:
            dsa = b.dbg_tensor("sall", [64, CT * 65], BF16)
            S.op("sync", lambda e, dsa=dsa: e.dma_start(
                out=dsa, in_=S_all[:, :]),
                waits=[e_scw[c_] for c_ in range(CT - 1)] + [e_zs],
                dma="s_dbg_sall")
        if "num" in b.dbg:
            dnum = b.dbg_tensor("num", [64, T], BF16)
            S.op("sync", lambda e, dnum=dnum: e.dma_start(
                out=dnum, in_=num_fm[0:64, :]),
                waits=[e_nev[c_] for c_ in range(CT)], dma="s_dbg_num")
        if "den" in b.dbg:
            dden = b.dbg_tensor("den", [128, CT], F32)
            S.op("sync", lambda e, dden=dden: e.dma_start(
                out=dden, in_=b.c_rho[:, 0:CT]), waits=[e_rho_all],
                dma="s_dbg_den")
        if SUB < 6:
            self.e_attn = [e_rho_all] * CT
            ps_t.close()
            ps_i.close()
            return
        ILAST = [e_nev[CT - 1], e_rho_all]
        ps_t.close()
        ps2 = ExitStack()
        pZ = [ps2.enter_context(nc.psum_tensor(f"p_z{i}", [128, 512], F32))
              for i in range(4)]
        pZp = [(pZ[0], pZ[1]), (pZ[2], pZ[3]), (pA, pN[0])]
        NJ = b.NJ
        done = {}
        e_cm_last = None
        for c in range(CT):
            w = [e_nev[c], e_rho_all, e_wr2]
            if c == 0:
                w += ILAST
            if c - 3 in done:
                w.append(done[c - 3])

            def mmz(e, c=c):
                for j in range(NJ):
                    e.matmul(pZp[c % 3][j][:, 0:b.NT],
                             num_fm[0:64, c * 128:(c + 1) * 128],
                             wo_sb[0:64, j * b.NT:(j + 1) * b.NT],
                             start=True, stop=True)
                return e.matmul(pDn[:, CT + c:CT + c + 1],
                                num_fm[0:64, c * 128:(c + 1) * 128],
                                wo_rsb[0:64, 0:1], start=True, stop=True)
            em = S.op("tensor", mmz, waits=w)
            if SUB == 6:
                e_ev0 = S.op("vector", lambda e, c=c: [
                    e.tensor_copy(b.scr[c % 2][:, j * b.NT:(j + 1) * b.NT],
                                  pZp[c % 3][j][:, 0:b.NT])
                    for j in range(NJ)][-1], waits=[em])
                done[c] = S.op("vector", lambda e, c=c: e.tensor_tensor(
                    b.xs[:, c, :], b.xs[:, c, :], b.scr[c % 2][:, :], ALU.add),
                    waits=[e_ev0])
                continue

            def sqz(e, c=c):
                r = None
                for j in range(NJ):
                    r = e.activation(b.zsq[:, j * b.NT:(j + 1) * b.NT],
                                     pZp[c % 3][j][:, 0:b.NT], AF.Square,
                                     accum_out=(b.c_s1 if j == 0 else b.c_s2)[:, c:c + 1])
                    e.drain()
                return r
            e_q = S.op("scalar", sqz, waits=[em])
            if SUB in (7, 8):
                done[c] = S.op("vector", lambda e, c=c: e.tensor_tensor(
                    b.xs[:, c, :], b.xs[:, c, :], b.zsq[:, :], ALU.add),
                    waits=[e_q])
                continue

            def colm(e, c=c):
                if NJ > 1:
                    e.tensor_tensor(b.c_s1[:, c:c + 1], b.c_s1[:, c:c + 1],
                                    b.c_s2[:, c:c + 1], ALU.add)
                    e.drain()
                e.tensor_scalar(b.c_t4[:, c:c + 1], pDn[:, CT + c:CT + c + 1],
                                1.0 / D, None, ALU.mult, ALU.bypass)
                e.drain()
                e.tensor_scalar(b.c_t1[:, c:c + 1], b.c_s1[:, c:c + 1], 1.0 / D, None,
                                ALU.mult, ALU.bypass)
                e.drain()
                e.tensor_tensor(b.c_t2[:, c:c + 1], b.c_t4[:, c:c + 1],
                                b.c_t4[:, c:c + 1], ALU.mult)
                e.drain()
                e.tensor_tensor(b.c_t1[:, c:c + 1], b.c_t1[:, c:c + 1],
                                b.c_t2[:, c:c + 1], ALU.subtract)
                e.drain()
                e.tensor_tensor(b.c_t2[:, c:c + 1], b.c_rho[:, c:c + 1],
                                b.c_rho[:, c:c + 1], ALU.mult)
                e.drain()
                e.tensor_tensor(b.c_t1[:, c:c + 1], b.c_t1[:, c:c + 1],
                                b.c_t2[:, c:c + 1], ALU.mult)
                e.drain()
                return e.tensor_scalar(b.c_t1[:, c:c + 1], b.c_t1[:, c:c + 1],
                                       LN_EPS, None, ALU.add, ALU.bypass)
            e_cm = S.op("vector", colm, waits=[e_q, em])
            e_sq = S.op("scalar", lambda e, c=c: e.activation(
                b.c_t2[:, c:c + 1], b.c_t1[:, c:c + 1], AF.Sqrt), waits=[e_cm])

            def colm2(e, c=c):
                e.reciprocal(b.c_t1[:, c:c + 1], b.c_t2[:, c:c + 1])
                e.drain()
                e.tensor_tensor(b.c_sc[:, c:c + 1], b.c_rho[:, c:c + 1],
                                b.c_t1[:, c:c + 1], ALU.mult)
                e.drain()
                return e.scalar_tensor_tensor(b.c_bi[:, c:c + 1], b.c_t4[:, c:c + 1],
                                              -1.0, b.c_sc[:, c:c + 1],
                                              ALU.mult, ALU.mult)
            e_cm2 = S.op("vector", colm2, waits=[e_sq])

            def evac(e, c=c):
                r = None
                for j in range(NJ):
                    r = e.activation(b.scr[c % 2][:, j * b.NT:(j + 1) * b.NT],
                                     pZp[c % 3][j][:, 0:b.NT], AF.Identity,
                                     scale=b.c_sc[:, c:c + 1], bias=b.c_bi[:, c:c + 1])
                    e.drain()
                return r
            e_ev = S.op("scalar", evac, waits=[e_cm2])
            done[c] = S.op("vector", lambda e, c=c: e.tensor_tensor(
                b.xs[:, c, :], b.xs[:, c, :], b.scr[c % 2][:, :], ALU.add),
                waits=[e_ev])
            e_cm_last = e_cm
            if SUB >= 9 and (c + 1) % self._G4 == 0:
                self.emit_zn_group((c + 1) // self._G4 - 1, done)
        self.e_attn = [done[c] for c in range(CT)]
        self._att_bank_free = [done[CT - 2], done[CT - 2], e_nev[CT - 1],
                               e_cm_last or e_nev[CT - 1]]
        self._att_done_last = [done[CT - 1], done[CT - 2]]
        if "x1" in b.dbg:
            dx = b.dbg_tensor("x1", [128, CT, D], BF16)
            S.op("sync", lambda e, dx=dx: e.dma_start(out=dx, in_=b.xs[:, :, :]),
                 waits=list(self.e_attn), dma="s_dbg_x1")
        ps2.close()
        ps_i.close()

    def emit_zn_group(self, g, done):
        b, S = self, self.S
        D = b.D
        G4 = self._G4
        if not hasattr(self, "_tpz_acc"):
            self._tpz_acc = {}
            self._zn_prev = {}
        tpz = self._tpz_acc
        g0, g1 = g * G4, (g + 1) * G4
        sl = slice(g0, g1)
        sqr = None
        for c in range(g0, g1):
            sqr = S.op("scalar", lambda e, c=c: e.activation(
                b.jscr[:, :], b.xs[:, c, :], AF.Square,
                accum_out=b.c_s2[:, c:c + 1]), waits=[done[c]])

        def colz(e, sl=sl):
            e.tensor_copy(b.c_mu2[:, sl], b.c_mu1[:, sl])
            e.drain()
            e.tensor_scalar(b.c_t1[:, sl], b.c_s2[:, sl], 1.0 / D, None,
                            ALU.mult, ALU.bypass)
            e.drain()
            e.tensor_tensor(b.c_t2[:, sl], b.c_mu2[:, sl], b.c_mu2[:, sl],
                            ALU.mult)
            e.drain()
            e.tensor_tensor(b.c_t1[:, sl], b.c_t1[:, sl], b.c_t2[:, sl],
                            ALU.subtract)
            e.drain()
            e.tensor_scalar(b.c_t3[:, sl], b.c_t1[:, sl], 1.0 + LN_EPS,
                            LN_EPS * LN_EPS, ALU.mult, ALU.add)
            e.drain()
            return e.tensor_scalar(b.c_t1[:, sl], b.c_t1[:, sl], LN_EPS, None,
                                   ALU.add, ALU.bypass)
        e_cz = S.op("vector", colz, waits=[sqr, self.e_xstat])

        def sq2(e, sl=sl):
            e.activation(b.c_t2[:, sl], b.c_t1[:, sl], AF.Sqrt)
            e.drain()
            return e.activation(b.c_t4[:, sl], b.c_t3[:, sl], AF.Sqrt)
        e_sq = S.op("scalar", sq2, waits=[e_cz])

        def rr2(e, sl=sl):
            e.reciprocal(b.c_r2[:, sl], b.c_t2[:, sl])
            e.drain()
            return e.reciprocal(b.c_rr[:, sl], b.c_t4[:, sl])
        e_rr = S.op("vector", rr2, waits=[e_sq])
        e_zn = None
        for c in range(g0, g1):
            wt_ = [e_rr]
            if c - 2 in self._zn_prev:
                wt_.append(self._zn_prev[c - 2])
            e_t = S.op("vector", lambda e, c=c: e.tensor_scalar(
                b.mo_scr[c % 2][:, :], b.xs[:, c, :], b.c_mu2[:, c:c + 1],
                b.c_rr[:, c:c + 1], ALU.subtract, ALU.mult), waits=wt_)
            e_g1 = S.op("gpsimd", lambda e, c=c: e.tensor_tensor(
                b.xn_tm[:, c, :], b.mo_scr[c % 2][:, :], b.gb[:, :], ALU.mult),
                waits=[e_t, self.e_bc], dma=False)
            e_zn = S.op("gpsimd", lambda e, c=c: e.tensor_tensor(
                b.xn_tm[:, c, :], b.xn_tm[:, c, :], b.bb[:, :], ALU.add),
                waits=[e_g1], dma=False)
            self._zn_prev[c] = e_g1
        wtp = [e_zn, self._last_tp]
        if g == 0 and self.e_dbg_xn1 is not None:
            wtp.append(self.e_dbg_xn1)
        tpz[g] = S.op("sync", lambda e, g=g: e.dma_start(
            out=b.xnf[:, g * G4:(g + 1) * G4, :, :],
            in_=b.xn_tm[:, g * G4:(g + 1) * G4, :]
                .rearrange("p c d -> p (c d)"), transpose=True),
            waits=wtp, dma=f"s_tz_{g}")
        self._last_tp = tpz[g]
        self._tpz_groups = tpz
        self._zn_last = e_zn
        self.e_tp_zn = tpz[g]

    def ph_ffn_pre(self):
        b, S = self, self.S
        CT = b.CT
        G4 = self._G4
        NG = CT // G4
        if not hasattr(self, "_tpz_acc") or len(self._tpz_acc) < NG:
            done = {c: self.e_attn[c] for c in range(CT)}
            for g in range(NG):
                if g not in getattr(self, "_tpz_acc", {}):
                    self.emit_zn_group(g, done)
        if "zn" in b.dbg:
            dx = b.dbg_tensor("zn", [128, b.CT, b.KD, 128], BF16)
            S.op("sync", lambda e, dx=dx: e.dma_start(out=dx, in_=b.xnf[:, :, :, :]),
                 waits=[self.e_tp_zn], dma="s_dbg_zn")

    def ph_gating(self, ps):
        b, S = self, self.S
        nc, D, KD, E, CT = b.nc, b.D, b.KD, b.E, b.CT
        pg = [ps.enter_context(nc.psum_tensor(f"p_g{i}", [128, 512], F32))
              for i in range(2)]
        ptw = [ps.enter_context(nc.psum_tensor(f"p_tw{i}", [128, 512], F32))
               for i in range(2)]
        GB_OFF = D
        prev = {}
        for c in range(CT):
            w = [self._tpz_groups[min(c // self._G4, len(self._tpz_groups) - 1)],
                 self.e_w2]
            if c < 2:
                w += self._att_done_last
            if c - 2 in prev:
                w.append(prev[c - 2][0])

            def mmg(e, c=c):
                for kc in range(KD):
                    e.matmul(pg[c % 2][:, 0:E], b.xnf[:, c, kc, :], b.gw_sb[:, kc, :],
                             start=(kc == 0), stop=False)
                return e.matmul(pg[c % 2][:, 0:E], b.ones_sb[0:1, :],
                                b.rows2_bf[0:1, GB_OFF:GB_OFF + E],
                                start=False, stop=True)
            em = S.op("tensor", mmg, waits=w)

            def gate_cols(e, c=c):
                p = pg[c % 2]
                e.reduce_max(b.c_t1[:, c:c + 1], p[:, 0:E], axis=AX.X)
                e.drain()
                e.tensor_scalar(b.zsq[:, 0:E], p[:, 0:E], b.c_t1[:, c:c + 1], None,
                                ALU.is_ge, ALU.bypass)
                e.drain()
                e.scalar_tensor_tensor(b.zsq[:, E:2 * E], b.zsq[:, 0:E], -BIG,
                                       p[:, 0:E], ALU.mult, ALU.add)
                e.drain()
                e.tensor_reduce(b.c_t2[:, c:c + 1], b.zsq[:, E:2 * E], axis=AX.X,
                                op=ALU.max)
                e.drain()
                e.tensor_scalar(b.zsq[:, 2 * E:3 * E], p[:, 0:E], b.c_t2[:, c:c + 1],
                                None, ALU.is_ge, ALU.bypass)
                e.drain()
                return e.tensor_scalar(b.c_t3[:, c:c + 1], b.c_t1[:, c:c + 1], -1.0,
                                       None, ALU.mult, ALU.bypass)
            e_gc = S.op("vector", gate_cols, waits=[em])
            e_ex = S.op("scalar", lambda e, c=c: e.activation(
                b.zsq[:, 3 * E:4 * E], pg[c % 2][:, 0:E], AF.Exp,
                bias=b.c_t3[:, c:c + 1]), waits=[e_gc])

            def wfin(e, c=c):
                e.tensor_tensor(b.zsq[:, 4 * E:5 * E], b.zsq[:, 3 * E:4 * E],
                                b.zsq[:, 2 * E:3 * E], ALU.mult)
                e.drain()
                e.tensor_reduce(b.c_t4[:, c:c + 1], b.zsq[:, 4 * E:5 * E], axis=AX.X,
                                op=ALU.add)
                e.drain()
                e.reciprocal(b.c_t1[:, c:c + 1], b.c_t4[:, c:c + 1])
                e.drain()
                return e.tensor_scalar(b.w_tm[:, c, :], b.zsq[:, 4 * E:5 * E],
                                       b.c_t1[:, c:c + 1], None, ALU.mult, ALU.bypass)
            e_wv = S.op("vector", wfin, waits=[e_ex])
            wt = [e_wv, self.e_i128f]
            if c < 2:
                wt += self._att_done_last
            if c - 2 in prev:
                wt.append(prev[c - 2][1])
            em_t = S.op("tensor", lambda e, c=c: e.transpose(
                ptw[c % 2][0:E, 0:128], b.w_tm[:, c, :], b.i128f_sb[:, :]), waits=wt)
            e_wf = S.op("vector", lambda e, c=c: e.tensor_copy(
                b.w_fm[0:E, c * 128:(c + 1) * 128], ptw[c % 2][0:E, 0:128]),
                waits=[em_t])
            prev[c] = (e_wv, e_wf)
        self.e_gate = [prev[c][0] for c in range(CT)]
        self.e_gate_fm = [prev[c][1] for c in range(CT)]
        self.e_gate_last_s = e_ex
        if "w" in b.dbg:
            dw = b.dbg_tensor("w", [128, CT, E], F32)
            S.op("sync", lambda e, dw=dw: e.dma_start(out=dw, in_=b.w_tm[:, :, :]),
                 waits=list(self.e_gate), dma="s_dbg_w")

    def ph_pass(self, p):
        b, S = self, self.S
        nc, D, KD, KH, E = b.nc, b.D, b.KD, b.KH, b.E
        CF, NT, NJ = b.CF, b.NT, b.NJ
        TN, NJH, LH = b.TN, b.NJH, b.LH
        c0 = p * CF
        arena = ExitStack()
        abel_tm = arena.enter_context(nc.sbuf_tensor(f"abel{p}", [128, CF, D], BF16))
        moe_acc = arena.enter_context(nc.sbuf_tensor(f"moe_acc{p}", [128, CF, D], BF16))
        pp_ = getattr(self, "_pass_prev", None)

        aw_sb = b.aw_sb
        e_awl = self.e_awl
        ps = ExitStack()
        pa = [ps.enter_context(nc.psum_tensor(f"p_a{p}_{i}", [128, 512], F32))
              for i in range(4)]
        pdbg = getattr(self, "_pass_dbg", {})
        AB_OFF = 0
        prev = {}
        e_d1 = {}
        for cr in range(CF):
            ca = c0 + cr
            for j in range(NJ):
                idx = cr * NJ + j
                w = [e_awl,
                     self._tpz_groups[min(ca // self._G4, len(self._tpz_groups) - 1)]]
                if pp_ is not None and idx < 4:
                    w += list(pp_["rot_pairs"][idx])
                    if p - 1 in pdbg:
                        w.append(pdbg[p - 1])
                elif pp_ is None and idx < 4:
                    w.append(self._att_bank_free[idx])
                if idx - 4 in prev:
                    w.append(prev[idx - 4])

                def mma(e, ca=ca, j=j, idx=idx):
                    for kc in range(KD):
                        e.matmul(pa[idx % 4][:, 0:NT], b.xnf[:, ca, kc, :],
                                 aw_sb[:, kc, j * NT:(j + 1) * NT],
                                 start=(kc == 0), stop=False)
                    return e.matmul(pa[idx % 4][:, 0:NT], b.ones_sb[0:1, :],
                                    b.rows2_bf[0:1, AB_OFF + j * NT:AB_OFF + (j + 1) * NT],
                                    start=False, stop=True)
                em = S.op("tensor", mma, waits=w)
                wg = [em]
                if p == 0 and idx < 4:
                    wg.append(self._tpz_groups[min(1, len(self._tpz_groups) - 1)])
                if pp_ is not None:
                    wg.append(pp_["mix2"][cr])
                prev[idx] = S.op("scalar", lambda e, cr=cr, j=j, idx=idx: e.activation(
                    abel_tm[:, cr, j * NT:(j + 1) * NT], pa[idx % 4][:, 0:NT], AF.Gelu),
                    waits=wg)
            e_d1[cr] = S.op("vector", lambda e, cr=cr, ca=ca: e.scalar_tensor_tensor(
                b.jscr[:, :], abel_tm[:, cr, :], 1.0, b.w1b[:, :],
                ALU.bypass, ALU.mult, accum_out=b.c_d1[:, ca:ca + 1]),
                waits=[prev[cr * NJ + NJ - 1]])
        e_abel_last = prev[CF * NJ - 1]
        if p == 0:
            self.ph_gating(ps)
        ps.close()

        ew_es = ExitStack()
        h_fm = ew_es.enter_context(nc.sbuf_tensor(f"h{p}", [128, KH, b.TF], BF16))
        NB1 = 2
        ew1b = ew_es.enter_context(nc.sbuf_tensor(f"ew1b{p}", [128, NB1, KD, 128], BF16))
        RS2 = min(8, NJ * KH)
        ew2b = ew_es.enter_context(nc.sbuf_tensor(f"ew2b{p}", [128, RS2, NT], BF16))
        TBC = min(4, CF)
        NTB = CF // TBC
        ps = ExitStack()
        po = [ps.enter_context(nc.psum_tensor(f"p_o{p}_{i}", [128, 512], F32))
              for i in range(TBC * NJ)]
        e_w1t, e_w1c, e_w2t, e_w2c = {}, {}, {}, {}
        e_hev, e_moe = {}, {}
        self._w2_cons = {}
        self._po_evac = {}

        def sched_w1(g, extra=()):
            ei_, m_ = divmod(g, KH)
            if ei_ >= E:
                return
            w = list(extra) + [self.e_cv1[ei_]]
            if g - NB1 >= 0:
                w.append(e_w1c[divmod(g - NB1, KH)])
            e_w1t[(ei_, m_)] = S.op("sync", lambda e, ei_=ei_, m_=m_: e.dma_start(
                out=ew1b[:, (ei_ * KH + m_) % NB1, :, :], in_=b.d_ew1b[ei_, m_]),
                waits=w, dma=f"s_e1{(ei_ * KH + m_) % NB1}")

        for g0 in range(NB1):
            sched_w1(g0, extra=[e_abel_last])
        for ei in range(E):
            NPO = len(po)
            for m in range(KH):
                for j in range(NJH):
                    idx = m * NJH + j
                    gj = min((c0 + LH * (j + 1) - 1) // self._G4,
                             len(self._tpz_groups) - 1)
                    w = [e_w1t[(ei, m)], self._tpz_groups[gj]]
                    if ei == 0 and idx < NPO:
                        w.append(e_abel_last)
                        if p == 0:
                            w += [self.e_gate_fm[-1], self.e_gate_last_s,
                                  self._zn_last]
                    if ei > 0 and idx < NPO:
                        key = (ei - 1, NJ - 1, idx % NPO)
                        if key in self._po_evac:
                            w.append(self._po_evac[key])
                    if idx >= NPO:
                        pm, pj = divmod(idx - NPO, NJH)
                        w.append(e_hev[(ei, pm, pj)])

                    def hmm(e, ei=ei, m=m, j=j, idx=idx):
                        r = None
                        for kc in range(KD):
                            r = e.matmul(po[idx % NPO][:, 0:TN],
                                         ew1b[:, (ei * KH + m) % NB1, kc, :],
                                         b.xnf[:, c0 + LH * j:c0 + LH * (j + 1), kc, :],
                                         start=(kc == 0), stop=(kc == KD - 1))
                        return r
                    em = S.op("tensor", hmm, waits=w)
                    if j == NJH - 1:
                        e_w1c[(ei, m)] = em
                        sched_w1(ei * KH + m + NB1)
                    e_hev[(ei, m, j)] = S.op(
                        "scalar", lambda e, ei=ei, m=m, j=j, idx=idx: e.activation(
                            h_fm[:, m, j * TN:(j + 1) * TN], po[idx % NPO][:, 0:TN],
                            AF.Gelu, bias=b.eb1c[:, m, ei:ei + 1]), waits=[em])
            h_last = e_hev[(ei, KH - 1, NJH - 1)]
            last_mm = None
            seqpos = 0
            w2_mm = {}
            for j2 in range(NJ):
                for k2 in range(KH):
                    gseq = ei * NJ * KH + seqpos
                    slot = gseq % RS2
                    w = [self.e_cv2[ei]]
                    if seqpos == 0:
                        w.append(e_w2c[ei - 1] if ei - 1 in e_w2c else e_abel_last)
                    if gseq - RS2 >= 0:
                        w.append(self._w2_cons[gseq - RS2])
                    e_w2t[gseq] = S.op("sync", lambda e, ei=ei, k2=k2, j2=j2,
                                       slot=slot:
                        e.dma_start(out=ew2b[:, slot, :],
                                    in_=b.d_ew2b[ei, k2 * 128:(k2 + 1) * 128,
                                                 j2 * NT:(j2 + 1) * NT]),
                        waits=w, dma=f"s_e2{slot}")

                    def omm(e, ei=ei, k2=k2, slot=slot):
                        r = None
                        for cr in range(CF):
                            r = e.matmul(po[cr][:, 0:NT],
                                         h_fm[:, k2, cr * 128:(cr + 1) * 128],
                                         ew2b[:, slot, :],
                                         start=(k2 == 0), stop=(k2 == KH - 1))
                        return r
                    wm = [e_w2t[gseq], h_last]
                    if k2 == 0:
                        for cr in range(CF):
                            key = (ei, j2 - 1, cr) if j2 > 0 else (ei - 1, NJ - 1, cr)
                            if key in self._po_evac:
                                wm.append(self._po_evac[key])
                        if pp_ is not None and j2 == 0 and ei == 0:
                            wm.append(pp_["tpm"])
                            for pr2 in pp_["rot_pairs"]:
                                wm += list(pr2)
                    em = S.op("tensor", omm, waits=wm)
                    self._w2_cons[gseq] = em
                    w2_mm[(j2, k2)] = em
                    last_mm = em
                    seqpos += 1
                for cr in range(CF):
                    ca = c0 + cr

                    def oev(e, ei=ei, cr=cr, ca=ca, j2=j2):
                        sl = (slice(None), cr, slice(j2 * NT, (j2 + 1) * NT))
                        if ei == 0:
                            return e.tensor_scalar(
                                moe_acc[sl], po[cr][:, 0:NT],
                                b.w_tm[:, ca, ei:ei + 1], None,
                                ALU.mult, ALU.bypass)
                        return e.scalar_tensor_tensor(
                            moe_acc[sl], po[cr][:, 0:NT],
                            b.w_tm[:, ca, ei:ei + 1],
                            moe_acc[sl], ALU.mult, ALU.add)
                    wv = [w2_mm[(j2, KH - 1)], self.e_gate[ca]]
                    if ei == 0 and p == 0:
                        wv.append(self.e_tp_zn)
                    if ei > 0:
                        wv.append(e_moe[(ei - 1, cr, j2)])
                    e_moe[(ei, cr, j2)] = S.op("vector", oev, waits=wv)
                    self._po_evac[(ei, j2, cr)] = e_moe[(ei, cr, j2)]
            e_w2c[ei] = last_mm
        for cr in range(CF):
            ca = c0 + cr
            for j2 in range(NJ):
                tb = cr // TBC
                pi = (cr % TBC) * NJ + j2
                wb = [self.e_gate_fm[ca], self.e_w2,
                      self._po_evac[(E - 1, NJ - 1, pi)]]
                if cr - TBC >= 0:
                    wb.append(e_moe[(E, cr - TBC, j2)])
                em = S.op("tensor", lambda e, ca=ca, j2=j2, pi=pi: e.matmul(
                    po[pi][:, 0:NT], b.w_fm[0:E, ca * 128:(ca + 1) * 128],
                    b.eb2_sb[0:E, j2 * NT:(j2 + 1) * NT], start=True, stop=True),
                    waits=wb)
                e_moe[(E, cr, j2)] = S.op(
                    "vector", lambda e, cr=cr, j2=j2, pi=pi: e.scalar_tensor_tensor(
                        moe_acc[:, cr, j2 * NT:(j2 + 1) * NT], po[pi][:, 0:NT],
                        1.0, moe_acc[:, cr, j2 * NT:(j2 + 1) * NT],
                        ALU.bypass, ALU.add),
                    waits=[em, e_moe[(E - 1, cr, j2)]])
        e_moe_done = [e_moe[(E, cr, j2)] for cr in range(CF) for j2 in range(NJ)]
        ps.close()
        ew_es.close()
        if f"moe{p}" in b.dbg:
            dm = b.dbg_tensor(f"moe{p}", [128, CF, D], BF16)
            self._pass_dbg = getattr(self, "_pass_dbg", {})
            self._pass_dbg[p] = S.op(
                "sync", lambda e, dm=dm: e.dma_start(out=dm, in_=moe_acc[:, :, :]),
                waits=e_moe_done, dma=f"s_dbg_moe{p}")

        moef = arena.enter_context(nc.sbuf_tensor(f"moef{p}", [128, CF, KD, 128], BF16))
        CH = max(1, CF // 2)
        tpm_c = {}
        e_h = self._last_tp
        for h in range(CF // CH):
            wtp_ = [e_moe[(E, cr, j2)] for j2 in range(NJ)
                    for cr in range(h * CH, (h + 1) * CH)]
            wtp_.append(e_h)
            e_h = S.op("sync", lambda e, h=h: e.dma_start(
                out=moef[:, h * CH:(h + 1) * CH, :, :],
                in_=moe_acc[:, h * CH:(h + 1) * CH, :]
                    .rearrange("p c d -> p (c d)"), transpose=True),
                waits=[w_ for w_ in wtp_ if w_ is not None], dma=f"s_tm{p}")
            for cr in range(h * CH, (h + 1) * CH):
                tpm_c[cr] = e_h
        e_tpm = tpm_c[CF - 1]
        self._last_tp = e_tpm

        Rf = arena.enter_context(nc.sbuf_tensor(f"Rf{p}", [128, KD, D], BF16))
        e_Rp = S.op("sync", lambda e: e.dma_start(
            out=Rf[:, :, :], in_=b.d_spill[5]),
            waits=[self.e_R, e_w2c[E - 1], e_w1c[(E - 1, KH - 1)]], dma="s_sp")
        ps = ExitStack()
        pr = [ps.enter_context(nc.psum_tensor(f"p_r{p}_{i}", [128, 512], F32))
              for i in range(4)]
        pd = ps.enter_context(nc.psum_tensor(f"p_d{p}", [128, CF], F32))
        e_store = []
        prevrot = {}
        prev_tail = {}
        pend_o = None

        def prev_slot_evt(lst, mod, slot):
            for c_ in range(CF - 1, -1, -1):
                if ((p - 1) * CF + c_) % mod == slot:
                    return lst[c_]
            return None
        for cr in range(CF):
            ca = c0 + cr
            rot_ev = []
            for j in range(NJ):
                idx = cr * NJ + j
                w = [tpm_c[cr], e_Rp]
                if idx - 4 in prevrot:
                    w += list(prevrot[idx - 4])

                def mmr(e, cr=cr, j=j, idx=idx):
                    r = None
                    for kc in range(KD):
                        r = e.matmul(pr[idx % 4][:, 0:NT], moef[:, cr, kc, :],
                                     Rf[:, kc, j * NT:(j + 1) * NT],
                                     start=(kc == 0), stop=(kc == KD - 1))
                    return r
                em = S.op("tensor", mmr, waits=w)
                wcp = [em]
                if cr - 6 in prev_tail:
                    wcp.append(prev_tail[cr - 6]["f"])
                elif pp_ is not None:
                    ev_ = prev_slot_evt(pp_["f"], 6, (c0 + cr) % 6)
                    if ev_ is not None:
                        wcp.append(ev_)
                e_cp = S.op("scalar", lambda e, cr=cr, j=j, idx=idx: e.copy(
                    b.mo_scr[(c0 + cr) % 6][:, j * NT:(j + 1) * NT], pr[idx % 4][:, 0:NT]),
                    waits=wcp)
                prevrot[idx] = (e_cp,)
                rot_ev.append(e_cp)

            def dmm(e, cr=cr):
                r = None
                for kc in range(KD):
                    r = e.matmul(pd[:, cr:cr + 1], moef[:, cr, kc, :],
                                 b.rgc[:, kc:kc + 1],
                                 start=(kc == 0), stop=(kc == KD - 1))
                return r
            em_d = S.op("tensor", dmm, waits=[tpm_c[cr], self.e_rgc])
            if pend_o is not None:
                pend_o()
                pend_o = None

            e_gc = S.op("vector", lambda e, ca=ca, cr=cr: e.tensor_tensor(
                b.c_t1[:, ca:ca + 1], pd[:, cr:cr + 1], b.c_d1[:, ca:ca + 1],
                ALU.add), waits=[em_d, e_d1[cr]])
            e_sg = S.op("scalar", lambda e, ca=ca: e.activation(
                b.c_g[:, ca:ca + 1], b.c_t1[:, ca:ca + 1], AF.Sigmoid,
                bias=b.gcb_col[:, 0:1]), waits=[e_gc])
            wmix = list(rot_ev)
            e_mix1 = S.op("gpsimd", lambda e, cr=cr: e.tensor_tensor(
                b.mo_scr[(c0 + cr) % 6][:, :], b.mo_scr[(c0 + cr) % 6][:, :], abel_tm[:, cr, :],
                ALU.subtract), waits=wmix, dma=False)
            e_mix2 = S.op("vector", lambda e, cr=cr, ca=ca: e.scalar_tensor_tensor(
                b.mo_scr[(c0 + cr) % 6][:, :], b.mo_scr[(c0 + cr) % 6][:, :], b.c_g[:, ca:ca + 1],
                abel_tm[:, cr, :], ALU.mult, ALU.add), waits=[e_mix1, e_sg])
            wfs = [e_mix2]
            if cr - 2 in prev_tail:
                wfs.append(prev_tail[cr - 2]["st"])
            elif pp_ is not None:
                ev_ = prev_slot_evt(pp_["st"], 2, (c0 + cr) % 2)
                if ev_ is not None:
                    wfs.append(ev_)
            e_fs = S.op("vector", lambda e, ca=ca, cr=cr: e.tensor_scalar(
                b.scr[(c0 + cr) % 2][:, :], b.xs[:, ca, :], b.c_mu2[:, ca:ca + 1],
                b.c_r2[:, ca:ca + 1], ALU.subtract, ALU.mult), waits=wfs)
            e_f = S.op("vector", lambda e, cr=cr, ca=ca: e.scalar_tensor_tensor(
                b.scr[(c0 + cr) % 2][:, :], b.scr[(c0 + cr) % 2][:, :], 1.0,
                b.mo_scr[(c0 + cr) % 6][:, :], ALU.bypass, ALU.add,
                accum_out=b.c_s1[:, ca:ca + 1]), waits=[e_fs, e_mix2])
            e_fq = S.op("scalar", lambda e, ca=ca, cr=cr: e.activation(
                b.zsq[:, :], b.scr[(c0 + cr) % 2][:, :], AF.Square,
                accum_out=b.c_s2[:, ca:ca + 1]), waits=[e_f])

            def colf(e, ca=ca):
                e.tensor_scalar(b.c_mf[:, ca:ca + 1], b.c_s1[:, ca:ca + 1], 1.0 / D,
                                None, ALU.mult, ALU.bypass)
                e.drain()
                e.tensor_scalar(b.c_t1[:, ca:ca + 1], b.c_s2[:, ca:ca + 1], 1.0 / D,
                                None, ALU.mult, ALU.bypass)
                e.drain()
                e.tensor_tensor(b.c_t2[:, ca:ca + 1], b.c_mf[:, ca:ca + 1],
                                b.c_mf[:, ca:ca + 1], ALU.mult)
                e.drain()
                e.tensor_tensor(b.c_t1[:, ca:ca + 1], b.c_t1[:, ca:ca + 1],
                                b.c_t2[:, ca:ca + 1], ALU.subtract)
                e.drain()
                return e.tensor_scalar(b.c_t1[:, ca:ca + 1],
                                       b.c_t1[:, ca:ca + 1], LN_EPS, None,
                                       ALU.add, ALU.bypass)
            e_cf = S.op("vector", colf, waits=[e_f, e_fq])
            e_sf = S.op("scalar", lambda e, ca=ca: e.activation(
                b.c_t2[:, ca:ca + 1], b.c_t1[:, ca:ca + 1], AF.Sqrt),
                waits=[e_cf])

            def colf2(e, ca=ca):
                e.reciprocal(b.c_rf[:, ca:ca + 1], b.c_t2[:, ca:ca + 1])
                e.drain()
                return e.scalar_tensor_tensor(b.c_t3[:, ca:ca + 1],
                                              b.c_mf[:, ca:ca + 1], -1.0,
                                              b.c_rf[:, ca:ca + 1], ALU.mult, ALU.mult)
            e_cf2 = S.op("vector", colf2, waits=[e_sf])
            e_tf = S.op("scalar", lambda e, ca=ca, cr=cr: e.activation(
                b.zsq[:, :], b.scr[(c0 + cr) % 2][:, :], AF.Identity,
                scale=b.c_rf[:, ca:ca + 1], bias=b.c_t3[:, ca:ca + 1]),
                waits=[e_cf2, e_fq])
            def emit_o(ca=ca, cr=cr, e_tf=e_tf):
                wo_ = [e_tf]
                if len(self._stores) >= 2:
                    wo_.append(self._stores[-2])
                e_o = S.op("vector", lambda e, ca=ca, cr=cr: e.tensor_tensor(
                    b.scr[(c0 + cr) % 2][:, :], b.xs[:, ca, :], b.zsq[:, :],
                    ALU.add), waits=wo_)
                st_ = S.op("sync", lambda e, ca=ca, cr=cr: e.dma_start(
                    out=b.d_y[ca], in_=b.scr[(c0 + cr) % 2][:, :]), waits=[e_o],
                    dma=f"s_out{(p * CF + cr) % 2}")
                e_store.append(st_)
                self._stores.append(st_)
                prev_tail[cr]["st"] = st_
            pend_o = emit_o
            prev_tail[cr] = dict(mix1=e_mix1, mix2=e_mix2, f=e_f, tf=e_tf)
        pend_o()
        self.e_pass_done[p] = e_store[-1]
        rp_ = {}
        for j_ in sorted(prevrot):
            rp_[j_ % 4] = prevrot[j_]
        self._pass_prev = dict(
            rot_pairs=[rp_.get(i, ()) for i in range(4)],
            tpm=e_tpm,
            mix2=[prev_tail[c]["mix2"] for c in range(CF)],
            f=[prev_tail[c]["f"] for c in range(CF)],
            st=[prev_tail[c]["st"] for c in range(CF)],
        )
        ps.close()
        arena.close()

    def ph_tail(self):
        waits = self._stores[-2:]
        if not waits or self.cfg.get("phases", 99) < 99:
            waits = list(self.S.last_evt.values())
        self.S.op("sync", lambda e: e.nop(), waits=waits, inc=False, dma=False)



_BUILD_CACHE = {}


def _get_nc(key, cfg):
    if key not in _BUILD_CACHE:
        _BUILD_CACHE[key] = Builder(cfg).build()
    return _BUILD_CACHE[key]


def host_inputs(cfg, inp):
    D, E = cfg["D"], cfg["E"]
    KD, KH = D // 128, (2 * D) // 128
    wq = np.asarray(inp["Wq"], dtype=np.float32)
    wk = np.asarray(inp["Wk"], dtype=np.float32)
    ew1 = np.asarray(inp["ew1"], dtype=np.float32)
    gcw = np.asarray(inp["gc_w"], dtype=np.float32)[:, 0]
    rows = np.concatenate([
        np.asarray(inp["ln_gamma"], dtype=np.float32),
        np.asarray(inp["ln_beta"], dtype=np.float32),
        gcw[:D], gcw[D:],
        np.asarray(inp["abel_b"], dtype=np.float32),
        np.asarray(inp["moe_gate_b"], dtype=np.float32),
        np.asarray(inp["gc_b"], dtype=np.float32),
    ])[None, :]
    return dict(
        wqk=np.ascontiguousarray(np.concatenate([wq.T, wk.T], axis=1)),
        wv=np.ascontiguousarray(np.asarray(inp["Wv"], dtype=np.float32).T),
        wo=np.ascontiguousarray(np.asarray(inp["Wo"], dtype=np.float32).T),
        aw=np.ascontiguousarray(np.asarray(inp["abel_w"], dtype=np.float32)),
        gw=np.ascontiguousarray(np.asarray(inp["moe_gate_w"], dtype=np.float32)),
        ew1=np.ascontiguousarray(
            ew1.reshape(E, KD, 128, KH, 128).transpose(0, 3, 2, 1, 4)),
        ew2=np.ascontiguousarray(np.asarray(inp["ew2"], dtype=np.float32)),
        eb1=np.ascontiguousarray(np.asarray(inp["eb1"], dtype=np.float32)),
        eb2=np.ascontiguousarray(np.asarray(inp["eb2"], dtype=np.float32)),
        gens=np.ascontiguousarray(np.asarray(inp["gens"], dtype=np.float32)),
        rows=np.ascontiguousarray(rows),
        tril=np.triu(np.ones((128, 128), dtype=BF)),
        eye64=np.concatenate([np.eye(64, dtype=BF)] * 2, axis=0),
        i128=np.eye(128, dtype=BF),
        i128f=np.eye(128, dtype=np.float32),
        eye8f=np.eye(8, dtype=np.float32),
        ones=np.ones((128, 128), dtype=BF),
    )


def kernel(**inputs):
    cfg = default_cfg()
    D, T = cfg["D"], cfg["T"]
    CT = T // 128
    NC = cfg["NCORES"]
    nc = _get_nc("main", cfg)
    shared = host_inputs(cfg, dict(
        Wq=inputs["L0_attn_Wq"], Wk=inputs["L0_attn_Wk"], Wv=inputs["L0_attn_Wv"],
        Wo=inputs["L0_attn_Wo"], ln_gamma=inputs["L0_ffn_ln_gamma"],
        ln_beta=inputs["L0_ffn_ln_beta"], abel_w=inputs["L0_ffn_abel_w"],
        abel_b=inputs["L0_ffn_abel_b"], moe_gate_w=inputs["L0_ffn_moe_gate_w"],
        moe_gate_b=inputs["L0_ffn_moe_gate_b"], ew1=inputs["L0_ffn_ew1"],
        eb1=inputs["L0_ffn_eb1"], ew2=inputs["L0_ffn_ew2"], eb2=inputs["L0_ffn_eb2"],
        gens=inputs["L0_ffn_gens"], gc_w=inputs["L0_ffn_gc_w"],
        gc_b=inputs["L0_ffn_gc_b"]))
    x = np.asarray(inputs["L0_attn_x"], dtype=np.float32)
    in_maps = []
    for c in range(NC):
        m = dict(shared)
        m["x"] = np.ascontiguousarray(x[c].reshape(CT, 128, D))
        in_maps.append(m)
    res = run_bass_kernel_spmd(nc, in_maps, core_ids=list(range(NC)))
    return np.stack([np.asarray(res.results[c]["y"]).reshape(T, D)
                     for c in range(NC)])


# version 30
# speedup vs baseline: 1.0035x; 1.0035x over previous
import math
from contextlib import ExitStack

import numpy as np
import ml_dtypes

import concourse.bass as bass
import concourse.mybir as mybir
from concourse.bass_utils import run_bass_kernel_spmd

F32 = mybir.dt.float32
BF16 = mybir.dt.bfloat16
AF = mybir.ActivationFunctionType
ALU = mybir.AluOpType
AX = mybir.AxisListType
BF = ml_dtypes.bfloat16

LN_EPS = 1e-5
DEN_EPS = 1e-6
BIG = 30000.0

ENGS = ("tensor", "vector", "scalar", "gpsimd", "sync")

_CNAMES = ["c0", "c1", "c2", "c3", "c4", "c5", "c6",
           "s0", "s1", "s2", "s3", "s4", "s5", "s6"]
COEF_IDX = {n: i for i, n in enumerate(_CNAMES)}


def coef_values():
    bc = [1.0 / math.factorial(2 * j) for j in range(9)]
    bs = [1.0 / math.factorial(2 * j + 1) for j in range(9)]
    v = {"c0": bc[0], "c1": bc[1], "c2": bc[2], "c3": bc[3], "c4": bc[4],
         "c5": bc[5], "c6": bc[6],
         "s0": bs[0], "s1": bs[1], "s2": bs[2], "s3": bs[3], "s4": bs[4],
         "s5": bs[5], "s6": bs[6]}
    return [v[n] for n in _CNAMES]


def default_cfg():
    return dict(T=2048, D=1024, AK=64, E=8, NROT=3, NP=2, NCORES=8, dbg=())


class Evt:
    __slots__ = ("eng", "val", "gid")

    def __init__(self, eng, val, gid):
        self.eng = eng
        self.val = val
        self.gid = gid


class Sched:

    COMPUTE_KEYS = ("tensor", "vector", "scalar", "gpsimd_c", "sync_c")

    def __init__(self):
        self.ops = {e: [] for e in ENGS}
        self.cnt = {}
        self.gid = 0

    def op(self, eng, fn, waits=(), inc=True, dma=None):
        self.gid += 1
        g = self.gid
        for w in waits:
            assert isinstance(w, Evt) and w.gid < g, "wait on future op"
        if isinstance(dma, str):
            key, stp = dma, 16
        elif eng == "gpsimd":
            assert dma is False, "gpsimd op needs dma channel or dma=False"
            key, stp = "gpsimd_c", 1
        elif eng == "sync":
            assert dma is False or dma is None
            key, stp = "sync_c", 1
        else:
            assert dma is None
            key, stp = eng, 1
        evt = None
        if inc:
            self.cnt[key] = self.cnt.get(key, 0) + stp
            evt = Evt(key, self.cnt[key], g)
            self.last_evt = getattr(self, "last_evt", {})
            self.last_evt[key] = evt
        elif key not in self.cnt:
            self.cnt[key] = 0
        self.ops[eng].append((fn, tuple(waits), evt, key, stp))
        return evt

    def emit(self, blk, sems):
        for eng in ENGS:
            oplist = self.ops[eng]
            if not oplist:
                continue

            def run(e, oplist=oplist):
                waited = {}
                first = True
                for fn, waits, evt, key, stp in oplist:
                    need = {}
                    for w in waits:
                        if w.eng == key and key in self.COMPUTE_KEYS:
                            continue
                        need[w.eng] = max(need.get(w.eng, 0), w.val)
                    for s_, v_ in need.items():
                        if waited.get(s_, 0) < v_:
                            e.wait_ge(sems[s_], v_)
                            waited[s_] = v_
                    if key in ("vector", "scalar", "gpsimd_c") and not first:
                        e.drain()
                    first = False
                    last = fn(e)
                    if evt is not None:
                        assert last is not None, f"op on {eng} returned None"
                        last.then_inc(sems[key], stp)
            getattr(blk, eng)(run)


class Builder:
    def __init__(self, cfg):
        self.cfg = cfg
        self.nc = bass.Bass("TRN2", target_bir_lowering=False)
        self.es = ExitStack()
        c = cfg
        self.D, self.T, self.AK, self.E = c["D"], c["T"], c["AK"], c["E"]
        self.NROT, self.NP = c["NROT"], c["NP"]
        self.KD = self.D // 128
        self.CT = self.T // 128
        self.H = 2 * self.D
        self.KH = self.H // 128
        self.TF = self.T // self.NP
        self.CF = self.TF // 128
        self.NT = min(512, self.D)
        self.NJ = self.D // self.NT
        self.QN = min(512, self.T)
        self.NQ = self.T // self.QN
        self.LQ = self.QN // 128
        self.TN = min(512, self.TF)
        self.NJH = self.TF // self.TN
        self.LH = self.TN // 128
        self.dbg = set(c.get("dbg", ()))
        self.dbg_outs = []

    def declare_io(self):
        nc = self.nc
        D, AK, E, H, KD, KH, CT = self.D, self.AK, self.E, self.H, self.KD, self.KH, self.CT
        dt = nc.dram_tensor
        self.d = {}

        def di(name, shape, dtype=F32):
            self.d[name] = dt(name, shape, dtype, kind="ExternalInput").ap()
        di("x", [CT, 128, D])
        di("wqk", [D, 128]); di("wv", [D, AK]); di("wo", [AK, D])
        di("aw", [D, D]); di("gw", [D, E])
        di("ew1", [E, KH, 128, KD, 128])
        di("ew2", [E, H, D])
        di("eb1", [E, H]); di("eb2", [E, D])
        di("gens", [self.NROT, D, D])
        self.ROWL = 5 * D + E + 1
        di("rows", [1, self.ROWL])
        di("tril", [128, 128], BF16)
        di("eye64", [128, 64], BF16)
        di("i128", [128, 128], BF16)
        di("i128f", [128, 128])
        di("eye8f", [8, 8])
        di("ones", [128, 128], BF16)
        self.d_y = dt("y", [CT, 128, D], F32, kind="ExternalOutput").ap()
        self.d_spill = dt("spill", [6, 128, KD, D], BF16, kind="Internal").ap()
        self.d_ew1b = dt("ew1b_h", [E, KH, 128, KD, 128], BF16, kind="Internal").ap()
        self.d_ew2b = dt("ew2b_h", [E, H, D], BF16, kind="Internal").ap()
        self.d_awb = dt("awb_h", [D, D], BF16, kind="Internal").ap()
        self.d_wqkb = dt("wqkb_h", [D, 128], BF16, kind="Internal").ap()
        self.d_wvb = dt("wvb_h", [D, self.AK], BF16, kind="Internal").ap()
        self.d_wob = dt("wob_h", [self.AK, D], BF16, kind="Internal").ap()
        self.d_rows4b = dt("rows4b_h", [1, 4 * D], BF16, kind="Internal").ap()

    def dbg_tensor(self, name, shape, dtype):
        ap = self.nc.dram_tensor("dbg_" + name, shape, dtype, kind="ExternalOutput").ap()
        self.dbg_outs.append("dbg_" + name)
        return ap

    def alloc_persistent(self):
        nc, es = self.nc, self.es
        D, AK, E = self.D, self.AK, self.E
        KD, KH, CT = self.KD, self.KH, self.CT
        sb = lambda name, shape, dtype: es.enter_context(nc.sbuf_tensor(name, shape, dtype))
        self.gw_sb = sb("gw_sb", [128, KD, E], BF16)
        self.eb2_sb = sb("eb2_sb", [E, D], BF16)
        self.eb1c = sb("eb1c", [128, KH, E], F32)
        self.eye8f_sb = sb("eye8f_sb", [8, 8], F32)
        self.ones_sb = sb("ones_sb", [128, 128], BF16)
        self.rows2_bf = sb("rows2_bf", [1, D + E + 1], BF16)
        self.gcb_col = sb("gcb_col", [128, 1], F32)
        self.eps5 = sb("eps5", [128, 1], F32)
        self.eps6 = sb("eps6", [128, 1], F32)
        col = lambda name: sb(name, [128, CT], F32)
        for n in ("c_s1", "c_s2", "c_mu1", "c_r1", "c_mu2", "c_r2", "c_muu",
                  "c_ru", "c_rho", "c_t1", "c_t2", "c_t3", "c_t4",
                  "c_sc", "c_bi", "c_d1", "c_d2a", "c_d2b", "c_g",
                  "c_mf", "c_rf", "c_var2", "c_rr"):
            setattr(self, n, col(n))
        self.xs = sb("xs", [128, CT, self.D], BF16)
        self.rgc = sb("rgc", [128, KD], BF16)
        self.rgc_f = sb("rgc_f", [128, KD], F32)
        self.grow = sb("grow", [1, D], BF16)

    def build(self):
        self.declare_io()
        self.alloc_persistent()
        self.S = Sched()
        PH = self.cfg.get("phases", 99)
        with self.nc.Block() as blk:
            self.ph_loads()
            if PH >= 1:
                self.ph_prep()
            if PH >= 2:
                self.ph_expm()
            else:
                self._gstack.close()
            self.alloc_main()
            if PH >= 3:
                self.ph_attention()
            if PH >= 4:
                self.ph_ffn_pre()
            if PH >= 5:
                self._att.close()
            self.e_pass_done = {}
            self._stores = []
            if PH >= 6:
                for p in range(min(self.NP, PH - 5)):
                    self.ph_pass(p)
            self.ph_tail()
            if PH < 5 and hasattr(self, "_att"):
                self._att.close()
            if PH in (5, 6) and hasattr(self, "_gate_ps"):
                pass
            if PH < 2 and hasattr(self, "_prep_ps"):
                self._prep_ps.close()
            sems = {k: self.es.enter_context(self.nc.semaphore("sem_" + k))
                    for k in self.S.cnt}
            self.S.emit(blk, sems)
        self.es.close()
        return self.nc

    def alloc_main(self):
        nc, es = self.nc, self.es
        D, CT, KD = self.D, self.CT, self.KD
        sb = lambda name, shape, dtype: es.enter_context(nc.sbuf_tensor(name, shape, dtype))
        self.i128f_sb = sb("i128f_sb", [128, 128], F32)
        self.i128b_sb = sb("i128b_sb", [128, 128], BF16)
        self.w_tm = sb("w_tm", [128, CT, self.E], F32)
        self.w_fm = sb("w_fm", [self.E, self.T], BF16)
        self.xnf = sb("xnf", [128, CT, KD, 128], BF16)
        self.aw_sb = sb("aw_sb", [128, KD, D], BF16)
        self.w1b = sb("w1b", [128, D], BF16)
        self.scr = [sb(f"scr{i}", [128, D], F32) for i in range(2)]
        self.zsq = sb("zsq", [128, D], F32)
        self.jscr = sb("jscr", [128, D], BF16)
        self.mo_scr = [sb(f"mo{i}", [128, D], BF16) for i in range(6)]

    def ph_loads(self):
        b, S = self, self.S
        self._last_tp = None
        D = b.D
        self._gstack = ExitStack()
        self.mGf = self._gstack.enter_context(
            b.nc.sbuf_tensor("mGf", [128, b.KD, D], F32))
        self.e_gl0 = S.op("sync", lambda e: e.dma_start(
            out=b.mGf[:, :, :],
            in_=b.d["gens"][0].rearrange("(k p) d -> p k d", p=128)),
            dma="s_gens")
        last = None
        for dst, src in [
            (b.eye8f_sb[:, :], b.d["eye8f"]),
            (b.ones_sb[:, :], b.d["ones"]),
        ]:
            last = S.op("sync", lambda e, d=dst, s_=src: e.dma_start(out=d, in_=s_),
                        dma="s_cst")
        self.e_cst = last
        self.e_x = S.op("gpsimd", lambda e: e.dma_start(
            out=b.xs[:, :, :], in_=b.d["x"].rearrange("c t d -> t c d")),
            waits=[self.e_gl0], dma="g_x")
        e1_ = S.op("gpsimd", lambda e: e.memset(b.eps5[:, :], LN_EPS), dma=False)
        self.e_eps = S.op("gpsimd", lambda e: e.memset(b.eps6[:, :], DEN_EPS),
                          waits=[e1_], dma=False)
        last = None
        for dst, src in [
            (b.gw_sb[:, :, :], b.d["gw"].rearrange("(k p) m -> p k m", p=128)),
            (b.eb2_sb[:, :], b.d["eb2"]),
            (b.rows2_bf[:, :], b.d["rows"][0:1, 4 * D:]),
            (b.grow[:, :], b.d["rows"][0:1, 3 * D:4 * D]),
        ]:
            last = S.op("gpsimd", lambda e, d=dst, s_=src: e.dma_start(out=d, in_=s_),
                        dma="g_w")
        self.e_w = last
        self.e_cvA = S.op("gpsimd", lambda e: e.dma_start(
            out=b.d_awb, in_=b.d["aw"]), dma="g_cvA")
        for dst_, src_ in [(b.d_wqkb, b.d["wqk"]), (b.d_wvb, b.d["wv"]),
                           (b.d_wob, b.d["wo"]),
                           (b.d_rows4b, b.d["rows"][0:1, 0:4 * D])]:
            self.e_cvW = S.op("gpsimd", lambda e, d=dst_, s_=src_: e.dma_start(
                out=d, in_=s_), dma="g_cvW")
        self.e_cv1, self.e_cv2 = {}, {}
        for ei in range(self.E):
            self.e_cv1[ei] = S.op("gpsimd", lambda e, ei=ei: e.dma_start(
                out=b.d_ew1b[ei], in_=b.d["ew1"][ei]), dma=f"g_cv1_{ei}")
            self.e_cv2[ei] = S.op("gpsimd", lambda e, ei=ei: e.dma_start(
                out=b.d_ew2b[ei], in_=b.d["ew2"][ei]), dma=f"g_cv2_{ei}")

    def ph_prep(self):
        b, S = self, self.S
        nc, D, E, KH = b.nc, b.D, b.E, b.KH
        ps = ExitStack()
        b._eb1r = ps.enter_context(nc.sbuf_tensor("eb1r", [E, self.H], F32))
        pb = [ps.enter_context(nc.psum_tensor(f"p_pr{i}", [128, 512], F32)) for i in range(2)]
        e_eb1 = S.op("sync", lambda e: e.dma_start(out=b._eb1r[:, :], in_=b.d["eb1"]),
                     dma="s_eb1")
        em = S.op("tensor", lambda e: e.matmul(
            pb[0][:, 0:1], b.ones_sb[0:1, :],
            b.rows2_bf[0:1, D + E:D + E + 1], start=True, stop=True),
            waits=[self.e_cst, self.e_w])
        e_gcb = S.op("vector", lambda e: e.tensor_copy(b.gcb_col[:, :], pb[0][:, 0:1]),
                     waits=[em])
        prev = {}
        for m in range(KH):
            pt = pb[m % 2]
            w = [e_eb1, e_gcb] + ([prev[m - 2]] if m - 2 in prev else [])
            em = S.op("tensor", lambda e, pt=pt, m=m: e.transpose(
                pt[:, 0:E], b._eb1r[0:E, m * 128:(m + 1) * 128], b.eye8f_sb[:, :]),
                waits=w)
            prev[m] = S.op("vector", lambda e, pt=pt, m=m: e.tensor_copy(
                b.eb1c[:, m, :], pt[:, 0:E]), waits=[em])
        self.e_prep = prev[KH - 1]
        self._prep_ps = ps

    def ph_expm(self):
        b, S = self, self.S
        nc, D, KD, NJ, NT = b.nc, b.D, b.KD, b.NJ, b.NT
        self._prep_ps.close()
        ws = ExitStack()
        sbm = lambda n: ws.enter_context(nc.sbuf_tensor(n, [128, KD, D], BF16))
        mG = sbm("mG")
        mB, mB2, mB3 = sbm("mB"), sbm("mB2"), sbm("mB3")
        mCmb, mU, mC, mS_ = sbm("mCmb"), sbm("mU"), sbm("mC"), sbm("mS")
        mGf = b.mGf
        b.ceye_sb = ws.enter_context(
            nc.sbuf_tensor("ceye_sb", [128, len(_CNAMES), 128], BF16))
        b.i128_sb = ws.enter_context(nc.sbuf_tensor("i128_sb", [128, 128], BF16))
        e_i1 = S.op("sync", lambda e: e.dma_start(
            out=b.i128_sb[:, :], in_=b.d["i128"]), waits=[], dma="s_ce")
        e_ce = None
        for ci, cv in enumerate(coef_values()):
            e_ce = S.op("scalar", lambda e, ci=ci, cv=cv: e.activation(
                b.ceye_sb[:, ci, :], b.i128_sb[:, :], AF.Identity, scale=cv),
                waits=[e_i1])
        ps = ExitStack()
        pp = [ps.enter_context(nc.psum_tensor(f"p_e{i}", [128, 512], F32)) for i in range(4)]
        pbx = ps.enter_context(nc.psum_tensor("p_ebx", [128, 512], F32))
        wsG = ws.enter_context(nc.sbuf_tensor("wsG", [128, D], BF16))

        bc = [1.0 / math.factorial(2 * j) for j in range(9)]
        bs = [1.0 / math.factorial(2 * j + 1) for j in range(9)]
        CE = COEF_IDX

        def mat_fm(lhsT, rhs, out, waits, scale=1.0, addm=None, diag=None,
                   addm_w=(), coefs=()):
            evacs = {}
            last = None
            for mc in range(KD):
                for j in range(NJ):
                    ti = mc * NJ + j
                    pt = pp[ti % 4]
                    w = list(waits) + ([evacs[ti - 4]] if ti - 4 in evacs else [])
                    dj = None
                    if diag is not None:
                        o = mc * 128 - j * NT
                        if 0 <= o and o + 128 <= NT:
                            dj = o

                    def mm(e, pt=pt, mc=mc, j=j, dj=dj, lhsT=lhsT, rhs=rhs):
                        r = None
                        n_extra = len(coefs) + (1 if dj is not None else 0)
                        for kc in range(KD):
                            stop = (kc == KD - 1) and n_extra == 0
                            r = e.matmul(pt[:, 0:NT],
                                         lhsT[:, kc, mc * 128:(mc + 1) * 128],
                                         rhs[:, kc, j * NT:(j + 1) * NT],
                                         start=(kc == 0), stop=stop)
                        k = 0
                        for ci, mat in coefs:
                            k += 1
                            stop = (k == n_extra)
                            r = e.matmul(pt[:, 0:NT], b.ceye_sb[:, ci, :],
                                         mat[:, mc, j * NT:(j + 1) * NT],
                                         start=False, stop=stop)
                        if dj is not None:
                            r = e.matmul(pt[:, dj:dj + 128], b.i128_sb[:, :],
                                         b.ceye_sb[:, diag, :],
                                         start=False, stop=True)
                        return r
                    em = S.op("tensor", mm, waits=w)

                    if addm is None:
                        def evs(e, pt=pt, mc=mc, j=j, out=out, scale=scale):
                            sl = (slice(None), mc, slice(j * NT, (j + 1) * NT))
                            return e.activation(out[sl], pt[:, 0:NT], AF.Identity,
                                                scale=scale)
                        evacs[ti] = last = S.op("scalar", evs, waits=[em])
                    else:
                        def ev(e, pt=pt, mc=mc, j=j, out=out, addm=addm, scale=scale):
                            sl = (slice(None), mc, slice(j * NT, (j + 1) * NT))
                            if scale == 1.0:
                                return e.tensor_tensor(out[sl], pt[:, 0:NT],
                                                       addm[sl], ALU.add)
                            return e.scalar_tensor_tensor(out[sl], pt[:, 0:NT], scale,
                                                          addm[sl], ALU.mult, ALU.add)
                        evacs[ti] = last = S.op("vector", ev,
                                                waits=[em] + list(addm_w))
            return last

        def combo(dst, srcs, waits):
            def vf(e):
                r = None
                for n_, (cf, m_) in enumerate(srcs):
                    if n_ > 0:
                        e.drain()
                    if n_ == 0:
                        r = e.tensor_scalar(dst[:, :, :], m_[:, :, :], cf, None,
                                            ALU.mult, ALU.bypass)
                    else:
                        r = e.scalar_tensor_tensor(dst[:, :, :], m_[:, :, :], cf,
                                                   dst[:, :, :], ALU.mult, ALU.add)
                return r
            return S.op("vector", vf, waits=waits)

        prev_last = [self.e_prep]
        sp_gate = []
        e_gl = self.e_gl0
        e_cast_last = None
        for i in range(b.NROT):
            e_g = S.op("vector", lambda e: e.tensor_scalar(
                mG[:, :, :], mGf[:, :, :], 1.0, None, ALU.mult, ALU.bypass),
                waits=[e_gl] + list(prev_last))
            e_cast_last = e_g
            if i + 1 < b.NROT:
                e_gl = S.op("sync", lambda e, i=i: e.dma_start(
                    out=mGf[:, :, :],
                    in_=b.d["gens"][i + 1].rearrange("(k p) d -> p k d", p=128)),
                    waits=[e_g], dma="s_gens")
            eB = mat_fm(mG, mG, mB, [e_g, e_ce], scale=-1.0)
            eB2 = mat_fm(mB, mB, mB2, [eB])
            eB3 = mat_fm(mB2, mB, mB3, [eB2])
            def acc3(dst, cfs, first_w):
                t1 = S.op("vector", lambda e: e.tensor_scalar(
                    dst[:, :, :], mB[:, :, :], cfs[0], None, ALU.mult,
                    ALU.bypass), waits=[eB] + first_w)
                t2 = S.op("vector", lambda e: e.scalar_tensor_tensor(
                    dst[:, :, :], mB2[:, :, :], cfs[1], dst[:, :, :],
                    ALU.mult, ALU.add), waits=[eB2, t1])
                return S.op("vector", lambda e: e.scalar_tensor_tensor(
                    dst[:, :, :], mB3[:, :, :], cfs[2], dst[:, :, :],
                    ALU.mult, ALU.add), waits=[eB3, t2])
            eD2 = acc3(mCmb, (bc[4], bc[5], bc[6]), sp_gate)
            eD2s = acc3(mU, (bs[4], bs[5], bs[6]), sp_gate)
            eC = mat_fm(mB3, mCmb, mC, [eD2],
                        coefs=[(CE["c1"], mB), (CE["c2"], mB2),
                               (CE["c3"], mB3)], diag=CE["c0"])
            eS = mat_fm(mB3, mU, mS_, [eD2s, eC],
                        coefs=[(CE["s1"], mB), (CE["s2"], mB2),
                               (CE["s3"], mB3)], diag=CE["s0"])
            if i == 0:
                eEv = mat_fm(mS_, mG, mU, [eS], scale=-1.0, addm=mC)
            else:
                eEv = mat_fm(mG, mS_, mU, [eS], scale=-1.0, addm=mC)
            prev_last = [eEv]
            if i < 2:
                e_sp1 = S.op("sync", lambda e, i=i: e.dma_start(
                    out=b.d_spill[i], in_=mU[:, :, :]), waits=[eEv], dma="s_sp")
                sp_gate = [e_sp1]
            else:
                sp_gate = []
        red = None
        for c in range(b.CT):
            red = S.op("vector", lambda e, c=c: e.reduce_sum(
                b.c_s1[:, c:c + 1], b.xs[:, c, :], axis=AX.X),
                waits=[self.e_x, e_cast_last])
        e_xred = red
        e_l = None
        for idx, dst in [(0, mB2), (1, mB3)]:
            e_l = S.op("sync", lambda e, idx=idx, dst=dst: e.dma_start(
                out=dst[:, :, :], in_=b.d_spill[idx]), waits=list(prev_last),
                dma="s_sp")
        eM = mat_fm(mB2, mB3, mC, [e_l])
        eMT = mat_fm(mB3, mB2, mS_, [eM])
        eRv = mat_fm(mS_, mU, mCmb, [eMT])
        self.e_R = S.op("sync", lambda e: e.dma_start(
            out=b.d_spill[5], in_=mCmb[:, :, :]), waits=[eRv], dma="s_sp")
        e_bx = None
        for j in range(b.NJ):
            embx = S.op("tensor", lambda e, j=j: e.matmul(
                pbx[:, 0:NT], b.ones_sb[0:1, :], b.grow[0:1, j * NT:(j + 1) * NT],
                start=True, stop=True),
                waits=[self.e_cst, self.e_w] + ([e_bx] if e_bx else []))
            e_bx = S.op("vector", lambda e, j=j: e.tensor_copy(
                wsG[:, j * NT:(j + 1) * NT], pbx[:, 0:NT]), waits=[embx])
        e_rg = None
        for kc in range(KD):
            e_rg = S.op("vector", lambda e, kc=kc: e.scalar_tensor_tensor(
                mGf[:, 0, :], mCmb[:, kc, :], 1.0, wsG[:, :],
                ALU.bypass, ALU.mult, accum_out=b.rgc_f[:, kc:kc + 1]),
                waits=[eRv, e_bx])
        self.e_rgc = S.op("scalar", lambda e: e.activation(
            b.rgc[:, :], b.rgc_f[:, :], AF.Identity), waits=[e_rg])
        red2 = {}
        sqr = None
        for c in range(b.CT):
            sl = 2 + (c % 2) if KD >= 4 else 1
            w = [self.e_x, e_cast_last]
            back = 2 if KD >= 4 else 1
            if c - back in red2:
                w.append(red2[c - back])
            sqr = S.op("scalar", lambda e, c=c, sl=sl: e.activation(
                mGf[:, sl, :], b.xs[:, c, :], AF.Square), waits=w)
            red2[c] = S.op("vector", lambda e, c=c, sl=sl: e.reduce_sum(
                b.c_s2[:, c:c + 1], mGf[:, sl, :], axis=AX.X), waits=[sqr])
        self.e_xstat = self.emit_mu_r(b.c_s1, b.c_s2, b.c_mu1, b.c_r1,
                                      [e_xred, red2[b.CT - 1]])
        if "R" in b.dbg:
            dR = b.dbg_tensor("R", [128, KD, D], BF16)
            S.op("sync", lambda e, dR=dR: e.dma_start(out=dR, in_=b.d_spill[5]),
                 waits=[self.e_R], dma="s_dbg_R")
        ps.close()
        ws.close()
        self._gstack.close()

    def emit_mu_r(self, s1, s2, mu, r, waits, var_out=None, sl=slice(None),
                  mu_src=None):
        b, S, D = self, self.S, self.D

        def colm(e):
            if mu_src is None:
                e.tensor_scalar(mu[:, sl], s1[:, sl], 1.0 / D, None, ALU.mult,
                                ALU.bypass)
            else:
                e.tensor_copy(mu[:, sl], mu_src[:, sl])
            e.drain()
            e.tensor_scalar(b.c_t1[:, sl], s2[:, sl], 1.0 / D, None, ALU.mult,
                            ALU.bypass)
            e.drain()
            e.tensor_tensor(b.c_t2[:, sl], mu[:, sl], mu[:, sl], ALU.mult)
            e.drain()
            e.tensor_tensor(b.c_t1[:, sl], b.c_t1[:, sl], b.c_t2[:, sl],
                            ALU.subtract)
            if var_out is not None:
                e.drain()
                e.tensor_copy(var_out[:, sl], b.c_t1[:, sl])
            e.drain()
            return e.tensor_scalar(b.c_t1[:, sl], b.c_t1[:, sl], LN_EPS, None,
                                   ALU.add, ALU.bypass)
        e1 = S.op("vector", colm, waits=waits)
        e2 = S.op("scalar", lambda e: e.activation(
            b.c_t2[:, sl], b.c_t1[:, sl], AF.Sqrt), waits=[e1])
        return S.op("vector", lambda e: e.reciprocal(r[:, sl], b.c_t2[:, sl]),
                    waits=[e2])

    def ph_attention(self):
        b, S = self, self.S
        nc, D, KD, AK, T, CT = b.nc, b.D, b.KD, b.AK, b.T, b.CT
        att = ExitStack()
        self._att = att
        wsret = [getattr(self, n) for n in ("e_R", "e_xstat", "e_rgc")
                 if hasattr(self, n)] or [self.e_prep]
        eR0 = wsret[0]
        sb = lambda n, shape, dtype: att.enter_context(nc.sbuf_tensor(n, shape, dtype))
        b.xn_tm = sb("xn_tm", [128, CT, D], BF16)
        b.gb = sb("gb", [128, D], BF16)
        b.bb = sb("bb", [128, D], BF16)
        rows4 = sb("rows4", [1, 4 * D], BF16)
        wqk_sb = sb("wqk_sb", [128, KD, 128], BF16)
        wv_sb = sb("wv_sb", [128, KD, AK], BF16)
        wo_sb = sb("wo_sb", [AK, D], BF16)
        b.tril_sb = sb("tril_sb", [128, 128], BF16)
        b.eye64_sb = sb("eye64_sb", [128, 64], BF16)
        e_trl = S.op("sync", lambda e: e.dma_start(
            out=b.tril_sb[:, :], in_=b.d["tril"]), waits=list(wsret), dma="s_att")
        self.e_i128f = S.op("sync", lambda e: e.dma_start(
            out=b.i128f_sb[:, :], in_=b.d["i128f"]), waits=list(wsret),
            dma="s_att")
        self.e_i128b = S.op("sync", lambda e: e.dma_start(
            out=b.i128b_sb[:, :], in_=b.d["i128"]), waits=list(wsret),
            dma="s_att")
        e_trl = S.op("sync", lambda e: e.dma_start(
            out=b.eye64_sb[:, :], in_=b.d["eye64"]), waits=[], dma="s_att")
        Q_fm = sb("Q_fm", [64, T], BF16)
        K_fm = sb("K_fm", [64, T], BF16)
        V_fm = sb("V_fm", [64, T], BF16)
        K_tm = sb("K_tm", [128, CT, 64], BF16)
        V1_tm = sb("V1_tm", [128, CT, 65], BF16)
        num_fm = sb("num_fm", [64, T], BF16)
        S_all = sb("S_all", [64, CT * 65], BF16)
        S_acc = sb("S_acc", [64, 65], F32)
        A_sb = sb("A_sb", [128, 2, 128], BF16)
        escr = sb("escr", [128, 512], F32)
        escr2 = sb("escr2", [128, 512], F32)
        wo_rsf = sb("wo_rsf", [64, 1], F32)
        wo_rsb = sb("wo_rsb", [64, 1], BF16)

        eR = wsret[0]
        self.e_awl = S.op("sync", lambda e: e.dma_start(
            out=b.aw_sb[:, :, :],
            in_=b.d_awb.rearrange("(k p) d -> p k d", p=128)),
            waits=wsret + [self.e_cvA], dma="s_aw")
        last = None
        for dst, srcap in [
            (wqk_sb[:, :, :], b.d_wqkb.rearrange("(k p) m -> p k m", p=128)),
            (wv_sb[:, :, :], b.d_wvb.rearrange("(k p) m -> p k m", p=128)),
            (wo_sb[:, :], b.d_wob),
            (rows4[:, :], b.d_rows4b),
        ]:
            last = S.op("sync", lambda e, d=dst, s_=srcap: e.dma_start(
                out=d, in_=s_), waits=list(wsret) + [self.e_cvW], dma="s_att")
        self.e_w2 = last
        psb = ExitStack()
        pbx = [psb.enter_context(nc.psum_tensor(f"p_bx{i}", [128, 512], F32))
               for i in range(2)]
        wbt = min(512, D)
        nbt = max(1, D // wbt)
        bca = {}
        dsts = [b.gb, b.bb, b.w1b]
        for r in range(3):
            for j in range(nbt):
                idx = r * nbt + j
                w = [self.e_w2] + ([bca[idx - 2]] if idx - 2 in bca else [])
                off = r * D + j * wbt
                em0 = S.op("tensor", lambda e, off=off, idx=idx: e.matmul(
                    pbx[idx % 2][:, 0:wbt], b.ones_sb[0:1, :],
                    rows4[0:1, off:off + wbt], start=True, stop=True), waits=w)
                bca[idx] = S.op("vector", lambda e, r=r, j=j, idx=idx: e.tensor_copy(
                    dsts[r][:, j * wbt:(j + 1) * wbt], pbx[idx % 2][:, 0:wbt]),
                    waits=[em0])
        self.e_bc = bca[3 * nbt - 1]
        psb.close()

        e_mr = self.e_xstat
        if "mu1" in b.dbg:
            dmu = b.dbg_tensor("mu1", [128, 2 * CT], F32)
            S.op("sync", lambda e, dmu=dmu: e.dma_start(
                out=dmu[:, 0:CT], in_=b.c_mu1[:, 0:CT]), waits=[e_mr],
                dma="s_dbg_mu1")
            S.op("sync", lambda e, dmu=dmu: e.dma_start(
                out=dmu[:, CT:2 * CT], in_=b.c_r1[:, 0:CT]), waits=[e_mr],
                dma="s_dbg_mu1")
        e_xn_parts = {}
        G4p = 4 if CT % 4 == 0 else CT
        for c in range(CT):
            e_xn = S.op("vector", lambda e, c=c: e.tensor_scalar(
                b.xn_tm[:, c, :], b.xs[:, c, :], b.c_mu1[:, c:c + 1],
                b.c_r1[:, c:c + 1], ALU.subtract, ALU.mult),
                waits=[e_mr] + wsret)
            if (c + 1) % G4p == 0:
                e_xn_parts[c // G4p] = e_xn
        G4 = 4 if CT % 4 == 0 else CT
        NG = CT // G4
        tpz1 = {}
        for g in range(NG):
            w = [e_xn_parts[g]]
            if self._last_tp is not None:
                w.append(self._last_tp)
            e_tpg = S.op("sync", lambda e, g=g: e.dma_start(
                out=b.xnf[:, g * G4:(g + 1) * G4, :, :],
                in_=b.xn_tm[:, g * G4:(g + 1) * G4, :]
                    .rearrange("p c d -> p (c d)"), transpose=True),
                waits=w, dma=f"s_t1_{g}")
            tpz1[g] = e_tpg
            self._last_tp = e_tpg
        e_tp = tpz1[NG - 1]
        self.e_tp_xn1 = e_tp
        self._tp1_groups = tpz1
        self._G4 = G4
        self.e_dbg_xn1 = None
        if "xn1" in b.dbg:
            dx = b.dbg_tensor("xn1", [128, CT, KD, 128], BF16)
            self.e_dbg_xn1 = S.op("sync", lambda e, dx=dx: e.dma_start(
                out=dx, in_=b.xnf[:, :, :, :]), waits=[e_tp], dma="s_dbg_xn1")

        SUB = self.cfg.get("attn_sub", 9)
        if SUB < 3:
            self.e_attn = [e_tp] * CT
            return
        ps1 = ExitStack()
        pq = [ps1.enter_context(nc.psum_tensor(f"p_q{i}", [128, 512], F32)) for i in range(2)]
        pv = [ps1.enter_context(nc.psum_tensor(f"p_v{i}", [128, 512], F32)) for i in range(2)]

        def rhs_ap(k, j):
            return b.xnf[:, self.LQ * j:self.LQ * (j + 1), k, :]

        eluv = {}
        for j in range(b.NQ):
            gj = min((self.LQ * (j + 1) - 1) // self._G4, len(tpz1) - 1)
            w = [tpz1[gj], self.e_w2]
            if j - 2 in eluv:
                w.append(eluv[j - 2][0])

            def mmq(e, j=j):
                for kc in range(KD):
                    e.matmul(pq[j % 2][0:64, 0:b.QN], wqk_sb[:, kc, 0:64],
                             rhs_ap(kc, j), start=(kc == 0), stop=(kc == KD - 1))
                r = None
                for kc in range(KD):
                    r = e.matmul(pq[j % 2][64:128, 0:b.QN], wqk_sb[:, kc, 64:128],
                                 rhs_ap(kc, j), start=(kc == 0), stop=(kc == KD - 1))
                return r
            em_q = S.op("tensor", mmq, waits=w)
            w2 = [eluv[j - 2][1]] if j - 2 in eluv else []

            def mmv(e, j=j):
                r = None
                for kc in range(KD):
                    r = e.matmul(pv[j % 2][0:64, 0:b.QN], wv_sb[:, kc, 0:AK],
                                 rhs_ap(kc, j), start=(kc == 0), stop=(kc == KD - 1))
                return r
            em_v = S.op("tensor", mmv, waits=w2)
            e1 = S.op("vector", lambda e, j=j: e.tensor_scalar(
                escr[:, 0:b.QN], pq[j % 2][:, 0:b.QN], 0.0, None, ALU.min, ALU.bypass),
                waits=[em_q])
            e2 = S.op("scalar", lambda e, j=j: e.activation(
                escr2[:, 0:b.QN], escr[:, 0:b.QN], AF.Exp), waits=[e1])
            e3 = S.op("vector", lambda e, j=j: e.tensor_scalar(
                escr[:, 0:b.QN], pq[j % 2][:, 0:b.QN], 0.0, None, ALU.max, ALU.bypass),
                waits=[e2, e1])
            e4a = S.op("vector", lambda e, j=j: e.tensor_tensor(
                Q_fm[0:64, j * b.QN:(j + 1) * b.QN], escr[0:64, 0:b.QN],
                escr2[0:64, 0:b.QN], ALU.add), waits=[e3, e2])
            e4 = S.op("vector", lambda e, j=j: e.tensor_tensor(
                K_fm[0:64, j * b.QN:(j + 1) * b.QN], escr[64:128, 0:b.QN],
                escr2[64:128, 0:b.QN], ALU.add), waits=[e4a])
            e5 = S.op("vector", lambda e, j=j: e.tensor_copy(
                V_fm[0:64, j * b.QN:(j + 1) * b.QN], pv[j % 2][0:64, 0:b.QN]),
                waits=[em_v])
            eluv[j] = (e4, e5)
        e_qkv_last = [eluv[b.NQ - 1][0], eluv[b.NQ - 1][1]]

        if SUB < 4:
            self.e_attn = [e_qkv_last[0]] * CT
            ps1.close()
            return
        e_wr1 = S.op("vector", lambda e: e.tensor_reduce(
            wo_rsf[0:64, 0:1], wo_sb[0:64, :], axis=AX.X, op=ALU.add),
            waits=[self.e_w2])
        e_wr2 = S.op("scalar", lambda e: e.activation(
            wo_rsb[0:64, 0:1], wo_rsf[0:64, 0:1], AF.Identity), waits=[e_wr1])
        ps1.close()
        ps_i = ExitStack()
        pA = ps_i.enter_context(nc.psum_tensor("p_A", [128, 512], F32))
        pN = [ps_i.enter_context(nc.psum_tensor(f"p_N{i}", [128, 512], F32))
              for i in range(2)]
        pDn = ps_i.enter_context(nc.psum_tensor("p_Dn", [128, 2 * CT], F32))
        ps_t = ExitStack()
        ptb = ps_t.enter_context(nc.psum_tensor("p_tb", [128, 1024], BF16))
        pSt = [ps_t.enter_context(nc.psum_tensor(f"p_st{i}", [128, 512], F32))
               for i in range(2)]
        e_ones = S.op("gpsimd", lambda e: e.memset(V1_tm[:, :, 64:65], 1.0),
                      waits=list(wsret), dma=False)
        e_zs = S.op("gpsimd", lambda e: e.memset(S_all[:, 0:65], 0.0),
                    waits=list(wsret), dma=False)
        e_zero = S.op("gpsimd", lambda e: e.memset(S_acc[:, :], 0.0),
                      waits=[e_zs], dma=False)
        qv_last = [eluv[b.NQ - 1][0], eluv[b.NQ - 1][1]]
        if b.NQ >= 2:
            qv_last += [eluv[b.NQ - 2][0], eluv[b.NQ - 2][1]]

        trs = {}
        e_sac, e_scw, e_mask, e_emn, e_emd, e_nev = {}, {}, {}, {}, {}, {}

        def phase_a(c):
            ws_ = [trs[c][0], trs[c][1]]
            ws_.append(e_sac[c - 2] if c - 2 in e_sac else e_zero)
            em_s = S.op("tensor", lambda e, c=c: e.matmul(
                pSt[c % 2][0:64, 0:65], K_tm[:, c, :], V1_tm[:, c, :],
                start=True, stop=True), waits=ws_)
            e_sac[c] = S.op("vector", lambda e, c=c: e.tensor_tensor(
                S_acc[:, :], S_acc[:, :], pSt[c % 2][0:64, 0:65], ALU.add),
                waits=[em_s] + ([e_zero] if c == 0 else []))
            e_scw[c] = S.op("vector", lambda e, c=c: e.tensor_copy(
                S_all[:, (c + 1) * 65:(c + 2) * 65], S_acc[:, :]),
                waits=[e_sac[c]])

        def phase_b(c):
            Q = Q_fm[0:64, c * 128:(c + 1) * 128]
            K = K_fm[0:64, c * 128:(c + 1) * 128]
            sA = slice((c % 4) * 128, (c % 4 + 1) * 128)
            w0 = list(qv_last)
            if c - 4 in e_mask:
                w0.append(e_mask[c - 4])
            em_a = S.op("tensor", lambda e, Q=Q, K=K, sA=sA: e.matmul(
                pA[:, sA], K, Q, start=True, stop=True), waits=w0)
            wm = [em_a] + ([e_trl] if c < 2 else [])
            for k_ in (c - 2,):
                if k_ in e_emn:
                    wm += [e_emn[k_], e_emd[k_]]
            e_mask[c] = S.op("vector", lambda e, c=c, sA=sA: e.tensor_tensor(
                A_sb[:, c % 2, :], pA[:, sA], b.tril_sb[:, :], ALU.mult),
                waits=wm)
            eS_ = e_scw[c - 1] if c - 1 in e_scw else e_zs

            def mmn(e, Q=Q, c=c):
                e.matmul(pN[c % 2][0:65, 0:128], S_all[:, c * 65:(c + 1) * 65],
                         Q, start=True, stop=False)
                return e.matmul(pN[c % 2][0:65, 0:128], V1_tm[:, c, :],
                                A_sb[:, c % 2, :], start=False, stop=True)
            wn = [e_mask[c], eS_, trs[c][1]]
            if c - 2 in e_nev:
                wn.append(e_nev[c - 2])
            e_emn[c] = S.op("tensor", mmn, waits=wn)

            def mmd(e, Q=Q, c=c):
                e.matmul(pDn[:, c:c + 1], Q, S_all[:, c * 65 + 64:(c + 1) * 65],
                         start=True, stop=False)
                return e.matmul(pDn[:, c:c + 1], A_sb[:, c % 2, :],
                                b.ones_sb[:, 0:1], start=False, stop=True)
            e_emd[c] = S.op("tensor", mmd, waits=[e_mask[c], eS_])
            e_nev[c] = S.op("vector", lambda e, c=c: e.tensor_copy(
                num_fm[0:64, c * 128:(c + 1) * 128], pN[c % 2][0:64, 0:128]),
                waits=[e_emn[c]])

        for c in range(CT):
            jj = min(b.NQ - 1, (c * 128) // b.QN)
            w = [eluv[jj][0], eluv[jj][1]]
            if c == 0:
                w += qv_last
            if c - 1 in trs:
                w.append(trs[c - 1][1])
            ek = S.op("tensor", lambda e, c=c: e.transpose(
                ptb[:, 0:64], K_fm[0:64, c * 128:(c + 1) * 128],
                b.eye64_sb[0:64, :]), waits=w + [e_trl])
            ekc = S.op("vector", lambda e, c=c: e.tensor_copy(
                K_tm[:, c, :], ptb[:, 0:64]), waits=[ek])
            ev_ = S.op("tensor", lambda e, c=c: e.transpose(
                ptb[:, 0:64], V_fm[0:64, c * 128:(c + 1) * 128],
                b.eye64_sb[0:64, :]), waits=[ekc])
            evc = S.op("vector", lambda e, c=c: e.tensor_copy(
                V1_tm[:, c, 0:64], ptb[:, 0:64]), waits=[ev_, e_ones])
            trs[c] = (ekc, evc)
            if c >= 1:
                if c - 1 < CT - 1:
                    phase_a(c - 1)
                if SUB >= 5:
                    phase_b(c - 1)
        if SUB < 5:
            self.e_attn = [trs[CT - 1][1]] * CT
            ps_t.close()
            ps_i.close()
            return
        phase_b(CT - 1)
        e_dad = S.op("vector", lambda e: e.tensor_scalar(
            b.c_t3[:, 0:CT], pDn[:, 0:CT], DEN_EPS, None, ALU.add, ALU.bypass),
            waits=[e_emd[c_] for c_ in range(CT)])
        e_rho_all = S.op("vector", lambda e: e.reciprocal(
            b.c_rho[:, 0:CT], b.c_t3[:, 0:CT]), waits=[e_dad])

        if "vfm" in b.dbg:
            dv = b.dbg_tensor("vfm", [64, T], BF16)
            S.op("sync", lambda e, dv=dv: e.dma_start(
                out=dv, in_=V_fm[0:64, :]),
                waits=[eluv[j_][1] for j_ in range(b.NQ)], dma="s_dbg_vfm")
        if "sall" in b.dbg:
            dsa = b.dbg_tensor("sall", [64, CT * 65], BF16)
            S.op("sync", lambda e, dsa=dsa: e.dma_start(
                out=dsa, in_=S_all[:, :]),
                waits=[e_scw[c_] for c_ in range(CT - 1)] + [e_zs],
                dma="s_dbg_sall")
        if "num" in b.dbg:
            dnum = b.dbg_tensor("num", [64, T], BF16)
            S.op("sync", lambda e, dnum=dnum: e.dma_start(
                out=dnum, in_=num_fm[0:64, :]),
                waits=[e_nev[c_] for c_ in range(CT)], dma="s_dbg_num")
        if "den" in b.dbg:
            dden = b.dbg_tensor("den", [128, CT], F32)
            S.op("sync", lambda e, dden=dden: e.dma_start(
                out=dden, in_=b.c_rho[:, 0:CT]), waits=[e_rho_all],
                dma="s_dbg_den")
        if SUB < 6:
            self.e_attn = [e_rho_all] * CT
            ps_t.close()
            ps_i.close()
            return
        ILAST = [e_nev[CT - 1], e_rho_all]
        ps_t.close()
        ps2 = ExitStack()
        pZ = [ps2.enter_context(nc.psum_tensor(f"p_z{i}", [128, 512], F32))
              for i in range(3)]
        pZp = [(pZ[0], pZ[1]), (pZ[2], pA), (pN[0], pN[1])]
        self._pT = ps2.enter_context(nc.psum_tensor("p_T", [128, 1024], BF16))
        NJ = b.NJ
        done = {}
        e_cm_last = None
        for c in range(CT):
            w = [e_nev[c], e_rho_all, e_wr2]
            if c == 0:
                w += ILAST
            if c - 3 in done:
                w.append(done[c - 3])

            def mmz(e, c=c):
                for j in range(NJ):
                    e.matmul(pZp[c % 3][j][:, 0:b.NT],
                             num_fm[0:64, c * 128:(c + 1) * 128],
                             wo_sb[0:64, j * b.NT:(j + 1) * b.NT],
                             start=True, stop=True)
                return e.matmul(pDn[:, CT + c:CT + c + 1],
                                num_fm[0:64, c * 128:(c + 1) * 128],
                                wo_rsb[0:64, 0:1], start=True, stop=True)
            em = S.op("tensor", mmz, waits=w)
            if SUB == 6:
                e_ev0 = S.op("vector", lambda e, c=c: [
                    e.tensor_copy(b.scr[c % 2][:, j * b.NT:(j + 1) * b.NT],
                                  pZp[c % 3][j][:, 0:b.NT])
                    for j in range(NJ)][-1], waits=[em])
                done[c] = S.op("vector", lambda e, c=c: e.tensor_tensor(
                    b.xs[:, c, :], b.xs[:, c, :], b.scr[c % 2][:, :], ALU.add),
                    waits=[e_ev0])
                continue

            def sqz(e, c=c):
                r = None
                for j in range(NJ):
                    r = e.activation(b.zsq[:, j * b.NT:(j + 1) * b.NT],
                                     pZp[c % 3][j][:, 0:b.NT], AF.Square,
                                     accum_out=(b.c_s1 if j == 0 else b.c_s2)[:, c:c + 1])
                    e.drain()
                return r
            e_q = S.op("scalar", sqz, waits=[em])
            if SUB in (7, 8):
                done[c] = S.op("vector", lambda e, c=c: e.tensor_tensor(
                    b.xs[:, c, :], b.xs[:, c, :], b.zsq[:, :], ALU.add),
                    waits=[e_q])
                continue

            def colm(e, c=c):
                if NJ > 1:
                    e.tensor_tensor(b.c_s1[:, c:c + 1], b.c_s1[:, c:c + 1],
                                    b.c_s2[:, c:c + 1], ALU.add)
                    e.drain()
                e.tensor_scalar(b.c_t4[:, c:c + 1], pDn[:, CT + c:CT + c + 1],
                                1.0 / D, None, ALU.mult, ALU.bypass)
                e.drain()
                e.tensor_scalar(b.c_t1[:, c:c + 1], b.c_s1[:, c:c + 1], 1.0 / D, None,
                                ALU.mult, ALU.bypass)
                e.drain()
                e.tensor_tensor(b.c_t2[:, c:c + 1], b.c_t4[:, c:c + 1],
                                b.c_t4[:, c:c + 1], ALU.mult)
                e.drain()
                e.tensor_tensor(b.c_t1[:, c:c + 1], b.c_t1[:, c:c + 1],
                                b.c_t2[:, c:c + 1], ALU.subtract)
                e.drain()
                e.tensor_tensor(b.c_t2[:, c:c + 1], b.c_rho[:, c:c + 1],
                                b.c_rho[:, c:c + 1], ALU.mult)
                e.drain()
                e.tensor_tensor(b.c_t1[:, c:c + 1], b.c_t1[:, c:c + 1],
                                b.c_t2[:, c:c + 1], ALU.mult)
                e.drain()
                return e.tensor_scalar(b.c_t1[:, c:c + 1], b.c_t1[:, c:c + 1],
                                       LN_EPS, None, ALU.add, ALU.bypass)
            e_cm = S.op("vector", colm, waits=[e_q, em])
            e_sq = S.op("scalar", lambda e, c=c: e.activation(
                b.c_t2[:, c:c + 1], b.c_t1[:, c:c + 1], AF.Sqrt), waits=[e_cm])

            def colm2(e, c=c):
                e.reciprocal(b.c_t1[:, c:c + 1], b.c_t2[:, c:c + 1])
                e.drain()
                e.tensor_tensor(b.c_sc[:, c:c + 1], b.c_rho[:, c:c + 1],
                                b.c_t1[:, c:c + 1], ALU.mult)
                e.drain()
                return e.scalar_tensor_tensor(b.c_bi[:, c:c + 1], b.c_t4[:, c:c + 1],
                                              -1.0, b.c_sc[:, c:c + 1],
                                              ALU.mult, ALU.mult)
            e_cm2 = S.op("vector", colm2, waits=[e_sq])

            def evac(e, c=c):
                r = None
                for j in range(NJ):
                    r = e.activation(b.scr[c % 2][:, j * b.NT:(j + 1) * b.NT],
                                     pZp[c % 3][j][:, 0:b.NT], AF.Identity,
                                     scale=b.c_sc[:, c:c + 1], bias=b.c_bi[:, c:c + 1])
                    e.drain()
                return r
            e_ev = S.op("scalar", evac, waits=[e_cm2])
            done[c] = S.op("vector", lambda e, c=c: e.tensor_tensor(
                b.xs[:, c, :], b.xs[:, c, :], b.scr[c % 2][:, :], ALU.add),
                waits=[e_ev])
            e_cm_last = e_cm
            if SUB >= 9 and (c + 1) % self._G4 == 0:
                g_ = (c + 1) // self._G4 - 1
                self.emit_zn_group(g_, done)
                if g_ >= 1:
                    self.emit_zn_tp(g_ - 1)
        self.e_attn = [done[c] for c in range(CT)]
        self._att_bank_free = [done[CT - 2], done[CT - 2], e_nev[CT - 1],
                               e_cm_last or e_nev[CT - 1]]
        self._att_done_last = [done[CT - 1], done[CT - 2]]
        if "x1" in b.dbg:
            dx = b.dbg_tensor("x1", [128, CT, D], BF16)
            S.op("sync", lambda e, dx=dx: e.dma_start(out=dx, in_=b.xs[:, :, :]),
                 waits=list(self.e_attn), dma="s_dbg_x1")
        ps2.close()
        ps_i.close()

    def emit_zn_group(self, g, done):
        b, S = self, self.S
        D = b.D
        G4 = self._G4
        if not hasattr(self, "_tpz_acc"):
            self._tpz_acc = {}
            self._zn_prev = {}
            self._zn_full = {}
            self._ztr = {}
        tpz = self._tpz_acc
        g0, g1 = g * G4, (g + 1) * G4
        sl = slice(g0, g1)
        sqr = None
        for c in range(g0, g1):
            sqr = S.op("scalar", lambda e, c=c: e.activation(
                b.jscr[:, :], b.xs[:, c, :], AF.Square,
                accum_out=b.c_s2[:, c:c + 1]), waits=[done[c]])

        def colz(e, sl=sl):
            e.tensor_copy(b.c_mu2[:, sl], b.c_mu1[:, sl])
            e.drain()
            e.tensor_scalar(b.c_t1[:, sl], b.c_s2[:, sl], 1.0 / D, None,
                            ALU.mult, ALU.bypass)
            e.drain()
            e.tensor_tensor(b.c_t2[:, sl], b.c_mu2[:, sl], b.c_mu2[:, sl],
                            ALU.mult)
            e.drain()
            e.tensor_tensor(b.c_t1[:, sl], b.c_t1[:, sl], b.c_t2[:, sl],
                            ALU.subtract)
            e.drain()
            e.tensor_scalar(b.c_t3[:, sl], b.c_t1[:, sl], 1.0 + LN_EPS,
                            LN_EPS * LN_EPS, ALU.mult, ALU.add)
            e.drain()
            return e.tensor_scalar(b.c_t1[:, sl], b.c_t1[:, sl], LN_EPS, None,
                                   ALU.add, ALU.bypass)
        e_cz = S.op("vector", colz, waits=[sqr, self.e_xstat])

        def sq2(e, sl=sl):
            e.activation(b.c_t2[:, sl], b.c_t1[:, sl], AF.Sqrt)
            e.drain()
            return e.activation(b.c_t4[:, sl], b.c_t3[:, sl], AF.Sqrt)
        e_sq = S.op("scalar", sq2, waits=[e_cz])

        def rr2(e, sl=sl):
            e.reciprocal(b.c_r2[:, sl], b.c_t2[:, sl])
            e.drain()
            return e.reciprocal(b.c_rr[:, sl], b.c_t4[:, sl])
        e_rr = S.op("vector", rr2, waits=[e_sq])
        e_zn = None
        for c in range(g0, g1):
            wt_ = [e_rr]
            if c - 2 in self._zn_prev:
                wt_.append(self._zn_prev[c - 2])
            e_t = S.op("vector", lambda e, c=c: e.tensor_scalar(
                b.mo_scr[c % 2][:, :], b.xs[:, c, :], b.c_mu2[:, c:c + 1],
                b.c_rr[:, c:c + 1], ALU.subtract, ALU.mult), waits=wt_)
            e_g1 = S.op("gpsimd", lambda e, c=c: e.tensor_tensor(
                b.xn_tm[:, c, :], b.mo_scr[c % 2][:, :], b.gb[:, :], ALU.mult),
                waits=[e_t, self.e_bc], dma=False)
            e_zn = S.op("gpsimd", lambda e, c=c: e.tensor_tensor(
                b.xn_tm[:, c, :], b.xn_tm[:, c, :], b.bb[:, :], ALU.add),
                waits=[e_g1], dma=False)
            self._zn_prev[c] = e_g1
            self._zn_full[c] = e_zn
        self._zn_last = e_zn

    def emit_zn_tp(self, g):
        b, S = self, self.S
        KD, G4 = b.KD, self._G4
        pT = self._pT
        ztr = self._ztr
        e_last = None
        for c in range(g * G4, (g + 1) * G4):
            for kc in range(KD):
                wt2 = [self._zn_full[c], self.e_i128b]
                if g == 0 and self.e_dbg_xn1 is not None:
                    wt2.append(self.e_dbg_xn1)
                if (c - 1, kc) in ztr:
                    wt2.append(ztr[(c - 1, kc)])
                et = S.op("tensor", lambda e, c=c, kc=kc: e.transpose(
                    pT[:, kc * 128:(kc + 1) * 128],
                    b.xn_tm[:, c, kc * 128:(kc + 1) * 128],
                    b.i128b_sb[:, :]), waits=wt2)
                ev2 = S.op("vector", lambda e, c=c, kc=kc: e.tensor_copy(
                    b.xnf[:, c, kc, :], pT[:, kc * 128:(kc + 1) * 128]),
                    waits=[et])
                ztr[(c, kc)] = ev2
                e_last = ev2
        self._tpz_acc[g] = e_last
        self._tpz_groups = self._tpz_acc
        self.e_tp_zn = e_last

    def ph_ffn_pre(self):
        b, S = self, self.S
        CT = b.CT
        G4 = self._G4
        NG = CT // G4
        done = {c: self.e_attn[c] for c in range(CT)}
        for g in range(NG):
            if not hasattr(self, "_zn_full") or (g * G4) not in self._zn_full:
                self.emit_zn_group(g, done)
        for g in range(NG):
            if g not in getattr(self, "_tpz_acc", {}):
                self.emit_zn_tp(g)
        if "zn" in b.dbg:
            dx = b.dbg_tensor("zn", [128, b.CT, b.KD, 128], BF16)
            S.op("sync", lambda e, dx=dx: e.dma_start(out=dx, in_=b.xnf[:, :, :, :]),
                 waits=[self.e_tp_zn], dma="s_dbg_zn")

    def ph_gating(self, ps):
        b, S = self, self.S
        nc, D, KD, E, CT = b.nc, b.D, b.KD, b.E, b.CT
        pg = [ps.enter_context(nc.psum_tensor(f"p_g{i}", [128, 512], F32))
              for i in range(2)]
        ptw = [ps.enter_context(nc.psum_tensor(f"p_tw{i}", [128, 512], F32))
               for i in range(2)]
        GB_OFF = D
        prev = {}
        for c in range(CT):
            w = [self._tpz_groups[min(c // self._G4, len(self._tpz_groups) - 1)],
                 self.e_w2]
            if c < 2:
                w += self._att_done_last
            if c - 2 in prev:
                w.append(prev[c - 2][0])

            def mmg(e, c=c):
                for kc in range(KD):
                    e.matmul(pg[c % 2][:, 0:E], b.xnf[:, c, kc, :], b.gw_sb[:, kc, :],
                             start=(kc == 0), stop=False)
                return e.matmul(pg[c % 2][:, 0:E], b.ones_sb[0:1, :],
                                b.rows2_bf[0:1, GB_OFF:GB_OFF + E],
                                start=False, stop=True)
            em = S.op("tensor", mmg, waits=w)

            def gate_cols(e, c=c):
                p = pg[c % 2]
                e.reduce_max(b.c_t1[:, c:c + 1], p[:, 0:E], axis=AX.X)
                e.drain()
                e.tensor_scalar(b.zsq[:, 0:E], p[:, 0:E], b.c_t1[:, c:c + 1], None,
                                ALU.is_ge, ALU.bypass)
                e.drain()
                e.scalar_tensor_tensor(b.zsq[:, E:2 * E], b.zsq[:, 0:E], -BIG,
                                       p[:, 0:E], ALU.mult, ALU.add)
                e.drain()
                e.tensor_reduce(b.c_t2[:, c:c + 1], b.zsq[:, E:2 * E], axis=AX.X,
                                op=ALU.max)
                e.drain()
                e.tensor_scalar(b.zsq[:, 2 * E:3 * E], p[:, 0:E], b.c_t2[:, c:c + 1],
                                None, ALU.is_ge, ALU.bypass)
                e.drain()
                return e.tensor_scalar(b.c_t3[:, c:c + 1], b.c_t1[:, c:c + 1], -1.0,
                                       None, ALU.mult, ALU.bypass)
            e_gc = S.op("vector", gate_cols, waits=[em])
            e_ex = S.op("scalar", lambda e, c=c: e.activation(
                b.zsq[:, 3 * E:4 * E], pg[c % 2][:, 0:E], AF.Exp,
                bias=b.c_t3[:, c:c + 1]), waits=[e_gc])

            def wfin(e, c=c):
                e.tensor_tensor(b.zsq[:, 4 * E:5 * E], b.zsq[:, 3 * E:4 * E],
                                b.zsq[:, 2 * E:3 * E], ALU.mult)
                e.drain()
                e.tensor_reduce(b.c_t4[:, c:c + 1], b.zsq[:, 4 * E:5 * E], axis=AX.X,
                                op=ALU.add)
                e.drain()
                e.reciprocal(b.c_t1[:, c:c + 1], b.c_t4[:, c:c + 1])
                e.drain()
                return e.tensor_scalar(b.w_tm[:, c, :], b.zsq[:, 4 * E:5 * E],
                                       b.c_t1[:, c:c + 1], None, ALU.mult, ALU.bypass)
            e_wv = S.op("vector", wfin, waits=[e_ex])
            wt = [e_wv, self.e_i128f]
            if c < 2:
                wt += self._att_done_last
            if c - 2 in prev:
                wt.append(prev[c - 2][1])
            em_t = S.op("tensor", lambda e, c=c: e.transpose(
                ptw[c % 2][0:E, 0:128], b.w_tm[:, c, :], b.i128f_sb[:, :]), waits=wt)
            e_wf = S.op("vector", lambda e, c=c: e.tensor_copy(
                b.w_fm[0:E, c * 128:(c + 1) * 128], ptw[c % 2][0:E, 0:128]),
                waits=[em_t])
            prev[c] = (e_wv, e_wf)
        self.e_gate = [prev[c][0] for c in range(CT)]
        self.e_gate_fm = [prev[c][1] for c in range(CT)]
        self.e_gate_last_s = e_ex
        if "w" in b.dbg:
            dw = b.dbg_tensor("w", [128, CT, E], F32)
            S.op("sync", lambda e, dw=dw: e.dma_start(out=dw, in_=b.w_tm[:, :, :]),
                 waits=list(self.e_gate), dma="s_dbg_w")

    def ph_pass(self, p):
        b, S = self, self.S
        nc, D, KD, KH, E = b.nc, b.D, b.KD, b.KH, b.E
        CF, NT, NJ = b.CF, b.NT, b.NJ
        TN, NJH, LH = b.TN, b.NJH, b.LH
        c0 = p * CF
        arena = ExitStack()
        abel_tm = arena.enter_context(nc.sbuf_tensor(f"abel{p}", [128, CF, D], BF16))
        moe_acc = arena.enter_context(nc.sbuf_tensor(f"moe_acc{p}", [128, CF, D], BF16))
        pp_ = getattr(self, "_pass_prev", None)

        aw_sb = b.aw_sb
        e_awl = self.e_awl
        ps = ExitStack()
        pa = [ps.enter_context(nc.psum_tensor(f"p_a{p}_{i}", [128, 512], F32))
              for i in range(4)]
        pdbg = getattr(self, "_pass_dbg", {})
        AB_OFF = 0
        prev = {}
        e_d1 = {}
        for cr in range(CF):
            ca = c0 + cr
            for j in range(NJ):
                idx = cr * NJ + j
                w = [e_awl,
                     self._tpz_groups[min(ca // self._G4, len(self._tpz_groups) - 1)]]
                if pp_ is not None and idx < 4:
                    w += list(pp_["rot_pairs"][idx])
                    if p - 1 in pdbg:
                        w.append(pdbg[p - 1])
                elif pp_ is None and idx < 4:
                    w.append(self._att_bank_free[idx])
                if idx - 4 in prev:
                    w.append(prev[idx - 4])

                def mma(e, ca=ca, j=j, idx=idx):
                    for kc in range(KD):
                        e.matmul(pa[idx % 4][:, 0:NT], b.xnf[:, ca, kc, :],
                                 aw_sb[:, kc, j * NT:(j + 1) * NT],
                                 start=(kc == 0), stop=False)
                    return e.matmul(pa[idx % 4][:, 0:NT], b.ones_sb[0:1, :],
                                    b.rows2_bf[0:1, AB_OFF + j * NT:AB_OFF + (j + 1) * NT],
                                    start=False, stop=True)
                em = S.op("tensor", mma, waits=w)
                wg = [em]
                if p == 0 and idx < 4:
                    wg.append(self._tpz_groups[min(1, len(self._tpz_groups) - 1)])
                if pp_ is not None:
                    wg.append(pp_["mix2"][cr])
                prev[idx] = S.op("scalar", lambda e, cr=cr, j=j, idx=idx: e.activation(
                    abel_tm[:, cr, j * NT:(j + 1) * NT], pa[idx % 4][:, 0:NT], AF.Gelu),
                    waits=wg)
            e_d1[cr] = S.op("vector", lambda e, cr=cr, ca=ca: e.scalar_tensor_tensor(
                b.jscr[:, :], abel_tm[:, cr, :], 1.0, b.w1b[:, :],
                ALU.bypass, ALU.mult, accum_out=b.c_d1[:, ca:ca + 1]),
                waits=[prev[cr * NJ + NJ - 1]])
        e_abel_last = prev[CF * NJ - 1]
        if p == 0:
            self.ph_gating(ps)
        ps.close()

        ew_es = ExitStack()
        h_fm = ew_es.enter_context(nc.sbuf_tensor(f"h{p}", [128, KH, b.TF], BF16))
        NB1 = 2
        ew1b = ew_es.enter_context(nc.sbuf_tensor(f"ew1b{p}", [128, NB1, KD, 128], BF16))
        RS2 = min(8, NJ * KH)
        ew2b = ew_es.enter_context(nc.sbuf_tensor(f"ew2b{p}", [128, RS2, NT], BF16))
        TBC = min(4, CF)
        NTB = CF // TBC
        ps = ExitStack()
        po = [ps.enter_context(nc.psum_tensor(f"p_o{p}_{i}", [128, 512], F32))
              for i in range(TBC * NJ)]
        e_w1t, e_w1c, e_w2t, e_w2c = {}, {}, {}, {}
        e_hev, e_moe = {}, {}
        self._w2_cons = {}
        self._po_evac = {}

        def sched_w1(g, extra=()):
            ei_, m_ = divmod(g, KH)
            if ei_ >= E:
                return
            w = list(extra) + [self.e_cv1[ei_]]
            if g - NB1 >= 0:
                w.append(e_w1c[divmod(g - NB1, KH)])
            e_w1t[(ei_, m_)] = S.op("sync", lambda e, ei_=ei_, m_=m_: e.dma_start(
                out=ew1b[:, (ei_ * KH + m_) % NB1, :, :], in_=b.d_ew1b[ei_, m_]),
                waits=w, dma=f"s_e1{(ei_ * KH + m_) % NB1}")

        for g0 in range(NB1):
            sched_w1(g0, extra=[e_abel_last])
        for ei in range(E):
            NPO = len(po)
            for m in range(KH):
                for j in range(NJH):
                    idx = m * NJH + j
                    gj = min((c0 + LH * (j + 1) - 1) // self._G4,
                             len(self._tpz_groups) - 1)
                    w = [e_w1t[(ei, m)], self._tpz_groups[gj]]
                    if ei == 0 and idx < NPO:
                        w.append(e_abel_last)
                        if p == 0:
                            w += [self.e_gate_fm[-1], self.e_gate_last_s,
                                  self._zn_last]
                    if ei > 0 and idx < NPO:
                        key = (ei - 1, NJ - 1, idx % NPO)
                        if key in self._po_evac:
                            w.append(self._po_evac[key])
                    if idx >= NPO:
                        pm, pj = divmod(idx - NPO, NJH)
                        w.append(e_hev[(ei, pm, pj)])

                    def hmm(e, ei=ei, m=m, j=j, idx=idx):
                        r = None
                        for kc in range(KD):
                            r = e.matmul(po[idx % NPO][:, 0:TN],
                                         ew1b[:, (ei * KH + m) % NB1, kc, :],
                                         b.xnf[:, c0 + LH * j:c0 + LH * (j + 1), kc, :],
                                         start=(kc == 0), stop=(kc == KD - 1))
                        return r
                    em = S.op("tensor", hmm, waits=w)
                    if j == NJH - 1:
                        e_w1c[(ei, m)] = em
                        sched_w1(ei * KH + m + NB1)
                    e_hev[(ei, m, j)] = S.op(
                        "scalar", lambda e, ei=ei, m=m, j=j, idx=idx: e.activation(
                            h_fm[:, m, j * TN:(j + 1) * TN], po[idx % NPO][:, 0:TN],
                            AF.Gelu, bias=b.eb1c[:, m, ei:ei + 1]), waits=[em])
            h_last = e_hev[(ei, KH - 1, NJH - 1)]
            last_mm = None
            seqpos = 0
            w2_mm = {}
            for j2 in range(NJ):
                for k2 in range(KH):
                    gseq = ei * NJ * KH + seqpos
                    slot = gseq % RS2
                    w = [self.e_cv2[ei]]
                    if seqpos == 0:
                        w.append(e_w2c[ei - 1] if ei - 1 in e_w2c else e_abel_last)
                    if gseq - RS2 >= 0:
                        w.append(self._w2_cons[gseq - RS2])
                    e_w2t[gseq] = S.op("sync", lambda e, ei=ei, k2=k2, j2=j2,
                                       slot=slot:
                        e.dma_start(out=ew2b[:, slot, :],
                                    in_=b.d_ew2b[ei, k2 * 128:(k2 + 1) * 128,
                                                 j2 * NT:(j2 + 1) * NT]),
                        waits=w, dma=f"s_e2{slot}")

                    def omm(e, ei=ei, k2=k2, slot=slot):
                        r = None
                        for cr in range(CF):
                            r = e.matmul(po[cr][:, 0:NT],
                                         h_fm[:, k2, cr * 128:(cr + 1) * 128],
                                         ew2b[:, slot, :],
                                         start=(k2 == 0), stop=(k2 == KH - 1))
                        return r
                    wm = [e_w2t[gseq], h_last]
                    if k2 == 0:
                        for cr in range(CF):
                            key = (ei, j2 - 1, cr) if j2 > 0 else (ei - 1, NJ - 1, cr)
                            if key in self._po_evac:
                                wm.append(self._po_evac[key])
                        if pp_ is not None and j2 == 0 and ei == 0:
                            wm.append(pp_["tpm"])
                            for pr2 in pp_["rot_pairs"]:
                                wm += list(pr2)
                    em = S.op("tensor", omm, waits=wm)
                    self._w2_cons[gseq] = em
                    w2_mm[(j2, k2)] = em
                    last_mm = em
                    seqpos += 1
                for cr in range(CF):
                    ca = c0 + cr

                    def oev(e, ei=ei, cr=cr, ca=ca, j2=j2):
                        sl = (slice(None), cr, slice(j2 * NT, (j2 + 1) * NT))
                        if ei == 0:
                            return e.tensor_scalar(
                                moe_acc[sl], po[cr][:, 0:NT],
                                b.w_tm[:, ca, ei:ei + 1], None,
                                ALU.mult, ALU.bypass)
                        return e.scalar_tensor_tensor(
                            moe_acc[sl], po[cr][:, 0:NT],
                            b.w_tm[:, ca, ei:ei + 1],
                            moe_acc[sl], ALU.mult, ALU.add)
                    wv = [w2_mm[(j2, KH - 1)], self.e_gate[ca]]
                    if ei == 0 and p == 0:
                        wv.append(self.e_tp_zn)
                    if ei > 0:
                        wv.append(e_moe[(ei - 1, cr, j2)])
                    e_moe[(ei, cr, j2)] = S.op("vector", oev, waits=wv)
                    self._po_evac[(ei, j2, cr)] = e_moe[(ei, cr, j2)]
            e_w2c[ei] = last_mm
        for cr in range(CF):
            ca = c0 + cr
            for j2 in range(NJ):
                tb = cr // TBC
                pi = (cr % TBC) * NJ + j2
                wb = [self.e_gate_fm[ca], self.e_w2,
                      self._po_evac[(E - 1, NJ - 1, pi)]]
                if cr - TBC >= 0:
                    wb.append(e_moe[(E, cr - TBC, j2)])
                em = S.op("tensor", lambda e, ca=ca, j2=j2, pi=pi: e.matmul(
                    po[pi][:, 0:NT], b.w_fm[0:E, ca * 128:(ca + 1) * 128],
                    b.eb2_sb[0:E, j2 * NT:(j2 + 1) * NT], start=True, stop=True),
                    waits=wb)
                e_moe[(E, cr, j2)] = S.op(
                    "vector", lambda e, cr=cr, j2=j2, pi=pi: e.scalar_tensor_tensor(
                        moe_acc[:, cr, j2 * NT:(j2 + 1) * NT], po[pi][:, 0:NT],
                        1.0, moe_acc[:, cr, j2 * NT:(j2 + 1) * NT],
                        ALU.bypass, ALU.add),
                    waits=[em, e_moe[(E - 1, cr, j2)]])
        e_moe_done = [e_moe[(E, cr, j2)] for cr in range(CF) for j2 in range(NJ)]
        ps.close()
        ew_es.close()
        if f"moe{p}" in b.dbg:
            dm = b.dbg_tensor(f"moe{p}", [128, CF, D], BF16)
            self._pass_dbg = getattr(self, "_pass_dbg", {})
            self._pass_dbg[p] = S.op(
                "sync", lambda e, dm=dm: e.dma_start(out=dm, in_=moe_acc[:, :, :]),
                waits=e_moe_done, dma=f"s_dbg_moe{p}")

        moef = arena.enter_context(nc.sbuf_tensor(f"moef{p}", [128, CF, KD, 128], BF16))
        CH = max(1, CF // 2)
        tpm_c = {}
        e_h = self._last_tp
        for h in range(CF // CH):
            wtp_ = [e_moe[(E, cr, j2)] for j2 in range(NJ)
                    for cr in range(h * CH, (h + 1) * CH)]
            wtp_.append(e_h)
            e_h = S.op("sync", lambda e, h=h: e.dma_start(
                out=moef[:, h * CH:(h + 1) * CH, :, :],
                in_=moe_acc[:, h * CH:(h + 1) * CH, :]
                    .rearrange("p c d -> p (c d)"), transpose=True),
                waits=[w_ for w_ in wtp_ if w_ is not None], dma=f"s_tm{p}")
            for cr in range(h * CH, (h + 1) * CH):
                tpm_c[cr] = e_h
        e_tpm = tpm_c[CF - 1]
        self._last_tp = e_tpm

        Rf = arena.enter_context(nc.sbuf_tensor(f"Rf{p}", [128, KD, D], BF16))
        e_Rp = S.op("sync", lambda e: e.dma_start(
            out=Rf[:, :, :], in_=b.d_spill[5]),
            waits=[self.e_R, e_w2c[E - 1], e_w1c[(E - 1, KH - 1)]], dma="s_sp")
        ps = ExitStack()
        pr = [ps.enter_context(nc.psum_tensor(f"p_r{p}_{i}", [128, 512], F32))
              for i in range(4)]
        pd = ps.enter_context(nc.psum_tensor(f"p_d{p}", [128, CF], F32))
        e_store = []
        prevrot = {}
        prev_tail = {}
        pend_o = None

        def prev_slot_evt(lst, mod, slot):
            for c_ in range(CF - 1, -1, -1):
                if ((p - 1) * CF + c_) % mod == slot:
                    return lst[c_]
            return None
        for cr in range(CF):
            ca = c0 + cr
            rot_ev = []
            for j in range(NJ):
                idx = cr * NJ + j
                w = [tpm_c[cr], e_Rp]
                if idx - 4 in prevrot:
                    w += list(prevrot[idx - 4])

                def mmr(e, cr=cr, j=j, idx=idx):
                    r = None
                    for kc in range(KD):
                        r = e.matmul(pr[idx % 4][:, 0:NT], moef[:, cr, kc, :],
                                     Rf[:, kc, j * NT:(j + 1) * NT],
                                     start=(kc == 0), stop=(kc == KD - 1))
                    return r
                em = S.op("tensor", mmr, waits=w)
                wcp = [em]
                if cr - 6 in prev_tail:
                    wcp.append(prev_tail[cr - 6]["f"])
                elif pp_ is not None:
                    ev_ = prev_slot_evt(pp_["f"], 6, (c0 + cr) % 6)
                    if ev_ is not None:
                        wcp.append(ev_)
                e_cp = S.op("scalar", lambda e, cr=cr, j=j, idx=idx: e.copy(
                    b.mo_scr[(c0 + cr) % 6][:, j * NT:(j + 1) * NT], pr[idx % 4][:, 0:NT]),
                    waits=wcp)
                prevrot[idx] = (e_cp,)
                rot_ev.append(e_cp)

            def dmm(e, cr=cr):
                r = None
                for kc in range(KD):
                    r = e.matmul(pd[:, cr:cr + 1], moef[:, cr, kc, :],
                                 b.rgc[:, kc:kc + 1],
                                 start=(kc == 0), stop=(kc == KD - 1))
                return r
            em_d = S.op("tensor", dmm, waits=[tpm_c[cr], self.e_rgc])
            if pend_o is not None:
                pend_o()
                pend_o = None

            e_gc = S.op("vector", lambda e, ca=ca, cr=cr: e.tensor_tensor(
                b.c_t1[:, ca:ca + 1], pd[:, cr:cr + 1], b.c_d1[:, ca:ca + 1],
                ALU.add), waits=[em_d, e_d1[cr]])
            e_sg = S.op("scalar", lambda e, ca=ca: e.activation(
                b.c_g[:, ca:ca + 1], b.c_t1[:, ca:ca + 1], AF.Sigmoid,
                bias=b.gcb_col[:, 0:1]), waits=[e_gc])
            wmix = list(rot_ev)
            e_mix1 = S.op("gpsimd", lambda e, cr=cr: e.tensor_tensor(
                b.mo_scr[(c0 + cr) % 6][:, :], b.mo_scr[(c0 + cr) % 6][:, :], abel_tm[:, cr, :],
                ALU.subtract), waits=wmix, dma=False)
            e_mix2 = S.op("vector", lambda e, cr=cr, ca=ca: e.scalar_tensor_tensor(
                b.mo_scr[(c0 + cr) % 6][:, :], b.mo_scr[(c0 + cr) % 6][:, :], b.c_g[:, ca:ca + 1],
                abel_tm[:, cr, :], ALU.mult, ALU.add), waits=[e_mix1, e_sg])
            wfs = [e_mix2]
            if cr - 2 in prev_tail:
                wfs.append(prev_tail[cr - 2]["st"])
            elif pp_ is not None:
                ev_ = prev_slot_evt(pp_["st"], 2, (c0 + cr) % 2)
                if ev_ is not None:
                    wfs.append(ev_)
            e_fs = S.op("vector", lambda e, ca=ca, cr=cr: e.tensor_scalar(
                b.scr[(c0 + cr) % 2][:, :], b.xs[:, ca, :], b.c_mu2[:, ca:ca + 1],
                b.c_r2[:, ca:ca + 1], ALU.subtract, ALU.mult), waits=wfs)
            e_f = S.op("vector", lambda e, cr=cr, ca=ca: e.scalar_tensor_tensor(
                b.scr[(c0 + cr) % 2][:, :], b.scr[(c0 + cr) % 2][:, :], 1.0,
                b.mo_scr[(c0 + cr) % 6][:, :], ALU.bypass, ALU.add,
                accum_out=b.c_s1[:, ca:ca + 1]), waits=[e_fs, e_mix2])
            e_fq = S.op("scalar", lambda e, ca=ca, cr=cr: e.activation(
                b.zsq[:, :], b.scr[(c0 + cr) % 2][:, :], AF.Square,
                accum_out=b.c_s2[:, ca:ca + 1]), waits=[e_f])

            def colf(e, ca=ca):
                e.tensor_scalar(b.c_mf[:, ca:ca + 1], b.c_s1[:, ca:ca + 1], 1.0 / D,
                                None, ALU.mult, ALU.bypass)
                e.drain()
                e.tensor_scalar(b.c_t1[:, ca:ca + 1], b.c_s2[:, ca:ca + 1], 1.0 / D,
                                None, ALU.mult, ALU.bypass)
                e.drain()
                e.tensor_tensor(b.c_t2[:, ca:ca + 1], b.c_mf[:, ca:ca + 1],
                                b.c_mf[:, ca:ca + 1], ALU.mult)
                e.drain()
                e.tensor_tensor(b.c_t1[:, ca:ca + 1], b.c_t1[:, ca:ca + 1],
                                b.c_t2[:, ca:ca + 1], ALU.subtract)
                e.drain()
                return e.tensor_scalar(b.c_t1[:, ca:ca + 1],
                                       b.c_t1[:, ca:ca + 1], LN_EPS, None,
                                       ALU.add, ALU.bypass)
            e_cf = S.op("vector", colf, waits=[e_f, e_fq])
            e_sf = S.op("scalar", lambda e, ca=ca: e.activation(
                b.c_t2[:, ca:ca + 1], b.c_t1[:, ca:ca + 1], AF.Sqrt),
                waits=[e_cf])

            def colf2(e, ca=ca):
                e.reciprocal(b.c_rf[:, ca:ca + 1], b.c_t2[:, ca:ca + 1])
                e.drain()
                return e.scalar_tensor_tensor(b.c_t3[:, ca:ca + 1],
                                              b.c_mf[:, ca:ca + 1], -1.0,
                                              b.c_rf[:, ca:ca + 1], ALU.mult, ALU.mult)
            e_cf2 = S.op("vector", colf2, waits=[e_sf])
            e_tf = S.op("scalar", lambda e, ca=ca, cr=cr: e.activation(
                b.zsq[:, :], b.scr[(c0 + cr) % 2][:, :], AF.Identity,
                scale=b.c_rf[:, ca:ca + 1], bias=b.c_t3[:, ca:ca + 1]),
                waits=[e_cf2, e_fq])
            def emit_o(ca=ca, cr=cr, e_tf=e_tf):
                wo_ = [e_tf]
                if len(self._stores) >= 2:
                    wo_.append(self._stores[-2])
                e_o = S.op("vector", lambda e, ca=ca, cr=cr: e.tensor_tensor(
                    b.scr[(c0 + cr) % 2][:, :], b.xs[:, ca, :], b.zsq[:, :],
                    ALU.add), waits=wo_)
                st_ = S.op("sync", lambda e, ca=ca, cr=cr: e.dma_start(
                    out=b.d_y[ca], in_=b.scr[(c0 + cr) % 2][:, :]), waits=[e_o],
                    dma=f"s_out{(p * CF + cr) % 2}")
                e_store.append(st_)
                self._stores.append(st_)
                prev_tail[cr]["st"] = st_
            pend_o = emit_o
            prev_tail[cr] = dict(mix1=e_mix1, mix2=e_mix2, f=e_f, tf=e_tf)
        pend_o()
        self.e_pass_done[p] = e_store[-1]
        rp_ = {}
        for j_ in sorted(prevrot):
            rp_[j_ % 4] = prevrot[j_]
        self._pass_prev = dict(
            rot_pairs=[rp_.get(i, ()) for i in range(4)],
            tpm=e_tpm,
            mix2=[prev_tail[c]["mix2"] for c in range(CF)],
            f=[prev_tail[c]["f"] for c in range(CF)],
            st=[prev_tail[c]["st"] for c in range(CF)],
        )
        ps.close()
        arena.close()

    def ph_tail(self):
        waits = self._stores[-2:]
        if not waits or self.cfg.get("phases", 99) < 99:
            waits = list(self.S.last_evt.values())
        self.S.op("sync", lambda e: e.nop(), waits=waits, inc=False, dma=False)



_BUILD_CACHE = {}


def _get_nc(key, cfg):
    if key not in _BUILD_CACHE:
        _BUILD_CACHE[key] = Builder(cfg).build()
    return _BUILD_CACHE[key]


def host_inputs(cfg, inp):
    D, E = cfg["D"], cfg["E"]
    KD, KH = D // 128, (2 * D) // 128
    wq = np.asarray(inp["Wq"], dtype=np.float32)
    wk = np.asarray(inp["Wk"], dtype=np.float32)
    ew1 = np.asarray(inp["ew1"], dtype=np.float32)
    gcw = np.asarray(inp["gc_w"], dtype=np.float32)[:, 0]
    rows = np.concatenate([
        np.asarray(inp["ln_gamma"], dtype=np.float32),
        np.asarray(inp["ln_beta"], dtype=np.float32),
        gcw[:D], gcw[D:],
        np.asarray(inp["abel_b"], dtype=np.float32),
        np.asarray(inp["moe_gate_b"], dtype=np.float32),
        np.asarray(inp["gc_b"], dtype=np.float32),
    ])[None, :]
    return dict(
        wqk=np.ascontiguousarray(np.concatenate([wq.T, wk.T], axis=1)),
        wv=np.ascontiguousarray(np.asarray(inp["Wv"], dtype=np.float32).T),
        wo=np.ascontiguousarray(np.asarray(inp["Wo"], dtype=np.float32).T),
        aw=np.ascontiguousarray(np.asarray(inp["abel_w"], dtype=np.float32)),
        gw=np.ascontiguousarray(np.asarray(inp["moe_gate_w"], dtype=np.float32)),
        ew1=np.ascontiguousarray(
            ew1.reshape(E, KD, 128, KH, 128).transpose(0, 3, 2, 1, 4)),
        ew2=np.ascontiguousarray(np.asarray(inp["ew2"], dtype=np.float32)),
        eb1=np.ascontiguousarray(np.asarray(inp["eb1"], dtype=np.float32)),
        eb2=np.ascontiguousarray(np.asarray(inp["eb2"], dtype=np.float32)),
        gens=np.ascontiguousarray(np.asarray(inp["gens"], dtype=np.float32)),
        rows=np.ascontiguousarray(rows),
        tril=np.triu(np.ones((128, 128), dtype=BF)),
        eye64=np.concatenate([np.eye(64, dtype=BF)] * 2, axis=0),
        i128=np.eye(128, dtype=BF),
        i128f=np.eye(128, dtype=np.float32),
        eye8f=np.eye(8, dtype=np.float32),
        ones=np.ones((128, 128), dtype=BF),
    )


def kernel(**inputs):
    cfg = default_cfg()
    D, T = cfg["D"], cfg["T"]
    CT = T // 128
    NC = cfg["NCORES"]
    nc = _get_nc("main", cfg)
    shared = host_inputs(cfg, dict(
        Wq=inputs["L0_attn_Wq"], Wk=inputs["L0_attn_Wk"], Wv=inputs["L0_attn_Wv"],
        Wo=inputs["L0_attn_Wo"], ln_gamma=inputs["L0_ffn_ln_gamma"],
        ln_beta=inputs["L0_ffn_ln_beta"], abel_w=inputs["L0_ffn_abel_w"],
        abel_b=inputs["L0_ffn_abel_b"], moe_gate_w=inputs["L0_ffn_moe_gate_w"],
        moe_gate_b=inputs["L0_ffn_moe_gate_b"], ew1=inputs["L0_ffn_ew1"],
        eb1=inputs["L0_ffn_eb1"], ew2=inputs["L0_ffn_ew2"], eb2=inputs["L0_ffn_eb2"],
        gens=inputs["L0_ffn_gens"], gc_w=inputs["L0_ffn_gc_w"],
        gc_b=inputs["L0_ffn_gc_b"]))
    x = np.asarray(inputs["L0_attn_x"], dtype=np.float32)
    in_maps = []
    for c in range(NC):
        m = dict(shared)
        m["x"] = np.ascontiguousarray(x[c].reshape(CT, 128, D))
        in_maps.append(m)
    res = run_bass_kernel_spmd(nc, in_maps, core_ids=list(range(NC)))
    return np.stack([np.asarray(res.results[c]["y"]).reshape(T, D)
                     for c in range(NC)])


# version 31
# speedup vs baseline: 1.0202x; 1.0167x over previous
import math
from contextlib import ExitStack

import numpy as np
import ml_dtypes

import concourse.bass as bass
import concourse.mybir as mybir
from concourse.bass_utils import run_bass_kernel_spmd

F32 = mybir.dt.float32
BF16 = mybir.dt.bfloat16
AF = mybir.ActivationFunctionType
ALU = mybir.AluOpType
AX = mybir.AxisListType
BF = ml_dtypes.bfloat16

LN_EPS = 1e-5
DEN_EPS = 1e-6
BIG = 30000.0

ENGS = ("tensor", "vector", "scalar", "gpsimd", "sync")

_CNAMES = ["c0", "c1", "c2", "c3", "c4", "c5", "c6",
           "s0", "s1", "s2", "s3", "s4", "s5", "s6"]
COEF_IDX = {n: i for i, n in enumerate(_CNAMES)}


def coef_values():
    bc = [1.0 / math.factorial(2 * j) for j in range(9)]
    bs = [1.0 / math.factorial(2 * j + 1) for j in range(9)]
    v = {"c0": bc[0], "c1": bc[1], "c2": bc[2], "c3": bc[3], "c4": bc[4],
         "c5": bc[5], "c6": bc[6],
         "s0": bs[0], "s1": bs[1], "s2": bs[2], "s3": bs[3], "s4": bs[4],
         "s5": bs[5], "s6": bs[6]}
    return [v[n] for n in _CNAMES]


def default_cfg():
    return dict(T=2048, D=1024, AK=64, E=8, NROT=3, NP=2, NCORES=8, dbg=())


class Evt:
    __slots__ = ("eng", "val", "gid")

    def __init__(self, eng, val, gid):
        self.eng = eng
        self.val = val
        self.gid = gid


class Sched:

    COMPUTE_KEYS = ("tensor", "vector", "scalar", "gpsimd_c", "sync_c")

    def __init__(self):
        self.ops = {e: [] for e in ENGS}
        self.cnt = {}
        self.gid = 0

    def op(self, eng, fn, waits=(), inc=True, dma=None):
        self.gid += 1
        g = self.gid
        for w in waits:
            assert isinstance(w, Evt) and w.gid < g, "wait on future op"
        if isinstance(dma, str):
            key, stp = dma, 16
        elif eng == "gpsimd":
            assert dma is False, "gpsimd op needs dma channel or dma=False"
            key, stp = "gpsimd_c", 1
        elif eng == "sync":
            assert dma is False or dma is None
            key, stp = "sync_c", 1
        else:
            assert dma is None
            key, stp = eng, 1
        evt = None
        if inc:
            self.cnt[key] = self.cnt.get(key, 0) + stp
            evt = Evt(key, self.cnt[key], g)
            self.last_evt = getattr(self, "last_evt", {})
            self.last_evt[key] = evt
        elif key not in self.cnt:
            self.cnt[key] = 0
        self.ops[eng].append((fn, tuple(waits), evt, key, stp))
        return evt

    def emit(self, blk, sems):
        for eng in ENGS:
            oplist = self.ops[eng]
            if not oplist:
                continue

            def run(e, oplist=oplist):
                waited = {}
                first = True
                for fn, waits, evt, key, stp in oplist:
                    need = {}
                    for w in waits:
                        if w.eng == key and key in self.COMPUTE_KEYS:
                            continue
                        need[w.eng] = max(need.get(w.eng, 0), w.val)
                    for s_, v_ in need.items():
                        if waited.get(s_, 0) < v_:
                            e.wait_ge(sems[s_], v_)
                            waited[s_] = v_
                    if key in ("vector", "scalar", "gpsimd_c") and not first:
                        e.drain()
                    first = False
                    last = fn(e)
                    if evt is not None:
                        assert last is not None, f"op on {eng} returned None"
                        last.then_inc(sems[key], stp)
            getattr(blk, eng)(run)


class Builder:
    def __init__(self, cfg):
        self.cfg = cfg
        self.nc = bass.Bass("TRN2", target_bir_lowering=False)
        self.es = ExitStack()
        c = cfg
        self.D, self.T, self.AK, self.E = c["D"], c["T"], c["AK"], c["E"]
        self.NROT, self.NP = c["NROT"], c["NP"]
        self.KD = self.D // 128
        self.CT = self.T // 128
        self.H = 2 * self.D
        self.KH = self.H // 128
        self.TF = self.T // self.NP
        self.CF = self.TF // 128
        self.NT = min(512, self.D)
        self.NJ = self.D // self.NT
        self.QN = min(512, self.T)
        self.NQ = self.T // self.QN
        self.LQ = self.QN // 128
        self.TN = min(512, self.TF)
        self.NJH = self.TF // self.TN
        self.LH = self.TN // 128
        self.dbg = set(c.get("dbg", ()))
        self.dbg_outs = []

    def declare_io(self):
        nc = self.nc
        D, AK, E, H, KD, KH, CT = self.D, self.AK, self.E, self.H, self.KD, self.KH, self.CT
        dt = nc.dram_tensor
        self.d = {}

        def di(name, shape, dtype=F32):
            self.d[name] = dt(name, shape, dtype, kind="ExternalInput").ap()
        di("x", [CT, 128, D])
        di("wqk", [D, 128]); di("wv", [D, AK]); di("wo", [AK, D])
        di("aw", [D, D]); di("gw", [D, E])
        di("ew1", [E, KH, 128, KD, 128])
        di("ew2", [E, H, D])
        di("eb1", [E, H]); di("eb2", [E, D])
        di("gens", [self.NROT, D, D])
        self.ROWL = 5 * D + E + 1
        di("rows", [1, self.ROWL])
        di("tril", [128, 128], BF16)
        di("eye64", [128, 64], BF16)
        di("i128", [128, 128], BF16)
        di("i128f", [128, 128])
        di("eye8f", [8, 8])
        di("ones", [128, 128], BF16)
        self.d_y = dt("y", [CT, 128, D], F32, kind="ExternalOutput").ap()
        self.d_spill = dt("spill", [6, 128, KD, D], BF16, kind="Internal").ap()
        self.d_ew1b = dt("ew1b_h", [E, KH, 128, KD, 128], BF16, kind="Internal").ap()
        self.d_ew2b = dt("ew2b_h", [E, H, D], BF16, kind="Internal").ap()
        self.d_awb = dt("awb_h", [D, D], BF16, kind="Internal").ap()
        self.d_wqkb = dt("wqkb_h", [D, 128], BF16, kind="Internal").ap()
        self.d_wvb = dt("wvb_h", [D, self.AK], BF16, kind="Internal").ap()
        self.d_wob = dt("wob_h", [self.AK, D], BF16, kind="Internal").ap()
        self.d_rows4b = dt("rows4b_h", [1, 4 * D], BF16, kind="Internal").ap()

    def dbg_tensor(self, name, shape, dtype):
        ap = self.nc.dram_tensor("dbg_" + name, shape, dtype, kind="ExternalOutput").ap()
        self.dbg_outs.append("dbg_" + name)
        return ap

    def alloc_persistent(self):
        nc, es = self.nc, self.es
        D, AK, E = self.D, self.AK, self.E
        KD, KH, CT = self.KD, self.KH, self.CT
        sb = lambda name, shape, dtype: es.enter_context(nc.sbuf_tensor(name, shape, dtype))
        self.gw_sb = sb("gw_sb", [128, KD, E], BF16)
        self.eb2_sb = sb("eb2_sb", [E, D], BF16)
        self.eb1c = sb("eb1c", [128, KH, E], F32)
        self.eye8f_sb = sb("eye8f_sb", [8, 8], F32)
        self.ones_sb = sb("ones_sb", [128, 128], BF16)
        self.rows2_bf = sb("rows2_bf", [1, D + E + 1], BF16)
        self.gcb_col = sb("gcb_col", [128, 1], F32)
        self.eps5 = sb("eps5", [128, 1], F32)
        self.eps6 = sb("eps6", [128, 1], F32)
        col = lambda name: sb(name, [128, CT], F32)
        for n in ("c_s1", "c_s2", "c_mu1", "c_r1", "c_mu2", "c_r2", "c_muu",
                  "c_ru", "c_rho", "c_t1", "c_t2", "c_t3", "c_t4",
                  "c_sc", "c_bi", "c_d1", "c_d2a", "c_d2b", "c_g",
                  "c_mf", "c_rf", "c_var2", "c_rr"):
            setattr(self, n, col(n))
        self.xs = sb("xs", [128, CT, self.D], BF16)
        self.rgc = sb("rgc", [128, KD], BF16)
        self.rgc_f = sb("rgc_f", [128, KD], F32)
        self.grow = sb("grow", [1, D], BF16)

    def build(self):
        self.declare_io()
        self.alloc_persistent()
        self.S = Sched()
        PH = self.cfg.get("phases", 99)
        with self.nc.Block() as blk:
            self.ph_loads()
            if PH >= 1:
                self.ph_prep()
            if PH >= 2:
                self.ph_expm()
            else:
                self._gstack.close()
            self.alloc_main()
            if PH >= 3:
                self.ph_attention()
            if PH >= 4:
                self.ph_ffn_pre()
            if PH >= 5:
                self._att.close()
            self.e_pass_done = {}
            self._stores = []
            if PH >= 6:
                for p in range(min(self.NP, PH - 5)):
                    self.ph_pass(p)
            self.ph_tail()
            if PH < 5 and hasattr(self, "_att"):
                self._att.close()
            if PH in (5, 6) and hasattr(self, "_gate_ps"):
                pass
            if PH < 2 and hasattr(self, "_prep_ps"):
                self._prep_ps.close()
            sems = {k: self.es.enter_context(self.nc.semaphore("sem_" + k))
                    for k in self.S.cnt}
            self.S.emit(blk, sems)
        self.es.close()
        return self.nc

    def alloc_main(self):
        nc, es = self.nc, self.es
        D, CT, KD = self.D, self.CT, self.KD
        sb = lambda name, shape, dtype: es.enter_context(nc.sbuf_tensor(name, shape, dtype))
        self.i128f_sb = sb("i128f_sb", [128, 128], F32)
        self.i128b_sb = sb("i128b_sb", [128, 128], BF16)
        self.w_tm = sb("w_tm", [128, CT, self.E], F32)
        self.w_fm = sb("w_fm", [self.E, self.T], BF16)
        self.xnf = sb("xnf", [128, CT, KD, 128], BF16)
        self.aw_sb = sb("aw_sb", [128, KD, D], BF16)
        self.w1b = sb("w1b", [128, D], BF16)
        self.scr = [sb(f"scr{i}", [128, D], F32) for i in range(2)]
        self.zsq = sb("zsq", [128, D], F32)
        self.jscr = sb("jscr", [128, D], BF16)
        self.mo_scr = [sb(f"mo{i}", [128, D], BF16) for i in range(6)]

    def ph_loads(self):
        b, S = self, self.S
        self._last_tp = None
        D = b.D
        self._gstack = ExitStack()
        self.mGf = self._gstack.enter_context(
            b.nc.sbuf_tensor("mGf", [128, b.KD, D], F32))
        self.e_gl0 = S.op("sync", lambda e: e.dma_start(
            out=b.mGf[:, :, :],
            in_=b.d["gens"][0].rearrange("(k p) d -> p k d", p=128)),
            dma="s_gens")
        last = None
        for dst, src in [
            (b.eye8f_sb[:, :], b.d["eye8f"]),
            (b.ones_sb[:, :], b.d["ones"]),
        ]:
            last = S.op("sync", lambda e, d=dst, s_=src: e.dma_start(out=d, in_=s_),
                        dma="s_cst")
        self.e_cst = last
        self.e_x = S.op("gpsimd", lambda e: e.dma_start(
            out=b.xs[:, :, :], in_=b.d["x"].rearrange("c t d -> t c d")),
            waits=[self.e_gl0], dma="g_x")
        e1_ = S.op("gpsimd", lambda e: e.memset(b.eps5[:, :], LN_EPS), dma=False)
        self.e_eps = S.op("gpsimd", lambda e: e.memset(b.eps6[:, :], DEN_EPS),
                          waits=[e1_], dma=False)
        last = None
        for dst, src in [
            (b.gw_sb[:, :, :], b.d["gw"].rearrange("(k p) m -> p k m", p=128)),
            (b.eb2_sb[:, :], b.d["eb2"]),
            (b.rows2_bf[:, :], b.d["rows"][0:1, 4 * D:]),
            (b.grow[:, :], b.d["rows"][0:1, 3 * D:4 * D]),
        ]:
            last = S.op("gpsimd", lambda e, d=dst, s_=src: e.dma_start(out=d, in_=s_),
                        dma="g_w")
        self.e_w = last
        self.e_cvA = S.op("gpsimd", lambda e: e.dma_start(
            out=b.d_awb, in_=b.d["aw"]), dma="g_cvA")
        for dst_, src_ in [(b.d_wqkb, b.d["wqk"]), (b.d_wvb, b.d["wv"]),
                           (b.d_wob, b.d["wo"]),
                           (b.d_rows4b, b.d["rows"][0:1, 0:4 * D])]:
            self.e_cvW = S.op("gpsimd", lambda e, d=dst_, s_=src_: e.dma_start(
                out=d, in_=s_), dma="g_cvW")
        self.e_cv1, self.e_cv2 = {}, {}
        for ei in range(self.E):
            self.e_cv1[ei] = S.op("gpsimd", lambda e, ei=ei: e.dma_start(
                out=b.d_ew1b[ei], in_=b.d["ew1"][ei]), dma=f"g_cv1_{ei}")
            self.e_cv2[ei] = S.op("gpsimd", lambda e, ei=ei: e.dma_start(
                out=b.d_ew2b[ei], in_=b.d["ew2"][ei]), dma=f"g_cv2_{ei}")

    def ph_prep(self):
        b, S = self, self.S
        nc, D, E, KH = b.nc, b.D, b.E, b.KH
        ps = ExitStack()
        b._eb1r = ps.enter_context(nc.sbuf_tensor("eb1r", [E, self.H], F32))
        pb = [ps.enter_context(nc.psum_tensor(f"p_pr{i}", [128, 512], F32)) for i in range(2)]
        e_eb1 = S.op("sync", lambda e: e.dma_start(out=b._eb1r[:, :], in_=b.d["eb1"]),
                     dma="s_eb1")
        em = S.op("tensor", lambda e: e.matmul(
            pb[0][:, 0:1], b.ones_sb[0:1, :],
            b.rows2_bf[0:1, D + E:D + E + 1], start=True, stop=True),
            waits=[self.e_cst, self.e_w])
        e_gcb = S.op("vector", lambda e: e.tensor_copy(b.gcb_col[:, :], pb[0][:, 0:1]),
                     waits=[em])
        prev = {}
        for m in range(KH):
            pt = pb[m % 2]
            w = [e_eb1, e_gcb] + ([prev[m - 2]] if m - 2 in prev else [])
            em = S.op("tensor", lambda e, pt=pt, m=m: e.transpose(
                pt[:, 0:E], b._eb1r[0:E, m * 128:(m + 1) * 128], b.eye8f_sb[:, :]),
                waits=w)
            prev[m] = S.op("vector", lambda e, pt=pt, m=m: e.tensor_copy(
                b.eb1c[:, m, :], pt[:, 0:E]), waits=[em])
        self.e_prep = prev[KH - 1]
        self._prep_ps = ps

    def ph_expm(self):
        b, S = self, self.S
        nc, D, KD, NJ, NT = b.nc, b.D, b.KD, b.NJ, b.NT
        self._prep_ps.close()
        ws = ExitStack()
        sbm = lambda n: ws.enter_context(nc.sbuf_tensor(n, [128, KD, D], BF16))
        mG = sbm("mG")
        mB, mB2, mB3 = sbm("mB"), sbm("mB2"), sbm("mB3")
        mCmb, mU, mC, mS_ = sbm("mCmb"), sbm("mU"), sbm("mC"), sbm("mS")
        mGf = b.mGf
        b.ceye_sb = ws.enter_context(
            nc.sbuf_tensor("ceye_sb", [128, len(_CNAMES), 128], BF16))
        b.i128_sb = ws.enter_context(nc.sbuf_tensor("i128_sb", [128, 128], BF16))
        e_i1 = S.op("sync", lambda e: e.dma_start(
            out=b.i128_sb[:, :], in_=b.d["i128"]), waits=[], dma="s_ce")
        e_ce = None
        for ci, cv in enumerate(coef_values()):
            e_ce = S.op("scalar", lambda e, ci=ci, cv=cv: e.activation(
                b.ceye_sb[:, ci, :], b.i128_sb[:, :], AF.Identity, scale=cv),
                waits=[e_i1])
        ps = ExitStack()
        pp = [ps.enter_context(nc.psum_tensor(f"p_e{i}", [128, 512], F32)) for i in range(4)]
        pbx = ps.enter_context(nc.psum_tensor("p_ebx", [128, 512], F32))
        wsG = ws.enter_context(nc.sbuf_tensor("wsG", [128, D], BF16))

        bc = [1.0 / math.factorial(2 * j) for j in range(9)]
        bs = [1.0 / math.factorial(2 * j + 1) for j in range(9)]
        CE = COEF_IDX

        def mat_fm(lhsT, rhs, out, waits, scale=1.0, addm=None, diag=None,
                   addm_w=(), coefs=()):
            evacs = {}
            last = None
            for mc in range(KD):
                for j in range(NJ):
                    ti = mc * NJ + j
                    pt = pp[ti % 4]
                    w = list(waits) + ([evacs[ti - 4]] if ti - 4 in evacs else [])
                    dj = None
                    if diag is not None:
                        o = mc * 128 - j * NT
                        if 0 <= o and o + 128 <= NT:
                            dj = o

                    def mm(e, pt=pt, mc=mc, j=j, dj=dj, lhsT=lhsT, rhs=rhs):
                        r = None
                        n_extra = len(coefs) + (1 if dj is not None else 0)
                        for kc in range(KD):
                            stop = (kc == KD - 1) and n_extra == 0
                            r = e.matmul(pt[:, 0:NT],
                                         lhsT[:, kc, mc * 128:(mc + 1) * 128],
                                         rhs[:, kc, j * NT:(j + 1) * NT],
                                         start=(kc == 0), stop=stop)
                        k = 0
                        for ci, mat in coefs:
                            k += 1
                            stop = (k == n_extra)
                            r = e.matmul(pt[:, 0:NT], b.ceye_sb[:, ci, :],
                                         mat[:, mc, j * NT:(j + 1) * NT],
                                         start=False, stop=stop)
                        if dj is not None:
                            r = e.matmul(pt[:, dj:dj + 128], b.i128_sb[:, :],
                                         b.ceye_sb[:, diag, :],
                                         start=False, stop=True)
                        return r
                    em = S.op("tensor", mm, waits=w)

                    if addm is None:
                        def evs(e, pt=pt, mc=mc, j=j, out=out, scale=scale):
                            sl = (slice(None), mc, slice(j * NT, (j + 1) * NT))
                            return e.activation(out[sl], pt[:, 0:NT], AF.Identity,
                                                scale=scale)
                        evacs[ti] = last = S.op("scalar", evs, waits=[em])
                    else:
                        def ev(e, pt=pt, mc=mc, j=j, out=out, addm=addm, scale=scale):
                            sl = (slice(None), mc, slice(j * NT, (j + 1) * NT))
                            if scale == 1.0:
                                return e.tensor_tensor(out[sl], pt[:, 0:NT],
                                                       addm[sl], ALU.add)
                            return e.scalar_tensor_tensor(out[sl], pt[:, 0:NT], scale,
                                                          addm[sl], ALU.mult, ALU.add)
                        evacs[ti] = last = S.op("vector", ev,
                                                waits=[em] + list(addm_w))
            return last

        def combo(dst, srcs, waits):
            def vf(e):
                r = None
                for n_, (cf, m_) in enumerate(srcs):
                    if n_ > 0:
                        e.drain()
                    if n_ == 0:
                        r = e.tensor_scalar(dst[:, :, :], m_[:, :, :], cf, None,
                                            ALU.mult, ALU.bypass)
                    else:
                        r = e.scalar_tensor_tensor(dst[:, :, :], m_[:, :, :], cf,
                                                   dst[:, :, :], ALU.mult, ALU.add)
                return r
            return S.op("vector", vf, waits=waits)

        prev_last = [self.e_prep]
        sp_gate = []
        e_gl = self.e_gl0
        e_cast_last = None
        for i in range(b.NROT):
            e_g = S.op("vector", lambda e: e.tensor_scalar(
                mG[:, :, :], mGf[:, :, :], 1.0, None, ALU.mult, ALU.bypass),
                waits=[e_gl] + list(prev_last))
            e_cast_last = e_g
            if i + 1 < b.NROT:
                e_gl = S.op("sync", lambda e, i=i: e.dma_start(
                    out=mGf[:, :, :],
                    in_=b.d["gens"][i + 1].rearrange("(k p) d -> p k d", p=128)),
                    waits=[e_g], dma="s_gens")
            eB = mat_fm(mG, mG, mB, [e_g, e_ce], scale=-1.0)
            eB2 = mat_fm(mB, mB, mB2, [eB])
            eB3 = mat_fm(mB2, mB, mB3, [eB2])
            def acc3(dst, cfs, first_w):
                t1 = S.op("vector", lambda e: e.tensor_scalar(
                    dst[:, :, :], mB[:, :, :], cfs[0], None, ALU.mult,
                    ALU.bypass), waits=[eB] + first_w)
                t2 = S.op("vector", lambda e: e.scalar_tensor_tensor(
                    dst[:, :, :], mB2[:, :, :], cfs[1], dst[:, :, :],
                    ALU.mult, ALU.add), waits=[eB2, t1])
                return S.op("vector", lambda e: e.scalar_tensor_tensor(
                    dst[:, :, :], mB3[:, :, :], cfs[2], dst[:, :, :],
                    ALU.mult, ALU.add), waits=[eB3, t2])
            eD2 = acc3(mCmb, (bc[4], bc[5], bc[6]), sp_gate)
            eD2s = acc3(mU, (bs[4], bs[5], bs[6]), sp_gate)
            eC = mat_fm(mB3, mCmb, mC, [eD2],
                        coefs=[(CE["c1"], mB), (CE["c2"], mB2),
                               (CE["c3"], mB3)], diag=CE["c0"])
            eS = mat_fm(mB3, mU, mS_, [eD2s, eC],
                        coefs=[(CE["s1"], mB), (CE["s2"], mB2),
                               (CE["s3"], mB3)], diag=CE["s0"])
            if i == 0:
                eEv = mat_fm(mS_, mG, mU, [eS], scale=-1.0, addm=mC)
            else:
                eEv = mat_fm(mG, mS_, mU, [eS], scale=-1.0, addm=mC)
            prev_last = [eEv]
            if i < 2:
                e_sp1 = S.op("sync", lambda e, i=i: e.dma_start(
                    out=b.d_spill[i], in_=mU[:, :, :]), waits=[eEv], dma="s_sp")
                sp_gate = [e_sp1]
            else:
                sp_gate = []
        red = None
        for c in range(b.CT):
            red = S.op("vector", lambda e, c=c: e.reduce_sum(
                b.c_s1[:, c:c + 1], b.xs[:, c, :], axis=AX.X),
                waits=[self.e_x, e_cast_last])
        e_xred = red
        e_l = None
        for idx, dst in [(0, mB2), (1, mB3)]:
            e_l = S.op("sync", lambda e, idx=idx, dst=dst: e.dma_start(
                out=dst[:, :, :], in_=b.d_spill[idx]), waits=list(prev_last),
                dma="s_sp")
        eM = mat_fm(mB2, mB3, mC, [e_l])
        eMT = mat_fm(mB3, mB2, mS_, [eM])
        eRv = mat_fm(mS_, mU, mCmb, [eMT])
        self.e_R = S.op("sync", lambda e: e.dma_start(
            out=b.d_spill[5], in_=mCmb[:, :, :]), waits=[eRv], dma="s_sp")
        e_bx = None
        for j in range(b.NJ):
            embx = S.op("tensor", lambda e, j=j: e.matmul(
                pbx[:, 0:NT], b.ones_sb[0:1, :], b.grow[0:1, j * NT:(j + 1) * NT],
                start=True, stop=True),
                waits=[self.e_cst, self.e_w] + ([e_bx] if e_bx else []))
            e_bx = S.op("vector", lambda e, j=j: e.tensor_copy(
                wsG[:, j * NT:(j + 1) * NT], pbx[:, 0:NT]), waits=[embx])
        e_rg = None
        for kc in range(KD):
            e_rg = S.op("vector", lambda e, kc=kc: e.scalar_tensor_tensor(
                mGf[:, 0, :], mCmb[:, kc, :], 1.0, wsG[:, :],
                ALU.bypass, ALU.mult, accum_out=b.rgc_f[:, kc:kc + 1]),
                waits=[eRv, e_bx])
        self.e_rgc = S.op("scalar", lambda e: e.activation(
            b.rgc[:, :], b.rgc_f[:, :], AF.Identity), waits=[e_rg])
        red2 = {}
        sqr = None
        for c in range(b.CT):
            sl = 2 + (c % 2) if KD >= 4 else 1
            w = [self.e_x, e_cast_last]
            back = 2 if KD >= 4 else 1
            if c - back in red2:
                w.append(red2[c - back])
            sqr = S.op("scalar", lambda e, c=c, sl=sl: e.activation(
                mGf[:, sl, :], b.xs[:, c, :], AF.Square), waits=w)
            red2[c] = S.op("vector", lambda e, c=c, sl=sl: e.reduce_sum(
                b.c_s2[:, c:c + 1], mGf[:, sl, :], axis=AX.X), waits=[sqr])
        self.e_xstat = self.emit_mu_r(b.c_s1, b.c_s2, b.c_mu1, b.c_r1,
                                      [e_xred, red2[b.CT - 1]])
        if "R" in b.dbg:
            dR = b.dbg_tensor("R", [128, KD, D], BF16)
            S.op("sync", lambda e, dR=dR: e.dma_start(out=dR, in_=b.d_spill[5]),
                 waits=[self.e_R], dma="s_dbg_R")
        ps.close()
        ws.close()
        self._gstack.close()

    def emit_mu_r(self, s1, s2, mu, r, waits, var_out=None, sl=slice(None),
                  mu_src=None):
        b, S, D = self, self.S, self.D

        def colm(e):
            if mu_src is None:
                e.tensor_scalar(mu[:, sl], s1[:, sl], 1.0 / D, None, ALU.mult,
                                ALU.bypass)
            else:
                e.tensor_copy(mu[:, sl], mu_src[:, sl])
            e.drain()
            e.tensor_scalar(b.c_t1[:, sl], s2[:, sl], 1.0 / D, None, ALU.mult,
                            ALU.bypass)
            e.drain()
            e.tensor_tensor(b.c_t2[:, sl], mu[:, sl], mu[:, sl], ALU.mult)
            e.drain()
            e.tensor_tensor(b.c_t1[:, sl], b.c_t1[:, sl], b.c_t2[:, sl],
                            ALU.subtract)
            if var_out is not None:
                e.drain()
                e.tensor_copy(var_out[:, sl], b.c_t1[:, sl])
            e.drain()
            return e.tensor_scalar(b.c_t1[:, sl], b.c_t1[:, sl], LN_EPS, None,
                                   ALU.add, ALU.bypass)
        e1 = S.op("vector", colm, waits=waits)
        e2 = S.op("scalar", lambda e: e.activation(
            b.c_t2[:, sl], b.c_t1[:, sl], AF.Sqrt), waits=[e1])
        return S.op("vector", lambda e: e.reciprocal(r[:, sl], b.c_t2[:, sl]),
                    waits=[e2])

    def ph_attention(self):
        b, S = self, self.S
        nc, D, KD, AK, T, CT = b.nc, b.D, b.KD, b.AK, b.T, b.CT
        att = ExitStack()
        self._att = att
        wsret = [getattr(self, n) for n in ("e_R", "e_xstat", "e_rgc")
                 if hasattr(self, n)] or [self.e_prep]
        eR0 = wsret[0]
        sb = lambda n, shape, dtype: att.enter_context(nc.sbuf_tensor(n, shape, dtype))
        b.xn_tm = sb("xn_tm", [128, CT, D], BF16)
        b.gb = sb("gb", [128, D], BF16)
        b.bb = sb("bb", [128, D], BF16)
        rows4 = sb("rows4", [1, 4 * D], BF16)
        wqk_sb = sb("wqk_sb", [128, KD, 128], BF16)
        wv_sb = sb("wv_sb", [128, KD, AK], BF16)
        wo_sb = sb("wo_sb", [AK, D], BF16)
        b.tril_sb = sb("tril_sb", [128, 128], BF16)
        b.eye64_sb = sb("eye64_sb", [128, 64], BF16)
        e_trl = S.op("sync", lambda e: e.dma_start(
            out=b.tril_sb[:, :], in_=b.d["tril"]), waits=list(wsret), dma="s_att")
        self.e_i128f = S.op("sync", lambda e: e.dma_start(
            out=b.i128f_sb[:, :], in_=b.d["i128f"]), waits=list(wsret),
            dma="s_att")
        self.e_i128b = S.op("sync", lambda e: e.dma_start(
            out=b.i128b_sb[:, :], in_=b.d["i128"]), waits=list(wsret),
            dma="s_att")
        e_trl = S.op("sync", lambda e: e.dma_start(
            out=b.eye64_sb[:, :], in_=b.d["eye64"]), waits=[], dma="s_att")
        Q_fm = sb("Q_fm", [64, T], BF16)
        K_fm = sb("K_fm", [64, T], BF16)
        V_fm = sb("V_fm", [64, T], BF16)
        K_tm = sb("K_tm", [128, CT, 64], BF16)
        V1_tm = sb("V1_tm", [128, CT, 65], BF16)
        num_fm = sb("num_fm", [64, T], BF16)
        S_all = sb("S_all", [64, CT * 65], BF16)
        S_acc = sb("S_acc", [64, 65], F32)
        A_sb = sb("A_sb", [128, 2, 128], BF16)
        escr = sb("escr", [128, 512], F32)
        escr2 = sb("escr2", [128, 512], F32)
        wo_rsf = sb("wo_rsf", [64, 1], F32)
        wo_rsb = sb("wo_rsb", [64, 1], BF16)

        eR = wsret[0]
        self.e_awl = S.op("sync", lambda e: e.dma_start(
            out=b.aw_sb[:, :, :],
            in_=b.d_awb.rearrange("(k p) d -> p k d", p=128)),
            waits=wsret + [self.e_cvA], dma="s_aw")
        last = None
        for dst, srcap in [
            (wqk_sb[:, :, :], b.d_wqkb.rearrange("(k p) m -> p k m", p=128)),
            (wv_sb[:, :, :], b.d_wvb.rearrange("(k p) m -> p k m", p=128)),
            (wo_sb[:, :], b.d_wob),
            (rows4[:, :], b.d_rows4b),
        ]:
            last = S.op("sync", lambda e, d=dst, s_=srcap: e.dma_start(
                out=d, in_=s_), waits=list(wsret) + [self.e_cvW], dma="s_att")
        self.e_w2 = last
        psb = ExitStack()
        pbx = [psb.enter_context(nc.psum_tensor(f"p_bx{i}", [128, 512], F32))
               for i in range(2)]
        wbt = min(512, D)
        nbt = max(1, D // wbt)
        bca = {}
        dsts = [b.gb, b.bb, b.w1b]
        for r in range(3):
            for j in range(nbt):
                idx = r * nbt + j
                w = [self.e_w2] + ([bca[idx - 2]] if idx - 2 in bca else [])
                off = r * D + j * wbt
                em0 = S.op("tensor", lambda e, off=off, idx=idx: e.matmul(
                    pbx[idx % 2][:, 0:wbt], b.ones_sb[0:1, :],
                    rows4[0:1, off:off + wbt], start=True, stop=True), waits=w)
                bca[idx] = S.op("vector", lambda e, r=r, j=j, idx=idx: e.tensor_copy(
                    dsts[r][:, j * wbt:(j + 1) * wbt], pbx[idx % 2][:, 0:wbt]),
                    waits=[em0])
        self.e_bc = bca[3 * nbt - 1]
        psb.close()

        e_mr = self.e_xstat
        if "mu1" in b.dbg:
            dmu = b.dbg_tensor("mu1", [128, 2 * CT], F32)
            S.op("sync", lambda e, dmu=dmu: e.dma_start(
                out=dmu[:, 0:CT], in_=b.c_mu1[:, 0:CT]), waits=[e_mr],
                dma="s_dbg_mu1")
            S.op("sync", lambda e, dmu=dmu: e.dma_start(
                out=dmu[:, CT:2 * CT], in_=b.c_r1[:, 0:CT]), waits=[e_mr],
                dma="s_dbg_mu1")
        e_xn_parts = {}
        G4p = 4 if CT % 4 == 0 else CT
        for c in range(CT):
            e_xn = S.op("vector", lambda e, c=c: e.tensor_scalar(
                b.xn_tm[:, c, :], b.xs[:, c, :], b.c_mu1[:, c:c + 1],
                b.c_r1[:, c:c + 1], ALU.subtract, ALU.mult),
                waits=[e_mr] + wsret)
            if (c + 1) % G4p == 0:
                e_xn_parts[c // G4p] = e_xn
        G4 = 4 if CT % 4 == 0 else CT
        NG = CT // G4
        tpz1 = {}
        for g in range(NG):
            w = [e_xn_parts[g]]
            if self._last_tp is not None:
                w.append(self._last_tp)
            e_tpg = S.op("sync", lambda e, g=g: e.dma_start(
                out=b.xnf[:, g * G4:(g + 1) * G4, :, :],
                in_=b.xn_tm[:, g * G4:(g + 1) * G4, :]
                    .rearrange("p c d -> p (c d)"), transpose=True),
                waits=w, dma=f"s_t1_{g}")
            tpz1[g] = e_tpg
            self._last_tp = e_tpg
        e_tp = tpz1[NG - 1]
        self.e_tp_xn1 = e_tp
        self._tp1_groups = tpz1
        self._G4 = G4
        self.e_dbg_xn1 = None
        if "xn1" in b.dbg:
            dx = b.dbg_tensor("xn1", [128, CT, KD, 128], BF16)
            self.e_dbg_xn1 = S.op("sync", lambda e, dx=dx: e.dma_start(
                out=dx, in_=b.xnf[:, :, :, :]), waits=[e_tp], dma="s_dbg_xn1")

        SUB = self.cfg.get("attn_sub", 9)
        if SUB < 3:
            self.e_attn = [e_tp] * CT
            return
        ps1 = ExitStack()
        pq = [ps1.enter_context(nc.psum_tensor(f"p_q{i}", [128, 512], F32)) for i in range(2)]
        pv = [ps1.enter_context(nc.psum_tensor(f"p_v{i}", [128, 512], F32)) for i in range(2)]

        def rhs_ap(k, j):
            return b.xnf[:, self.LQ * j:self.LQ * (j + 1), k, :]

        eluv = {}
        for j in range(b.NQ):
            gj = min((self.LQ * (j + 1) - 1) // self._G4, len(tpz1) - 1)
            w = [tpz1[gj], self.e_w2]
            if j - 2 in eluv:
                w.append(eluv[j - 2][0])

            def mmq(e, j=j):
                for kc in range(KD):
                    e.matmul(pq[j % 2][0:64, 0:b.QN], wqk_sb[:, kc, 0:64],
                             rhs_ap(kc, j), start=(kc == 0), stop=(kc == KD - 1))
                r = None
                for kc in range(KD):
                    r = e.matmul(pq[j % 2][64:128, 0:b.QN], wqk_sb[:, kc, 64:128],
                                 rhs_ap(kc, j), start=(kc == 0), stop=(kc == KD - 1))
                return r
            em_q = S.op("tensor", mmq, waits=w)
            w2 = [eluv[j - 2][1]] if j - 2 in eluv else []

            def mmv(e, j=j):
                r = None
                for kc in range(KD):
                    r = e.matmul(pv[j % 2][0:64, 0:b.QN], wv_sb[:, kc, 0:AK],
                                 rhs_ap(kc, j), start=(kc == 0), stop=(kc == KD - 1))
                return r
            em_v = S.op("tensor", mmv, waits=w2)
            e1 = S.op("vector", lambda e, j=j: e.tensor_scalar(
                escr[:, 0:b.QN], pq[j % 2][:, 0:b.QN], 0.0, None, ALU.min, ALU.bypass),
                waits=[em_q])
            e2 = S.op("scalar", lambda e, j=j: e.activation(
                escr2[:, 0:b.QN], escr[:, 0:b.QN], AF.Exp), waits=[e1])
            e3 = S.op("vector", lambda e, j=j: e.tensor_scalar(
                escr[:, 0:b.QN], pq[j % 2][:, 0:b.QN], 0.0, None, ALU.max, ALU.bypass),
                waits=[e2, e1])
            e4a = S.op("vector", lambda e, j=j: e.tensor_tensor(
                Q_fm[0:64, j * b.QN:(j + 1) * b.QN], escr[0:64, 0:b.QN],
                escr2[0:64, 0:b.QN], ALU.add), waits=[e3, e2])
            e4 = S.op("vector", lambda e, j=j: e.tensor_tensor(
                K_fm[0:64, j * b.QN:(j + 1) * b.QN], escr[64:128, 0:b.QN],
                escr2[64:128, 0:b.QN], ALU.add), waits=[e4a])
            e5 = S.op("vector", lambda e, j=j: e.tensor_copy(
                V_fm[0:64, j * b.QN:(j + 1) * b.QN], pv[j % 2][0:64, 0:b.QN]),
                waits=[em_v])
            eluv[j] = (e4, e5)
        e_qkv_last = [eluv[b.NQ - 1][0], eluv[b.NQ - 1][1]]

        if SUB < 4:
            self.e_attn = [e_qkv_last[0]] * CT
            ps1.close()
            return
        e_wr1 = S.op("vector", lambda e: e.tensor_reduce(
            wo_rsf[0:64, 0:1], wo_sb[0:64, :], axis=AX.X, op=ALU.add),
            waits=[self.e_w2])
        e_wr2 = S.op("scalar", lambda e: e.activation(
            wo_rsb[0:64, 0:1], wo_rsf[0:64, 0:1], AF.Identity), waits=[e_wr1])
        ps1.close()
        ps_i = ExitStack()
        pA = ps_i.enter_context(nc.psum_tensor("p_A", [128, 512], F32))
        pN = [ps_i.enter_context(nc.psum_tensor(f"p_N{i}", [128, 512], F32))
              for i in range(2)]
        pDn = ps_i.enter_context(nc.psum_tensor("p_Dn", [128, 2 * CT], F32))
        ps_t = ExitStack()
        ptb = ps_t.enter_context(nc.psum_tensor("p_tb", [128, 1024], BF16))
        pSt = [ps_t.enter_context(nc.psum_tensor(f"p_st{i}", [128, 512], F32))
               for i in range(2)]
        e_ones = S.op("gpsimd", lambda e: e.memset(V1_tm[:, :, 64:65], 1.0),
                      waits=list(wsret), dma=False)
        e_zs = S.op("gpsimd", lambda e: e.memset(S_all[:, 0:65], 0.0),
                    waits=list(wsret), dma=False)
        e_zero = S.op("gpsimd", lambda e: e.memset(S_acc[:, :], 0.0),
                      waits=[e_zs], dma=False)
        qv_last = [eluv[b.NQ - 1][0], eluv[b.NQ - 1][1]]
        if b.NQ >= 2:
            qv_last += [eluv[b.NQ - 2][0], eluv[b.NQ - 2][1]]

        trs = {}
        e_sac, e_scw, e_mask, e_emn, e_emd, e_nev = {}, {}, {}, {}, {}, {}

        def phase_a(c):
            ws_ = [trs[c][0], trs[c][1]]
            ws_.append(e_sac[c - 2] if c - 2 in e_sac else e_zero)
            em_s = S.op("tensor", lambda e, c=c: e.matmul(
                pSt[c % 2][0:64, 0:65], K_tm[:, c, :], V1_tm[:, c, :],
                start=True, stop=True), waits=ws_)
            e_sac[c] = S.op("vector", lambda e, c=c: e.tensor_tensor(
                S_acc[:, :], S_acc[:, :], pSt[c % 2][0:64, 0:65], ALU.add),
                waits=[em_s] + ([e_zero] if c == 0 else []))
            e_scw[c] = S.op("vector", lambda e, c=c: e.tensor_copy(
                S_all[:, (c + 1) * 65:(c + 2) * 65], S_acc[:, :]),
                waits=[e_sac[c]])

        def phase_b(c):
            Q = Q_fm[0:64, c * 128:(c + 1) * 128]
            K = K_fm[0:64, c * 128:(c + 1) * 128]
            sA = slice((c % 4) * 128, (c % 4 + 1) * 128)
            w0 = list(qv_last)
            if c - 4 in e_mask:
                w0.append(e_mask[c - 4])
            em_a = S.op("tensor", lambda e, Q=Q, K=K, sA=sA: e.matmul(
                pA[:, sA], K, Q, start=True, stop=True), waits=w0)
            wm = [em_a] + ([e_trl] if c < 2 else [])
            for k_ in (c - 2,):
                if k_ in e_emn:
                    wm += [e_emn[k_], e_emd[k_]]
            e_mask[c] = S.op("vector", lambda e, c=c, sA=sA: e.tensor_tensor(
                A_sb[:, c % 2, :], pA[:, sA], b.tril_sb[:, :], ALU.mult),
                waits=wm)
            eS_ = e_scw[c - 1] if c - 1 in e_scw else e_zs

            def mmn(e, Q=Q, c=c):
                e.matmul(pN[c % 2][0:65, 0:128], S_all[:, c * 65:(c + 1) * 65],
                         Q, start=True, stop=False)
                return e.matmul(pN[c % 2][0:65, 0:128], V1_tm[:, c, :],
                                A_sb[:, c % 2, :], start=False, stop=True)
            wn = [e_mask[c], eS_, trs[c][1]]
            if c - 2 in e_nev:
                wn.append(e_nev[c - 2])
            e_emn[c] = S.op("tensor", mmn, waits=wn)

            def mmd(e, Q=Q, c=c):
                e.matmul(pDn[:, c:c + 1], Q, S_all[:, c * 65 + 64:(c + 1) * 65],
                         start=True, stop=False)
                return e.matmul(pDn[:, c:c + 1], A_sb[:, c % 2, :],
                                b.ones_sb[:, 0:1], start=False, stop=True)
            e_emd[c] = S.op("tensor", mmd, waits=[e_mask[c], eS_])
            e_nev[c] = S.op("vector", lambda e, c=c: e.tensor_copy(
                num_fm[0:64, c * 128:(c + 1) * 128], pN[c % 2][0:64, 0:128]),
                waits=[e_emn[c]])

        for c in range(CT):
            jj = min(b.NQ - 1, (c * 128) // b.QN)
            w = [eluv[jj][0], eluv[jj][1]]
            if c == 0:
                w += qv_last
            if c - 1 in trs:
                w.append(trs[c - 1][1])
            ek = S.op("tensor", lambda e, c=c: e.transpose(
                ptb[:, 0:64], K_fm[0:64, c * 128:(c + 1) * 128],
                b.eye64_sb[0:64, :]), waits=w + [e_trl])
            ekc = S.op("vector", lambda e, c=c: e.tensor_copy(
                K_tm[:, c, :], ptb[:, 0:64]), waits=[ek])
            ev_ = S.op("tensor", lambda e, c=c: e.transpose(
                ptb[:, 0:64], V_fm[0:64, c * 128:(c + 1) * 128],
                b.eye64_sb[0:64, :]), waits=[ekc])
            evc = S.op("vector", lambda e, c=c: e.tensor_copy(
                V1_tm[:, c, 0:64], ptb[:, 0:64]), waits=[ev_, e_ones])
            trs[c] = (ekc, evc)
            if c >= 1:
                if c - 1 < CT - 1:
                    phase_a(c - 1)
                if SUB >= 5:
                    phase_b(c - 1)
        if SUB < 5:
            self.e_attn = [trs[CT - 1][1]] * CT
            ps_t.close()
            ps_i.close()
            return
        phase_b(CT - 1)
        e_dad = S.op("vector", lambda e: e.tensor_scalar(
            b.c_t3[:, 0:CT], pDn[:, 0:CT], DEN_EPS, None, ALU.add, ALU.bypass),
            waits=[e_emd[c_] for c_ in range(CT)])
        e_rho_all = S.op("vector", lambda e: e.reciprocal(
            b.c_rho[:, 0:CT], b.c_t3[:, 0:CT]), waits=[e_dad])

        if "vfm" in b.dbg:
            dv = b.dbg_tensor("vfm", [64, T], BF16)
            S.op("sync", lambda e, dv=dv: e.dma_start(
                out=dv, in_=V_fm[0:64, :]),
                waits=[eluv[j_][1] for j_ in range(b.NQ)], dma="s_dbg_vfm")
        if "sall" in b.dbg:
            dsa = b.dbg_tensor("sall", [64, CT * 65], BF16)
            S.op("sync", lambda e, dsa=dsa: e.dma_start(
                out=dsa, in_=S_all[:, :]),
                waits=[e_scw[c_] for c_ in range(CT - 1)] + [e_zs],
                dma="s_dbg_sall")
        if "num" in b.dbg:
            dnum = b.dbg_tensor("num", [64, T], BF16)
            S.op("sync", lambda e, dnum=dnum: e.dma_start(
                out=dnum, in_=num_fm[0:64, :]),
                waits=[e_nev[c_] for c_ in range(CT)], dma="s_dbg_num")
        if "den" in b.dbg:
            dden = b.dbg_tensor("den", [128, CT], F32)
            S.op("sync", lambda e, dden=dden: e.dma_start(
                out=dden, in_=b.c_rho[:, 0:CT]), waits=[e_rho_all],
                dma="s_dbg_den")
        if SUB < 6:
            self.e_attn = [e_rho_all] * CT
            ps_t.close()
            ps_i.close()
            return
        ILAST = [e_nev[CT - 1], e_rho_all]
        ps_t.close()
        ps2 = ExitStack()
        pZ = [ps2.enter_context(nc.psum_tensor(f"p_z{i}", [128, 512], F32))
              for i in range(3)]
        pZp = [(pZ[0], pZ[1]), (pZ[2], pA), (pN[0], pN[1])]
        NJ = b.NJ
        done = {}
        e_cm_last = None
        for c in range(CT):
            w = [e_nev[c], e_rho_all, e_wr2]
            if c == 0:
                w += ILAST
            if c - 3 in done:
                w.append(done[c - 3])

            def mmz(e, c=c):
                for j in range(NJ):
                    e.matmul(pZp[c % 3][j][:, 0:b.NT],
                             num_fm[0:64, c * 128:(c + 1) * 128],
                             wo_sb[0:64, j * b.NT:(j + 1) * b.NT],
                             start=True, stop=True)
                return e.matmul(pDn[:, CT + c:CT + c + 1],
                                num_fm[0:64, c * 128:(c + 1) * 128],
                                wo_rsb[0:64, 0:1], start=True, stop=True)
            em = S.op("tensor", mmz, waits=w)
            if SUB == 6:
                e_ev0 = S.op("vector", lambda e, c=c: [
                    e.tensor_copy(b.scr[c % 2][:, j * b.NT:(j + 1) * b.NT],
                                  pZp[c % 3][j][:, 0:b.NT])
                    for j in range(NJ)][-1], waits=[em])
                done[c] = S.op("vector", lambda e, c=c: e.tensor_tensor(
                    b.xs[:, c, :], b.xs[:, c, :], b.scr[c % 2][:, :], ALU.add),
                    waits=[e_ev0])
                continue

            def sqz(e, c=c):
                r = None
                for j in range(NJ):
                    r = e.activation(b.zsq[:, j * b.NT:(j + 1) * b.NT],
                                     pZp[c % 3][j][:, 0:b.NT], AF.Square,
                                     accum_out=(b.c_s1 if j == 0 else b.c_s2)[:, c:c + 1])
                    e.drain()
                return r
            e_q = S.op("scalar", sqz, waits=[em])
            if SUB in (7, 8):
                done[c] = S.op("vector", lambda e, c=c: e.tensor_tensor(
                    b.xs[:, c, :], b.xs[:, c, :], b.zsq[:, :], ALU.add),
                    waits=[e_q])
                continue

            def colm(e, c=c):
                if NJ > 1:
                    e.tensor_tensor(b.c_s1[:, c:c + 1], b.c_s1[:, c:c + 1],
                                    b.c_s2[:, c:c + 1], ALU.add)
                    e.drain()
                e.tensor_scalar(b.c_t4[:, c:c + 1], pDn[:, CT + c:CT + c + 1],
                                1.0 / D, None, ALU.mult, ALU.bypass)
                e.drain()
                e.tensor_scalar(b.c_t1[:, c:c + 1], b.c_s1[:, c:c + 1], 1.0 / D, None,
                                ALU.mult, ALU.bypass)
                e.drain()
                e.tensor_tensor(b.c_t2[:, c:c + 1], b.c_t4[:, c:c + 1],
                                b.c_t4[:, c:c + 1], ALU.mult)
                e.drain()
                e.tensor_tensor(b.c_t1[:, c:c + 1], b.c_t1[:, c:c + 1],
                                b.c_t2[:, c:c + 1], ALU.subtract)
                e.drain()
                e.tensor_tensor(b.c_t2[:, c:c + 1], b.c_rho[:, c:c + 1],
                                b.c_rho[:, c:c + 1], ALU.mult)
                e.drain()
                e.tensor_tensor(b.c_t1[:, c:c + 1], b.c_t1[:, c:c + 1],
                                b.c_t2[:, c:c + 1], ALU.mult)
                e.drain()
                return e.tensor_scalar(b.c_t1[:, c:c + 1], b.c_t1[:, c:c + 1],
                                       LN_EPS, None, ALU.add, ALU.bypass)
            e_cm = S.op("vector", colm, waits=[e_q, em])
            e_sq = S.op("scalar", lambda e, c=c: e.activation(
                b.c_t2[:, c:c + 1], b.c_t1[:, c:c + 1], AF.Sqrt), waits=[e_cm])

            def colm2(e, c=c):
                e.reciprocal(b.c_t1[:, c:c + 1], b.c_t2[:, c:c + 1])
                e.drain()
                e.tensor_tensor(b.c_sc[:, c:c + 1], b.c_rho[:, c:c + 1],
                                b.c_t1[:, c:c + 1], ALU.mult)
                e.drain()
                return e.scalar_tensor_tensor(b.c_bi[:, c:c + 1], b.c_t4[:, c:c + 1],
                                              -1.0, b.c_sc[:, c:c + 1],
                                              ALU.mult, ALU.mult)
            e_cm2 = S.op("vector", colm2, waits=[e_sq])

            def evac(e, c=c):
                r = None
                for j in range(NJ):
                    r = e.activation(b.scr[c % 2][:, j * b.NT:(j + 1) * b.NT],
                                     pZp[c % 3][j][:, 0:b.NT], AF.Identity,
                                     scale=b.c_sc[:, c:c + 1], bias=b.c_bi[:, c:c + 1])
                    e.drain()
                return r
            e_ev = S.op("scalar", evac, waits=[e_cm2])
            done[c] = S.op("vector", lambda e, c=c: e.tensor_tensor(
                b.xs[:, c, :], b.xs[:, c, :], b.scr[c % 2][:, :], ALU.add),
                waits=[e_ev])
            e_cm_last = e_cm
            if SUB >= 9 and (c + 1) % self._G4 == 0:
                g_ = (c + 1) // self._G4 - 1
                self.emit_zn_group(g_, done)
                if g_ >= 1:
                    self.emit_zn_tp(g_ - 1)
        self.e_attn = [done[c] for c in range(CT)]
        self._att_bank_free = [done[CT - 2], done[CT - 2], done[CT - 2],
                               e_cm_last or e_nev[CT - 1]]
        self._att_done_last = [done[CT - 1], done[CT - 2]]
        if "x1" in b.dbg:
            dx = b.dbg_tensor("x1", [128, CT, D], BF16)
            S.op("sync", lambda e, dx=dx: e.dma_start(out=dx, in_=b.xs[:, :, :]),
                 waits=list(self.e_attn), dma="s_dbg_x1")
        ps2.close()
        ps_i.close()

    def emit_zn_group(self, g, done):
        b, S = self, self.S
        D = b.D
        G4 = self._G4
        if not hasattr(self, "_tpz_acc"):
            self._tpz_acc = {}
            self._zn_prev = {}
            self._zn_full = {}
            self._ztr = {}
        tpz = self._tpz_acc
        g0, g1 = g * G4, (g + 1) * G4
        sl = slice(g0, g1)
        sqr = None
        for c in range(g0, g1):
            sqr = S.op("scalar", lambda e, c=c: e.activation(
                b.jscr[:, :], b.xs[:, c, :], AF.Square,
                accum_out=b.c_s2[:, c:c + 1]), waits=[done[c]])

        def colz(e, sl=sl):
            e.tensor_copy(b.c_mu2[:, sl], b.c_mu1[:, sl])
            e.drain()
            e.tensor_scalar(b.c_t1[:, sl], b.c_s2[:, sl], 1.0 / D, None,
                            ALU.mult, ALU.bypass)
            e.drain()
            e.tensor_tensor(b.c_t2[:, sl], b.c_mu2[:, sl], b.c_mu2[:, sl],
                            ALU.mult)
            e.drain()
            e.tensor_tensor(b.c_t1[:, sl], b.c_t1[:, sl], b.c_t2[:, sl],
                            ALU.subtract)
            e.drain()
            e.tensor_scalar(b.c_t3[:, sl], b.c_t1[:, sl], 1.0 + LN_EPS,
                            LN_EPS * LN_EPS, ALU.mult, ALU.add)
            e.drain()
            return e.tensor_scalar(b.c_t1[:, sl], b.c_t1[:, sl], LN_EPS, None,
                                   ALU.add, ALU.bypass)
        e_cz = S.op("vector", colz, waits=[sqr, self.e_xstat])

        def sq2(e, sl=sl):
            e.activation(b.c_t2[:, sl], b.c_t1[:, sl], AF.Sqrt)
            e.drain()
            return e.activation(b.c_t4[:, sl], b.c_t3[:, sl], AF.Sqrt)
        e_sq = S.op("scalar", sq2, waits=[e_cz])

        def rr2(e, sl=sl):
            e.reciprocal(b.c_r2[:, sl], b.c_t2[:, sl])
            e.drain()
            return e.reciprocal(b.c_rr[:, sl], b.c_t4[:, sl])
        e_rr = S.op("vector", rr2, waits=[e_sq])
        e_zn = None
        for c in range(g0, g1):
            wt_ = [e_rr]
            if c - 4 in self._zn_prev:
                wt_.append(self._zn_prev[c - 4])
            e_t = S.op("vector", lambda e, c=c: e.tensor_scalar(
                b.mo_scr[c % 4][:, :], b.xs[:, c, :], b.c_mu2[:, c:c + 1],
                b.c_rr[:, c:c + 1], ALU.subtract, ALU.mult), waits=wt_)
            e_g1 = S.op("vector", lambda e, c=c: e.scalar_tensor_tensor(
                b.xn_tm[:, c, :], b.mo_scr[c % 4][:, :], 1.0, b.gb[:, :],
                ALU.bypass, ALU.mult), waits=[e_t, self.e_bc])
            e_zn = S.op("gpsimd", lambda e, c=c: e.tensor_tensor(
                b.xn_tm[:, c, :], b.xn_tm[:, c, :], b.bb[:, :], ALU.add),
                waits=[e_g1], dma=False)
            self._zn_prev[c] = e_g1
            self._zn_full[c] = e_zn
        self._zn_last = e_zn

    def emit_zn_tp(self, g):
        b, S = self, self.S
        G4 = self._G4
        wtp = [self._zn_full[c] for c in range(g * G4, (g + 1) * G4)]
        wtp.append(self._last_tp)
        if g == 0 and self.e_dbg_xn1 is not None:
            wtp.append(self.e_dbg_xn1)
        e_tp = S.op("sync", lambda e, g=g: e.dma_start(
            out=b.xnf[:, g * G4:(g + 1) * G4, :, :],
            in_=b.xn_tm[:, g * G4:(g + 1) * G4, :]
                .rearrange("p c d -> p (c d)"), transpose=True),
            waits=[w_ for w_ in wtp if w_ is not None], dma=f"s_tz_{g}")
        self._last_tp = e_tp
        self._tpz_acc[g] = e_tp
        self._tpz_groups = self._tpz_acc
        self.e_tp_zn = e_tp

    def ph_ffn_pre(self):
        b, S = self, self.S
        CT = b.CT
        G4 = self._G4
        NG = CT // G4
        done = {c: self.e_attn[c] for c in range(CT)}
        for g in range(NG):
            if not hasattr(self, "_zn_full") or (g * G4) not in self._zn_full:
                self.emit_zn_group(g, done)
        for g in range(NG):
            if g not in getattr(self, "_tpz_acc", {}):
                self.emit_zn_tp(g)
        if "zn" in b.dbg:
            dx = b.dbg_tensor("zn", [128, b.CT, b.KD, 128], BF16)
            S.op("sync", lambda e, dx=dx: e.dma_start(out=dx, in_=b.xnf[:, :, :, :]),
                 waits=[self.e_tp_zn], dma="s_dbg_zn")

    def ph_gating(self, ps):
        b, S = self, self.S
        nc, D, KD, E, CT = b.nc, b.D, b.KD, b.E, b.CT
        pg = [ps.enter_context(nc.psum_tensor(f"p_g{i}", [128, 512], F32))
              for i in range(2)]
        ptw = [ps.enter_context(nc.psum_tensor(f"p_tw{i}", [128, 512], F32))
               for i in range(2)]
        GB_OFF = D
        prev = {}
        for c in range(CT):
            w = [self._tpz_groups[min(c // self._G4, len(self._tpz_groups) - 1)],
                 self.e_w2]
            if c < 2:
                w += self._att_done_last
            if c - 2 in prev:
                w.append(prev[c - 2][0])

            def mmg(e, c=c):
                for kc in range(KD):
                    e.matmul(pg[c % 2][:, 0:E], b.xnf[:, c, kc, :], b.gw_sb[:, kc, :],
                             start=(kc == 0), stop=False)
                return e.matmul(pg[c % 2][:, 0:E], b.ones_sb[0:1, :],
                                b.rows2_bf[0:1, GB_OFF:GB_OFF + E],
                                start=False, stop=True)
            em = S.op("tensor", mmg, waits=w)

            def gate_cols(e, c=c):
                p = pg[c % 2]
                e.reduce_max(b.c_t1[:, c:c + 1], p[:, 0:E], axis=AX.X)
                e.drain()
                e.tensor_scalar(b.zsq[:, 0:E], p[:, 0:E], b.c_t1[:, c:c + 1], None,
                                ALU.is_ge, ALU.bypass)
                e.drain()
                e.scalar_tensor_tensor(b.zsq[:, E:2 * E], b.zsq[:, 0:E], -BIG,
                                       p[:, 0:E], ALU.mult, ALU.add)
                e.drain()
                e.tensor_reduce(b.c_t2[:, c:c + 1], b.zsq[:, E:2 * E], axis=AX.X,
                                op=ALU.max)
                e.drain()
                e.tensor_scalar(b.zsq[:, 2 * E:3 * E], p[:, 0:E], b.c_t2[:, c:c + 1],
                                None, ALU.is_ge, ALU.bypass)
                e.drain()
                return e.tensor_scalar(b.c_t3[:, c:c + 1], b.c_t1[:, c:c + 1], -1.0,
                                       None, ALU.mult, ALU.bypass)
            e_gc = S.op("vector", gate_cols, waits=[em])
            e_ex = S.op("scalar", lambda e, c=c: e.activation(
                b.zsq[:, 3 * E:4 * E], pg[c % 2][:, 0:E], AF.Exp,
                bias=b.c_t3[:, c:c + 1]), waits=[e_gc])

            def wfin(e, c=c):
                e.tensor_tensor(b.zsq[:, 4 * E:5 * E], b.zsq[:, 3 * E:4 * E],
                                b.zsq[:, 2 * E:3 * E], ALU.mult)
                e.drain()
                e.tensor_reduce(b.c_t4[:, c:c + 1], b.zsq[:, 4 * E:5 * E], axis=AX.X,
                                op=ALU.add)
                e.drain()
                e.reciprocal(b.c_t1[:, c:c + 1], b.c_t4[:, c:c + 1])
                e.drain()
                return e.tensor_scalar(b.w_tm[:, c, :], b.zsq[:, 4 * E:5 * E],
                                       b.c_t1[:, c:c + 1], None, ALU.mult, ALU.bypass)
            e_wv = S.op("vector", wfin, waits=[e_ex])
            wt = [e_wv, self.e_i128f]
            if c < 2:
                wt += self._att_done_last
            if c - 2 in prev:
                wt.append(prev[c - 2][1])
            em_t = S.op("tensor", lambda e, c=c: e.transpose(
                ptw[c % 2][0:E, 0:128], b.w_tm[:, c, :], b.i128f_sb[:, :]), waits=wt)
            e_wf = S.op("vector", lambda e, c=c: e.tensor_copy(
                b.w_fm[0:E, c * 128:(c + 1) * 128], ptw[c % 2][0:E, 0:128]),
                waits=[em_t])
            prev[c] = (e_wv, e_wf)
        self.e_gate = [prev[c][0] for c in range(CT)]
        self.e_gate_fm = [prev[c][1] for c in range(CT)]
        self.e_gate_last_s = e_ex
        if "w" in b.dbg:
            dw = b.dbg_tensor("w", [128, CT, E], F32)
            S.op("sync", lambda e, dw=dw: e.dma_start(out=dw, in_=b.w_tm[:, :, :]),
                 waits=list(self.e_gate), dma="s_dbg_w")

    def ph_pass(self, p):
        b, S = self, self.S
        nc, D, KD, KH, E = b.nc, b.D, b.KD, b.KH, b.E
        CF, NT, NJ = b.CF, b.NT, b.NJ
        TN, NJH, LH = b.TN, b.NJH, b.LH
        c0 = p * CF
        arena = ExitStack()
        abel_tm = arena.enter_context(nc.sbuf_tensor(f"abel{p}", [128, CF, D], BF16))
        moe_acc = arena.enter_context(nc.sbuf_tensor(f"moe_acc{p}", [128, CF, D], BF16))
        pp_ = getattr(self, "_pass_prev", None)

        aw_sb = b.aw_sb
        e_awl = self.e_awl
        ps = ExitStack()
        pa = [ps.enter_context(nc.psum_tensor(f"p_a{p}_{i}", [128, 512], F32))
              for i in range(4)]
        pdbg = getattr(self, "_pass_dbg", {})
        AB_OFF = 0
        prev = {}
        e_d1 = {}
        for cr in range(CF):
            ca = c0 + cr
            for j in range(NJ):
                idx = cr * NJ + j
                w = [e_awl,
                     self._tpz_groups[min(ca // self._G4, len(self._tpz_groups) - 1)]]
                if pp_ is not None and idx < 4:
                    w += list(pp_["rot_pairs"][idx])
                    if p - 1 in pdbg:
                        w.append(pdbg[p - 1])
                elif pp_ is None and idx < 4:
                    w.append(self._att_bank_free[idx])
                if idx - 4 in prev:
                    w.append(prev[idx - 4])

                def mma(e, ca=ca, j=j, idx=idx):
                    for kc in range(KD):
                        e.matmul(pa[idx % 4][:, 0:NT], b.xnf[:, ca, kc, :],
                                 aw_sb[:, kc, j * NT:(j + 1) * NT],
                                 start=(kc == 0), stop=False)
                    return e.matmul(pa[idx % 4][:, 0:NT], b.ones_sb[0:1, :],
                                    b.rows2_bf[0:1, AB_OFF + j * NT:AB_OFF + (j + 1) * NT],
                                    start=False, stop=True)
                em = S.op("tensor", mma, waits=w)
                wg = [em]
                if p == 0 and idx < 4:
                    wg.append(self._tpz_groups[min(1, len(self._tpz_groups) - 1)])
                if pp_ is not None:
                    wg.append(pp_["mix2"][cr])
                prev[idx] = S.op("scalar", lambda e, cr=cr, j=j, idx=idx: e.activation(
                    abel_tm[:, cr, j * NT:(j + 1) * NT], pa[idx % 4][:, 0:NT], AF.Gelu),
                    waits=wg)
            e_d1[cr] = S.op("vector", lambda e, cr=cr, ca=ca: e.scalar_tensor_tensor(
                b.jscr[:, :], abel_tm[:, cr, :], 1.0, b.w1b[:, :],
                ALU.bypass, ALU.mult, accum_out=b.c_d1[:, ca:ca + 1]),
                waits=[prev[cr * NJ + NJ - 1]])
        e_abel_last = prev[CF * NJ - 1]
        if p == 0:
            self.ph_gating(ps)
        ps.close()

        ew_es = ExitStack()
        h_fm = ew_es.enter_context(nc.sbuf_tensor(f"h{p}", [128, KH, b.TF], BF16))
        NB1 = 2
        ew1b = ew_es.enter_context(nc.sbuf_tensor(f"ew1b{p}", [128, NB1, KD, 128], BF16))
        RS2 = min(8, NJ * KH)
        ew2b = ew_es.enter_context(nc.sbuf_tensor(f"ew2b{p}", [128, RS2, NT], BF16))
        TBC = min(4, CF)
        NTB = CF // TBC
        ps = ExitStack()
        po = [ps.enter_context(nc.psum_tensor(f"p_o{p}_{i}", [128, 512], F32))
              for i in range(TBC * NJ)]
        e_w1t, e_w1c, e_w2t, e_w2c = {}, {}, {}, {}
        e_hev, e_moe = {}, {}
        self._w2_cons = {}
        self._po_evac = {}

        def sched_w1(g, extra=()):
            ei_, m_ = divmod(g, KH)
            if ei_ >= E:
                return
            w = list(extra) + [self.e_cv1[ei_]]
            if g - NB1 >= 0:
                w.append(e_w1c[divmod(g - NB1, KH)])
            e_w1t[(ei_, m_)] = S.op("sync", lambda e, ei_=ei_, m_=m_: e.dma_start(
                out=ew1b[:, (ei_ * KH + m_) % NB1, :, :], in_=b.d_ew1b[ei_, m_]),
                waits=w, dma=f"s_e1{(ei_ * KH + m_) % NB1}")

        for g0 in range(NB1):
            sched_w1(g0, extra=[e_abel_last])
        for ei in range(E):
            NPO = len(po)
            for m in range(KH):
                for j in range(NJH):
                    idx = m * NJH + j
                    gj = min((c0 + LH * (j + 1) - 1) // self._G4,
                             len(self._tpz_groups) - 1)
                    w = [e_w1t[(ei, m)], self._tpz_groups[gj]]
                    if ei == 0 and idx < NPO:
                        w.append(e_abel_last)
                        if p == 0:
                            w += [self.e_gate_fm[-1], self.e_gate_last_s,
                                  self._zn_last]
                    if ei > 0 and idx < NPO:
                        key = (ei - 1, NJ - 1, idx % NPO)
                        if key in self._po_evac:
                            w.append(self._po_evac[key])
                    if idx >= NPO:
                        pm, pj = divmod(idx - NPO, NJH)
                        w.append(e_hev[(ei, pm, pj)])

                    def hmm(e, ei=ei, m=m, j=j, idx=idx):
                        r = None
                        for kc in range(KD):
                            r = e.matmul(po[idx % NPO][:, 0:TN],
                                         ew1b[:, (ei * KH + m) % NB1, kc, :],
                                         b.xnf[:, c0 + LH * j:c0 + LH * (j + 1), kc, :],
                                         start=(kc == 0), stop=(kc == KD - 1))
                        return r
                    em = S.op("tensor", hmm, waits=w)
                    if j == NJH - 1:
                        e_w1c[(ei, m)] = em
                        sched_w1(ei * KH + m + NB1)
                    e_hev[(ei, m, j)] = S.op(
                        "scalar", lambda e, ei=ei, m=m, j=j, idx=idx: e.activation(
                            h_fm[:, m, j * TN:(j + 1) * TN], po[idx % NPO][:, 0:TN],
                            AF.Gelu, bias=b.eb1c[:, m, ei:ei + 1]), waits=[em])
            h_last = e_hev[(ei, KH - 1, NJH - 1)]
            last_mm = None
            seqpos = 0
            w2_mm = {}
            for j2 in range(NJ):
                for k2 in range(KH):
                    gseq = ei * NJ * KH + seqpos
                    slot = gseq % RS2
                    w = [self.e_cv2[ei]]
                    if seqpos == 0:
                        w.append(e_w2c[ei - 1] if ei - 1 in e_w2c else e_abel_last)
                    if gseq - RS2 >= 0:
                        w.append(self._w2_cons[gseq - RS2])
                    e_w2t[gseq] = S.op("sync", lambda e, ei=ei, k2=k2, j2=j2,
                                       slot=slot:
                        e.dma_start(out=ew2b[:, slot, :],
                                    in_=b.d_ew2b[ei, k2 * 128:(k2 + 1) * 128,
                                                 j2 * NT:(j2 + 1) * NT]),
                        waits=w, dma=f"s_e2{slot}")

                    def omm(e, ei=ei, k2=k2, slot=slot):
                        r = None
                        for cr in range(CF):
                            r = e.matmul(po[cr][:, 0:NT],
                                         h_fm[:, k2, cr * 128:(cr + 1) * 128],
                                         ew2b[:, slot, :],
                                         start=(k2 == 0), stop=(k2 == KH - 1))
                        return r
                    wm = [e_w2t[gseq], h_last]
                    if k2 == 0:
                        for cr in range(CF):
                            key = (ei, j2 - 1, cr) if j2 > 0 else (ei - 1, NJ - 1, cr)
                            if key in self._po_evac:
                                wm.append(self._po_evac[key])
                        if pp_ is not None and j2 == 0 and ei == 0:
                            wm.append(pp_["tpm"])
                            for pr2 in pp_["rot_pairs"]:
                                wm += list(pr2)
                    em = S.op("tensor", omm, waits=wm)
                    self._w2_cons[gseq] = em
                    w2_mm[(j2, k2)] = em
                    last_mm = em
                    seqpos += 1
                for cr in range(CF):
                    ca = c0 + cr

                    def oev(e, ei=ei, cr=cr, ca=ca, j2=j2):
                        sl = (slice(None), cr, slice(j2 * NT, (j2 + 1) * NT))
                        if ei == 0:
                            return e.tensor_scalar(
                                moe_acc[sl], po[cr][:, 0:NT],
                                b.w_tm[:, ca, ei:ei + 1], None,
                                ALU.mult, ALU.bypass)
                        return e.scalar_tensor_tensor(
                            moe_acc[sl], po[cr][:, 0:NT],
                            b.w_tm[:, ca, ei:ei + 1],
                            moe_acc[sl], ALU.mult, ALU.add)
                    wv = [w2_mm[(j2, KH - 1)], self.e_gate[ca]]
                    if ei == 0 and p == 0:
                        wv.append(self.e_tp_zn)
                    if ei > 0:
                        wv.append(e_moe[(ei - 1, cr, j2)])
                    e_moe[(ei, cr, j2)] = S.op("vector", oev, waits=wv)
                    self._po_evac[(ei, j2, cr)] = e_moe[(ei, cr, j2)]
            e_w2c[ei] = last_mm
        for cr in range(CF):
            ca = c0 + cr
            for j2 in range(NJ):
                tb = cr // TBC
                pi = (cr % TBC) * NJ + j2
                wb = [self.e_gate_fm[ca], self.e_w2,
                      self._po_evac[(E - 1, NJ - 1, pi)]]
                if cr - TBC >= 0:
                    wb.append(e_moe[(E, cr - TBC, j2)])
                em = S.op("tensor", lambda e, ca=ca, j2=j2, pi=pi: e.matmul(
                    po[pi][:, 0:NT], b.w_fm[0:E, ca * 128:(ca + 1) * 128],
                    b.eb2_sb[0:E, j2 * NT:(j2 + 1) * NT], start=True, stop=True),
                    waits=wb)
                e_moe[(E, cr, j2)] = S.op(
                    "vector", lambda e, cr=cr, j2=j2, pi=pi: e.scalar_tensor_tensor(
                        moe_acc[:, cr, j2 * NT:(j2 + 1) * NT], po[pi][:, 0:NT],
                        1.0, moe_acc[:, cr, j2 * NT:(j2 + 1) * NT],
                        ALU.bypass, ALU.add),
                    waits=[em, e_moe[(E - 1, cr, j2)]])
        e_moe_done = [e_moe[(E, cr, j2)] for cr in range(CF) for j2 in range(NJ)]
        ps.close()
        ew_es.close()
        if f"moe{p}" in b.dbg:
            dm = b.dbg_tensor(f"moe{p}", [128, CF, D], BF16)
            self._pass_dbg = getattr(self, "_pass_dbg", {})
            self._pass_dbg[p] = S.op(
                "sync", lambda e, dm=dm: e.dma_start(out=dm, in_=moe_acc[:, :, :]),
                waits=e_moe_done, dma=f"s_dbg_moe{p}")

        moef = arena.enter_context(nc.sbuf_tensor(f"moef{p}", [128, CF, KD, 128], BF16))
        CH = max(1, CF // 2)
        tpm_c = {}
        e_h = self._last_tp
        for h in range(CF // CH):
            wtp_ = [e_moe[(E, cr, j2)] for j2 in range(NJ)
                    for cr in range(h * CH, (h + 1) * CH)]
            wtp_.append(e_h)
            e_h = S.op("sync", lambda e, h=h: e.dma_start(
                out=moef[:, h * CH:(h + 1) * CH, :, :],
                in_=moe_acc[:, h * CH:(h + 1) * CH, :]
                    .rearrange("p c d -> p (c d)"), transpose=True),
                waits=[w_ for w_ in wtp_ if w_ is not None], dma=f"s_tm{p}")
            for cr in range(h * CH, (h + 1) * CH):
                tpm_c[cr] = e_h
        e_tpm = tpm_c[CF - 1]
        self._last_tp = e_tpm

        Rf = arena.enter_context(nc.sbuf_tensor(f"Rf{p}", [128, KD, D], BF16))
        e_Rp = S.op("sync", lambda e: e.dma_start(
            out=Rf[:, :, :], in_=b.d_spill[5]),
            waits=[self.e_R, e_w2c[E - 1], e_w1c[(E - 1, KH - 1)]], dma="s_sp")
        ps = ExitStack()
        pr = [ps.enter_context(nc.psum_tensor(f"p_r{p}_{i}", [128, 512], F32))
              for i in range(4)]
        pd = ps.enter_context(nc.psum_tensor(f"p_d{p}", [128, CF], F32))
        e_store = []
        prevrot = {}
        prev_tail = {}
        pend_o = None

        def prev_slot_evt(lst, mod, slot):
            for c_ in range(CF - 1, -1, -1):
                if ((p - 1) * CF + c_) % mod == slot:
                    return lst[c_]
            return None
        for cr in range(CF):
            ca = c0 + cr
            rot_ev = []
            for j in range(NJ):
                idx = cr * NJ + j
                w = [tpm_c[cr], e_Rp]
                if idx - 4 in prevrot:
                    w += list(prevrot[idx - 4])

                def mmr(e, cr=cr, j=j, idx=idx):
                    r = None
                    for kc in range(KD):
                        r = e.matmul(pr[idx % 4][:, 0:NT], moef[:, cr, kc, :],
                                     Rf[:, kc, j * NT:(j + 1) * NT],
                                     start=(kc == 0), stop=(kc == KD - 1))
                    return r
                em = S.op("tensor", mmr, waits=w)
                wcp = [em]
                if cr - 6 in prev_tail:
                    wcp.append(prev_tail[cr - 6]["f"])
                elif pp_ is not None:
                    ev_ = prev_slot_evt(pp_["f"], 6, (c0 + cr) % 6)
                    if ev_ is not None:
                        wcp.append(ev_)
                e_cp = S.op("scalar", lambda e, cr=cr, j=j, idx=idx: e.copy(
                    b.mo_scr[(c0 + cr) % 6][:, j * NT:(j + 1) * NT], pr[idx % 4][:, 0:NT]),
                    waits=wcp)
                prevrot[idx] = (e_cp,)
                rot_ev.append(e_cp)

            def dmm(e, cr=cr):
                r = None
                for kc in range(KD):
                    r = e.matmul(pd[:, cr:cr + 1], moef[:, cr, kc, :],
                                 b.rgc[:, kc:kc + 1],
                                 start=(kc == 0), stop=(kc == KD - 1))
                return r
            em_d = S.op("tensor", dmm, waits=[tpm_c[cr], self.e_rgc])
            if pend_o is not None:
                pend_o()
                pend_o = None

            e_gc = S.op("vector", lambda e, ca=ca, cr=cr: e.tensor_tensor(
                b.c_t1[:, ca:ca + 1], pd[:, cr:cr + 1], b.c_d1[:, ca:ca + 1],
                ALU.add), waits=[em_d, e_d1[cr]])
            e_sg = S.op("scalar", lambda e, ca=ca: e.activation(
                b.c_g[:, ca:ca + 1], b.c_t1[:, ca:ca + 1], AF.Sigmoid,
                bias=b.gcb_col[:, 0:1]), waits=[e_gc])
            wmix = list(rot_ev)
            e_mix1 = S.op("gpsimd", lambda e, cr=cr: e.tensor_tensor(
                b.mo_scr[(c0 + cr) % 6][:, :], b.mo_scr[(c0 + cr) % 6][:, :], abel_tm[:, cr, :],
                ALU.subtract), waits=wmix, dma=False)
            e_mix2 = S.op("vector", lambda e, cr=cr, ca=ca: e.scalar_tensor_tensor(
                b.mo_scr[(c0 + cr) % 6][:, :], b.mo_scr[(c0 + cr) % 6][:, :], b.c_g[:, ca:ca + 1],
                abel_tm[:, cr, :], ALU.mult, ALU.add), waits=[e_mix1, e_sg])
            wfs = [e_mix2]
            if cr - 2 in prev_tail:
                wfs.append(prev_tail[cr - 2]["st"])
            elif pp_ is not None:
                ev_ = prev_slot_evt(pp_["st"], 2, (c0 + cr) % 2)
                if ev_ is not None:
                    wfs.append(ev_)
            e_fs = S.op("vector", lambda e, ca=ca, cr=cr: e.tensor_scalar(
                b.scr[(c0 + cr) % 2][:, :], b.xs[:, ca, :], b.c_mu2[:, ca:ca + 1],
                b.c_r2[:, ca:ca + 1], ALU.subtract, ALU.mult), waits=wfs)
            e_f = S.op("vector", lambda e, cr=cr, ca=ca: e.scalar_tensor_tensor(
                b.scr[(c0 + cr) % 2][:, :], b.scr[(c0 + cr) % 2][:, :], 1.0,
                b.mo_scr[(c0 + cr) % 6][:, :], ALU.bypass, ALU.add,
                accum_out=b.c_s1[:, ca:ca + 1]), waits=[e_fs, e_mix2])
            e_fq = S.op("scalar", lambda e, ca=ca, cr=cr: e.activation(
                b.zsq[:, :], b.scr[(c0 + cr) % 2][:, :], AF.Square,
                accum_out=b.c_s2[:, ca:ca + 1]), waits=[e_f])

            def colf(e, ca=ca):
                e.tensor_scalar(b.c_mf[:, ca:ca + 1], b.c_s1[:, ca:ca + 1], 1.0 / D,
                                None, ALU.mult, ALU.bypass)
                e.drain()
                e.tensor_scalar(b.c_t1[:, ca:ca + 1], b.c_s2[:, ca:ca + 1], 1.0 / D,
                                None, ALU.mult, ALU.bypass)
                e.drain()
                e.tensor_tensor(b.c_t2[:, ca:ca + 1], b.c_mf[:, ca:ca + 1],
                                b.c_mf[:, ca:ca + 1], ALU.mult)
                e.drain()
                e.tensor_tensor(b.c_t1[:, ca:ca + 1], b.c_t1[:, ca:ca + 1],
                                b.c_t2[:, ca:ca + 1], ALU.subtract)
                e.drain()
                return e.tensor_scalar(b.c_t1[:, ca:ca + 1],
                                       b.c_t1[:, ca:ca + 1], LN_EPS, None,
                                       ALU.add, ALU.bypass)
            e_cf = S.op("vector", colf, waits=[e_f, e_fq])
            e_sf = S.op("scalar", lambda e, ca=ca: e.activation(
                b.c_t2[:, ca:ca + 1], b.c_t1[:, ca:ca + 1], AF.Sqrt),
                waits=[e_cf])

            def colf2(e, ca=ca):
                e.reciprocal(b.c_rf[:, ca:ca + 1], b.c_t2[:, ca:ca + 1])
                e.drain()
                return e.scalar_tensor_tensor(b.c_t3[:, ca:ca + 1],
                                              b.c_mf[:, ca:ca + 1], -1.0,
                                              b.c_rf[:, ca:ca + 1], ALU.mult, ALU.mult)
            e_cf2 = S.op("vector", colf2, waits=[e_sf])
            e_tf = S.op("scalar", lambda e, ca=ca, cr=cr: e.activation(
                b.zsq[:, :], b.scr[(c0 + cr) % 2][:, :], AF.Identity,
                scale=b.c_rf[:, ca:ca + 1], bias=b.c_t3[:, ca:ca + 1]),
                waits=[e_cf2, e_fq])
            def emit_o(ca=ca, cr=cr, e_tf=e_tf):
                wo_ = [e_tf]
                if len(self._stores) >= 2:
                    wo_.append(self._stores[-2])
                e_o = S.op("vector", lambda e, ca=ca, cr=cr: e.tensor_tensor(
                    b.scr[(c0 + cr) % 2][:, :], b.xs[:, ca, :], b.zsq[:, :],
                    ALU.add), waits=wo_)
                st_ = S.op("sync", lambda e, ca=ca, cr=cr: e.dma_start(
                    out=b.d_y[ca], in_=b.scr[(c0 + cr) % 2][:, :]), waits=[e_o],
                    dma=f"s_out{(p * CF + cr) % 2}")
                e_store.append(st_)
                self._stores.append(st_)
                prev_tail[cr]["st"] = st_
            pend_o = emit_o
            prev_tail[cr] = dict(mix1=e_mix1, mix2=e_mix2, f=e_f, tf=e_tf)
        pend_o()
        self.e_pass_done[p] = e_store[-1]
        rp_ = {}
        for j_ in sorted(prevrot):
            rp_[j_ % 4] = prevrot[j_]
        self._pass_prev = dict(
            rot_pairs=[rp_.get(i, ()) for i in range(4)],
            tpm=e_tpm,
            mix2=[prev_tail[c]["mix2"] for c in range(CF)],
            f=[prev_tail[c]["f"] for c in range(CF)],
            st=[prev_tail[c]["st"] for c in range(CF)],
        )
        ps.close()
        arena.close()

    def ph_tail(self):
        waits = self._stores[-2:]
        if not waits or self.cfg.get("phases", 99) < 99:
            waits = list(self.S.last_evt.values())
        self.S.op("sync", lambda e: e.nop(), waits=waits, inc=False, dma=False)



_BUILD_CACHE = {}


def _get_nc(key, cfg):
    if key not in _BUILD_CACHE:
        _BUILD_CACHE[key] = Builder(cfg).build()
    return _BUILD_CACHE[key]


def host_inputs(cfg, inp):
    D, E = cfg["D"], cfg["E"]
    KD, KH = D // 128, (2 * D) // 128
    wq = np.asarray(inp["Wq"], dtype=np.float32)
    wk = np.asarray(inp["Wk"], dtype=np.float32)
    ew1 = np.asarray(inp["ew1"], dtype=np.float32)
    gcw = np.asarray(inp["gc_w"], dtype=np.float32)[:, 0]
    rows = np.concatenate([
        np.asarray(inp["ln_gamma"], dtype=np.float32),
        np.asarray(inp["ln_beta"], dtype=np.float32),
        gcw[:D], gcw[D:],
        np.asarray(inp["abel_b"], dtype=np.float32),
        np.asarray(inp["moe_gate_b"], dtype=np.float32),
        np.asarray(inp["gc_b"], dtype=np.float32),
    ])[None, :]
    return dict(
        wqk=np.ascontiguousarray(np.concatenate([wq.T, wk.T], axis=1)),
        wv=np.ascontiguousarray(np.asarray(inp["Wv"], dtype=np.float32).T),
        wo=np.ascontiguousarray(np.asarray(inp["Wo"], dtype=np.float32).T),
        aw=np.ascontiguousarray(np.asarray(inp["abel_w"], dtype=np.float32)),
        gw=np.ascontiguousarray(np.asarray(inp["moe_gate_w"], dtype=np.float32)),
        ew1=np.ascontiguousarray(
            ew1.reshape(E, KD, 128, KH, 128).transpose(0, 3, 2, 1, 4)),
        ew2=np.ascontiguousarray(np.asarray(inp["ew2"], dtype=np.float32)),
        eb1=np.ascontiguousarray(np.asarray(inp["eb1"], dtype=np.float32)),
        eb2=np.ascontiguousarray(np.asarray(inp["eb2"], dtype=np.float32)),
        gens=np.ascontiguousarray(np.asarray(inp["gens"], dtype=np.float32)),
        rows=np.ascontiguousarray(rows),
        tril=np.triu(np.ones((128, 128), dtype=BF)),
        eye64=np.concatenate([np.eye(64, dtype=BF)] * 2, axis=0),
        i128=np.eye(128, dtype=BF),
        i128f=np.eye(128, dtype=np.float32),
        eye8f=np.eye(8, dtype=np.float32),
        ones=np.ones((128, 128), dtype=BF),
    )


def kernel(**inputs):
    cfg = default_cfg()
    D, T = cfg["D"], cfg["T"]
    CT = T // 128
    NC = cfg["NCORES"]
    nc = _get_nc("main", cfg)
    shared = host_inputs(cfg, dict(
        Wq=inputs["L0_attn_Wq"], Wk=inputs["L0_attn_Wk"], Wv=inputs["L0_attn_Wv"],
        Wo=inputs["L0_attn_Wo"], ln_gamma=inputs["L0_ffn_ln_gamma"],
        ln_beta=inputs["L0_ffn_ln_beta"], abel_w=inputs["L0_ffn_abel_w"],
        abel_b=inputs["L0_ffn_abel_b"], moe_gate_w=inputs["L0_ffn_moe_gate_w"],
        moe_gate_b=inputs["L0_ffn_moe_gate_b"], ew1=inputs["L0_ffn_ew1"],
        eb1=inputs["L0_ffn_eb1"], ew2=inputs["L0_ffn_ew2"], eb2=inputs["L0_ffn_eb2"],
        gens=inputs["L0_ffn_gens"], gc_w=inputs["L0_ffn_gc_w"],
        gc_b=inputs["L0_ffn_gc_b"]))
    x = np.asarray(inputs["L0_attn_x"], dtype=np.float32)
    in_maps = []
    for c in range(NC):
        m = dict(shared)
        m["x"] = np.ascontiguousarray(x[c].reshape(CT, 128, D))
        in_maps.append(m)
    res = run_bass_kernel_spmd(nc, in_maps, core_ids=list(range(NC)))
    return np.stack([np.asarray(res.results[c]["y"]).reshape(T, D)
                     for c in range(NC)])
